# version 10
# baseline (speedup 1.0000x reference)
import math
import numpy as np
import concourse.bass as bass
import concourse.mybir as mybir
from concourse.bass_utils import run_bass_kernel_spmd
from concourse.alu_op_type import AluOpType as OP

F32 = mybir.dt.float32
BF16 = mybir.dt.bfloat16
AF = mybir.ActivationFunctionType
AX = mybir.AxisListType.X

T = 2048
D = 1024
NCH = 8
LAMBDA_INIT = 0.8 - 0.6 * math.exp(-0.3 * 12)
SCALING = 32.0 ** -0.5

ENGINES = ("sync", "tensor", "vector", "scalar", "gpsimd")


class MK:
    def __init__(self):
        self.nc = bass.Bass("TRN2", target_bir_lowering=False)
        from contextlib import ExitStack
        self.es = ExitStack()
        nc = self.nc
        self.stream = {e: [] for e in ENGINES}
        self.sem = {}
        self.ctr = {}
        names = ["sv", "sc", "so"]
        names += ["st%d" % i for i in range(8)]
        names += ["sg%d" % i for i in range(8)] + ["ss%d" % i for i in range(8)]
        for name in names:
            self.sem[name] = self.es.enter_context(nc.semaphore(name))
            self.ctr[name] = 0
        self.dma_ring = {"gpsimd": 0, "sync": 0}
        self.dma_last = {}
        self.t_ring = 0
        self.tring_n = 1
        self.tlast = None

        def din(name, shape):
            return nc.dram_tensor(name, shape, F32, kind="ExternalInput").ap()

        self.x_in = din("x", [D, T])
        self.wqkvT = din("wqkvT", [D, 3 * D])
        self.woT = din("woT", [D, D])
        self.lam = din("lam", [1, 128])
        self.sublnw4 = din("sublnw4", [128, 256])
        self.w1T = din("w1T", [8, D, 2048])
        self.w2T = din("w2T", [8, 2048, D])
        self.wmmT = din("wmmT", [D, 16])
        self.wshiftT = din("wshiftT", [D, D])
        self.wkvT = din("wkvT", [D, 128])
        self.woutT = din("woutT", [64, D])
        self.tdec = din("tdec", [64, 1])
        self.tfirst = din("tfirst", [64, 1])
        self.w1l = din("w1l", [D, 512])
        self.b1l = din("b1l", [128, 4])
        self.w2l = din("w2l", [512, D])
        self.w1h = din("w1h", [D, 4096])
        self.b1h = din("b1h", [128, 32])
        self.w2h = din("w2h", [4096, D])
        self.colpack = din("colpack", [128, 9 * NCH])
        self.ltri = din("ltri", [128, 16])
        self.constv = din("constv", [128, 8])
        self.eye = din("eye", [128, 128])
        self.onesin = din("onesin", [128, 128])

        self.y_out = nc.dram_tensor("y", [D, T], F32, kind="ExternalOutput").ap()
        self.xspill = nc.dram_tensor("xspill", [D, T], F32, kind="Internal").ap()
        self.srow_d = nc.dram_tensor("srow_d", [1, T], F32, kind="Internal").ap()
        self.crow_d = nc.dram_tensor("crow_d", [16, 8, 128], BF16, kind="Internal").ap()
        self.zslots = nc.dram_tensor("zslots", [5248, 1024], BF16, kind="Internal").ap()
        self.yslots = nc.dram_tensor("yslots", [5248, 1024], BF16, kind="Internal").ap()
        self.w1scr = nc.dram_tensor("w1scr", [2048, 8192], BF16, kind="Internal").ap()
        self.w2scr = nc.dram_tensor("w2scr", [2048, 8192], BF16, kind="Internal").ap()
        self.rb2 = nc.dram_tensor("rb2", [4, T], F32, kind="Internal").ap()

        self.xs_t = self.es.enter_context(nc.sbuf_tensor("xs", [128, NCH * T], F32))
        self.ar_t = self.es.enter_context(nc.sbuf_tensor("ar", [128, 65536], BF16))
        self.xs_a = self.xs_t.ap()
        self.ar = self.ar_t.ap()
        self.arf = self.ar.bitcast(F32)

        self.PS = self.es.enter_context(nc.psum_tensor("PS", [128, 3072], F32))
        self.PT = self.es.enter_context(nc.psum_tensor("PT", [128, 2048], BF16))
        self.ps = self.PS.ap()
        self.pt = self.PT.ap()

        self.identb = self.AB(124.5, 128)
        self.identf = self.AF(127.15, 128)
        self.onesf = self.AF(124.75, 128)
        self.onesb = self.AB(125.25, 128)
        self.subw_s = self.AF(125.5, 256)
        self.COLS = {}
        for i, n in enumerate(["normw", "sgate", "lnw", "lnb", "rtok", "gl", "gh",
                               "b2l", "b2h"]):
            self.COLS[n] = self.AF(126.5 + i * 0.03125, NCH)
        self.GC = self.AF(126.8, 3 * NCH)
        self.MISC = self.AF(126.9, 64)
        self.CONSTS = [0.0, 1.0, -0.5, 1e-5, 1e-6, SCALING, 1.0 / 64.0,
                       math.log(1.0 - LAMBDA_INIT)]
        cv = self.MISC[:, 46:54]
        self.cv = cv
        for i, v in enumerate(self.CONSTS):
            nc.const_aps.register(F32, v, cv[:, i:i + 1])
        self.RA = [self.AB(c * 4.0, T) for c in range(NCH)]
        self.ld = {}
        self.bank_last_read = {}
        self.pt_last_read = {}
        self.pt_idx = 0

    def AB(self, kb, cols):
        o = int(round(kb * 512))
        assert o + cols <= 65536, (kb, cols)
        return self.ar[:, o:o + cols]

    def AF(self, kb, cols):
        o = int(round(kb * 256))
        assert o + cols <= 32768, (kb, cols)
        return self.arf[:, o:o + cols]

    def xc(self, c, lo=0, hi=T):
        return self.xs_a[:, c * T + lo:c * T + hi]

    def bank(self, i, lo=0, hi=512):
        return self.ps[:, i * 512 + lo:i * 512 + hi]

    def colv(self, name, c=0):
        return self.COLS[name][:, c:c + 1]

    def gcv(self, which, c):
        return self.GC[:, which * NCH + c:which * NCH + c + 1]

    def mcol(self, i):
        return self.MISC[:, i:i + 1]

    OWN = {"vector": "sv", "scalar": "sc", "tensor": "st"}

    def emit(self, eng, fn, inc=None, n=1):
        own = self.OWN.get(eng)
        if eng in ("vector", "scalar"):
            assert inc in (None, own), (eng, inc)
            prev = self.ctr[own]
            h = self.sem[own]
            if prev > 0:
                self.stream[eng].append(
                    lambda e, h=h, prev=prev: e.wait_ge(h, prev))
            self.ctr[own] += 1
            def op(e, fn=fn, h=h):
                fn(e).then_inc(h, 1)
            self.stream[eng].append(op)
            return (own, self.ctr[own])
        if inc is None:
            self.stream[eng].append(fn)
            return None
        if eng == "tensor":
            assert inc == "st"
            inc = "st%d" % (self.t_ring % self.tring_n)
            self.t_ring += 1
        h = self.sem[inc]
        if eng == "tensor" and self.ctr[inc] > 0:
            prev = self.ctr[inc]
            self.stream[eng].append(lambda e, h=h, prev=prev: e.wait_ge(h, prev))
        self.ctr[inc] += n
        def op2(e, fn=fn, h=h, n=n):
            fn(e).then_inc(h, n)
        self.stream[eng].append(op2)
        ev = (inc, self.ctr[inc])
        if eng == "tensor":
            self.tlast = ev
        return ev

    def ckpt(self, eng, sem=None):
        if eng in ("vector", "scalar"):
            return (self.OWN[eng], self.ctr[self.OWN[eng]])
        for i in range(8):
            s = "st%d" % i
            if self.ctr[s] > 0:
                self.wait("tensor", (s, self.ctr[s]))
        return self.emit(eng, lambda e: e.nop(), inc="st")

    def wait(self, eng, ev):
        if ev is None:
            return
        sem, val = ev
        h = self.sem[sem]
        self.stream[eng].append(lambda e, h=h, val=val: e.wait_ge(h, val))

    def dma(self, eng, out, in_, sem=None):
        if sem is None:
            pfx = "sg" if eng == "gpsimd" else "ss"
            i = self.dma_ring[eng]
            self.dma_ring[eng] = (i + 1) % 8
            sem = "%s%d" % (pfx, i)
            if sem in self.dma_last:
                self.wait(eng, (sem, self.dma_last[sem]))
        ev = self.emit(eng, lambda e, out=out, in_=in_: e.dma_start(out=out, in_=in_), inc=sem, n=16)
        self.dma_last[sem] = ev[1]
        return ev

    def barrier(self):
        evs = [self.ckpt(e, s) for e, s in
               (("tensor", "st"), ("vector", "sv"), ("scalar", "sc"))]
        for e in ("tensor", "vector", "scalar"):
            for ev in evs:
                self.wait(e, ev)
        self.gate_t = evs[0]
        return evs[0]

    def use_ring(self, n):
        self.tring_n = n
        self.t_ring = 0

    def use_bank(self, i, ev=None):
        self.wait("tensor", self.bank_last_read.get(i))
        if ev is not None:
            self.bank_last_read[i] = ev

    def read_bank(self, i, ev):
        self.bank_last_read[i] = ev

    def TT(self, out, a, bb, op, inc=None):
        return self.emit("vector", lambda e, out=out, a=a, bb=bb, op=op:
                         e.tensor_tensor(out=out, in0=a, in1=bb, op=op), inc=inc)

    def TS(self, out, a, s1, s2, op0, op1=None, inc=None):
        if op1 is None:
            return self.emit("vector", lambda e, out=out, a=a, s1=s1, op0=op0:
                             e.tensor_scalar(out=out, in0=a, scalar1=s1, scalar2=None, op0=op0), inc=inc)
        return self.emit("vector", lambda e, out=out, a=a, s1=s1, s2=s2, op0=op0, op1=op1:
                         e.tensor_scalar(out=out, in0=a, scalar1=s1, scalar2=s2, op0=op0, op1=op1), inc=inc)

    def STT(self, out, a, s, bb, op0, op1, inc=None):
        return self.emit("vector", lambda e, out=out, a=a, s=s, bb=bb, op0=op0, op1=op1:
                         e.scalar_tensor_tensor(out=out, in0=a, scalar=s, in1=bb, op0=op0, op1=op1), inc=inc)

    def VCOPY(self, out, a, inc=None):
        return self.emit("vector", lambda e, out=out, a=a: e.tensor_copy(out, a), inc=inc)

    def RED(self, out, a, op, inc=None, absv=None):
        return self.emit("vector", lambda e, out=out, a=a, op=op, absv=absv:
                         e.tensor_reduce(out=out, in_=a, axis=AX, op=op, apply_absolute_value=absv), inc=inc)

    def RECIP(self, out, a, inc=None):
        return self.emit("vector", lambda e, out=out, a=a: e.reciprocal(out, a), inc=inc)

    def MEMSET(self, ap, v, inc=None):
        return self.emit("vector", lambda e, ap=ap, v=v: e.memset(ap, v), inc=inc)

    def SCAN(self, out, d0, d1, inc=None):
        return self.emit("vector", lambda e, out=out, d0=d0, d1=d1:
                         e.tensor_tensor_scan(out=out, data0=d0, data1=d1, initial=0.0,
                                              op0=OP.mult, op1=OP.add), inc=inc)

    def ACT(self, out, a, fn, bias=0.0, scale=1.0, accum=None, inc=None):
        return self.emit("scalar", lambda e, out=out, a=a, fn=fn, bias=bias, scale=scale, accum=accum:
                         e.activation(out=out, in_=a, func=fn, bias=bias, scale=scale, accum_out=accum), inc=inc)

    def SCOPY(self, out, a, inc=None):
        return self.emit("scalar", lambda e, out=out, a=a: e.copy(out, a), inc=inc)

    def MM(self, out, lhsT, rhs, start, stop, tp=None, inc=None):
        return self.emit("tensor", lambda e, out=out, lhsT=lhsT, rhs=rhs, start=start, stop=stop, tp=tp:
                         e.matmul(out, lhsT, rhs, start=start, stop=stop, tile_position=tp), inc=inc)

    def TRANS(self, out, in_, ident, inc=None):
        return self.emit("tensor", lambda e, out=out, in_=in_, ident=ident:
                         e.transpose(out, in_, ident), inc=inc)

    def pt_slot(self):
        i = self.pt_idx
        self.pt_idx += 1
        off = [0, 1024, 256, 1280][i % 4]
        return self.pt[:, off:off + 128], i % 4

    def ln_stats(self, srcs, sqkb, eps, fin, rms=False, base=110.5, extra=None):
        mu = self.AF(base, 512)
        r = self.AF(base + 2.0, 512)
        mur = self.AF(base + 4.0, 512)
        for tt in range(4):
            lo = tt * 512
            sq_evs = []
            for c in range(len(srcs)):
                sq = self.AB(sqkb + c * 1.0, 512)
                sq_evs.append((sq, self.ACT(sq, srcs[c][:, lo:lo + 512], AF.Square, inc="sc")))
            self.use_bank(0)
            self.use_bank(1)
            sum_ev = None
            if not rms:
                for c in range(NCH):
                    sum_ev = self.MM(self.bank(0), self.onesf, srcs[c][:, lo:lo + 512],
                                     c == 0, c == NCH - 1,
                                     inc=("st" if c == NCH - 1 else None))
            for c in range(NCH):
                self.wait("tensor", sq_evs[c][1])
                self.MM(self.bank(1), self.onesb, sq_evs[c][0], c == 0, c == NCH - 1,
                        inc=("st" if c == NCH - 1 else None))
            mm_ev = self.tlast
            self.wait("vector", sum_ev)
            self.wait("vector", mm_ev)
            if not rms:
                self.TS(mu, self.bank(0), 1.0 / D, None, OP.mult)
                self.TT(mur, mu, mu, OP.mult)
                self.TS(r, self.bank(1), 1.0 / D, None, OP.mult)
                self.TT(r, r, mur, OP.subtract, inc="sv")
            else:
                self.TS(r, self.bank(1), 1.0 / D, None, OP.mult, inc="sv")
            var_ev = ("sv", self.ctr["sv"])
            if extra is not None:
                extra(tt, lo, r, var_ev)
            self.wait("scalar", var_ev)
            self.ACT(r, r, AF.Ln, bias=eps)
            self.ACT(r, r, AF.Exp, scale=-0.5, inc="sc")
            r_ev = ("sc", self.ctr["sc"])
            self.wait("vector", r_ev)
            if not rms:
                self.TT(mur, mu, r, OP.mult)
            fin(tt, lo, mu, r, mur)
            free_ev = self.ckpt("vector", "sv")
            self.read_bank(0, free_ev)
            self.read_bank(1, free_ev)
        self.wait("scalar", ("sv", self.ctr["sv"]))

    def proj_fm(self, wsrc3, col0, out_cb, rhs_list, slots, slot_free, nouts,
                resid=False, evac="scalar"):
        act_evs = {}
        for oc in range(nouts):
            si = oc % 4
            sl = slots[si]
            self.wait("gpsimd", slot_free[si])
            ldev = self.dma("gpsimd", sl.rearrange("p (c n) -> p c n", c=NCH),
                            wsrc3[:, :, col0 + oc * 128: col0 + (oc + 1) * 128])
            self.wait("tensor", ldev)
            for tt in range(4):
                bk = tt % 2
                self.use_bank(bk)
                for c in range(NCH):
                    self.MM(self.bank(bk), sl[:, c * 128:(c + 1) * 128],
                            rhs_list[c][:, tt * 512:(tt + 1) * 512], c == 0, c == NCH - 1,
                            inc=("st" if c == NCH - 1 else None))
                mm_ev = self.tlast
                ev = out_cb(oc, tt, self.bank(bk), mm_ev)
                self.read_bank(bk, ev)
            slot_free[si] = self.ckpt("tensor", "st")
        return slot_free

    def ph0_loads(self):
        b = self
        b.ld["x"] = b.dma("sync", b.xs_a.rearrange("p (c t) -> p c t", c=NCH),
                          b.x_in.rearrange("(c p) t -> p c t", p=128))
        b.ld["eye_b"] = b.dma("gpsimd", b.identb, b.eye)
        b.ld["ones_f"] = b.dma("sync", b.onesf, b.onesin)
        b.ld["ones_b"] = b.dma("gpsimd", b.onesb, b.onesin)
        b.ld["subw"] = b.dma("sync", b.subw_s, b.sublnw4)
        b.ld["cols"] = b.dma("sync", b.arf[:, int(126.5 * 256):int(126.5 * 256) + 9 * NCH],
                             b.colpack)
        b.ld["b1l"] = b.dma("sync", b.MISC[:, 0:4], b.b1l)
        b.ld["b1h"] = b.dma("sync", b.MISC[:, 4:36], b.b1h)
        b.ld["constv"] = b.dma("sync", b.cv, b.constv)
        b.ld["ltri"] = b.dma("sync", b.AF(127.65, 16), b.ltri)
        b.ld["identf"] = b.dma("sync", b.identf, b.eye)
        b.ld["tdec"] = b.dma("sync", b.mcol(41)[0:64, :], b.tdec)
        b.ld["tf"] = b.dma("sync", b.mcol(42)[0:64, :], b.tfirst)

    def ph1_lambda_ln1(self):
        b = self
        lamrow = b.AF(110.5, 128)
        b.ld["lam"] = b.dma("sync", lamrow[0:1, :], b.lam)
        for ev in (b.ld["lam"], b.ld["ones_f"]):
            b.wait("vector", ev)
        b.wait("scalar", b.ld["constv"])
        b.wait("vector", b.ld["constv"])
        b.TT(lamrow[0:1, 0:32], lamrow[0:1, 0:32], lamrow[0:1, 32:64], OP.mult)
        b.TT(lamrow[0:1, 64:96], lamrow[0:1, 64:96], lamrow[0:1, 96:128], OP.mult)
        b.RED(lamrow[0:1, 32:33], lamrow[0:1, 0:32], OP.add)
        b.RED(lamrow[0:1, 96:97], lamrow[0:1, 64:96], OP.add, inc="sv")
        b.wait("scalar", ("sv", b.ctr["sv"]))
        b.ACT(lamrow[0:1, 33:34], lamrow[0:1, 32:33], AF.Exp)
        b.ACT(lamrow[0:1, 97:98], lamrow[0:1, 96:97], AF.Exp, inc="sc")
        b.wait("vector", ("sc", b.ctr["sc"]))
        b.TT(lamrow[0:1, 0:1], lamrow[0:1, 33:34], lamrow[0:1, 97:98], OP.subtract)
        b.TS(lamrow[0:1, 0:1], lamrow[0:1, 0:1], 1.0, LAMBDA_INIT, OP.mult, OP.add, inc="sv")
        b.wait("tensor", ("sv", b.ctr["sv"]))
        b.use_bank(0)
        b.MM(b.bank(0, 0, 1), b.onesf[0:1, :], lamrow[0:1, 0:1], True, True, inc="st")
        b.wait("vector", b.tlast)
        b.TS(b.mcol(36), b.bank(0, 0, 1), 1.0, None, OP.mult)
        ev = b.TS(b.mcol(37), b.bank(0, 0, 1), -1.0, None, OP.mult, inc="sv")
        b.read_bank(0, ev)

        b.wait("vector", b.ld["cols"])
        b.wait("scalar", b.ld["cols"])
        b.ACT(b.GC[:, 0:NCH], b.COLS["sgate"], AF.Sigmoid, inc="sc")
        b.wait("vector", ("sc", b.ctr["sc"]))
        b.TS(b.GC[:, NCH:2 * NCH], b.GC[:, 0:NCH], -1.0, 1.0, OP.mult, OP.add)
        b.RECIP(b.GC[:, 2 * NCH:3 * NCH], b.COLS["gl"])
        b.TT(b.GC[:, 2 * NCH:3 * NCH], b.COLS["gh"], b.GC[:, 2 * NCH:3 * NCH], OP.mult, inc="sv")
        b.wait("scalar", b.ld["tdec"])
        b.wait("scalar", b.ld["tf"])
        b.ACT(b.mcol(41)[0:64, :], b.mcol(41)[0:64, :], AF.Exp)
        b.ACT(b.mcol(43)[0:64, :], b.mcol(42)[0:64, :], AF.Exp, inc="sc")
        b.wait("vector", ("sc", b.ctr["sc"]))
        b.TS(b.mcol(43)[0:64, :], b.mcol(43)[0:64, :], -1.0, None, OP.mult, inc="sv")

        zt0 = b.AB(32.0, 1024)
        b.MEMSET(zt0, 0.0, inc="sv")
        z0ev = ("sv", b.ctr["sv"])
        b.wait("sync", z0ev)
        b.zinit_evs = []
        for k in range(41):
            b.zinit_evs.append(b.dma("sync", b.zslots[k * 128:(k + 1) * 128, :], zt0))
        for e in range(8):
            b.zinit_evs.append(b.dma("sync", b.yslots[e * 656 + 640:e * 656 + 656, :],
                                     zt0[0:16, :]))
        b.wait("scalar", b.ld["x"])
        b.wait("tensor", b.ld["x"])
        b.wait("tensor", b.ld["ones_b"])
        xln = b.RA
        tmp = b.AF(96.5, 512)

        def fin(tt, lo, mu, r, mur):
            for c in range(NCH):
                b.TT(tmp, b.xc(c, lo, lo + 512), r, OP.mult)
                b.TT(xln[c][:, lo:lo + 512], tmp, mur, OP.subtract)

        b.ln_stats([b.xc(c) for c in range(NCH)], sqkb=64.5, eps=1e-5, fin=fin)
        b.ckpt("vector", "sv")
        b.xln_ev = ("sv", b.ctr["sv"])

    def ph1b_v(self):
        b = self
        vtm = b.AB(32.0, 16896)
        b.vtm = vtm
        b.wait("tensor", b.ld["eye_b"])
        b.wait("vector", b.xln_ev)
        for zev in b.zinit_evs:
            b.wait("vector", zev)
        b.MEMSET(vtm, 1.0, inc="sv")
        vtm_ones = ("sv", b.ctr["sv"])
        b.wait("tensor", vtm_ones)
        b.wait("tensor", b.xln_ev)
        b.wait("scalar", b.xln_ev)
        b.wq_slots = [b.AB(116.5 + i * 2.0, 1024) for i in range(4)]
        b.slot_free = [None] * 4
        b.wqkv3 = b.wqkvT.rearrange("(c p) n -> p c n", p=128)
        vv = b.AB(104.5, 2048)
        xln = b.RA
        vtm_cp = None
        for oc in range(NCH):
            si = oc % 4
            sl = b.wq_slots[si]
            b.wait("gpsimd", b.slot_free[si])
            ldev = b.dma("gpsimd", sl.rearrange("p (c n) -> p c n", c=NCH),
                         b.wqkv3[:, :, 2 * D + oc * 128: 2 * D + (oc + 1) * 128])
            b.wait("tensor", ldev)
            tr_last = None
            for tt in range(4):
                bk = tt % 2
                self.use_bank(bk)
                for c in range(NCH):
                    b.MM(b.bank(bk), sl[:, c * 128:(c + 1) * 128],
                         xln[c][:, tt * 512:(tt + 1) * 512], c == 0, c == NCH - 1,
                         inc=("st" if c == NCH - 1 else None))
                mm_ev = b.tlast
                b.wait("scalar", mm_ev)
                ev = b.ACT(vv[:, tt * 512:(tt + 1) * 512], b.bank(bk), AF.Copy, inc="sc")
                b.read_bank(bk, ev)
            vv_ev = ("sc", b.ctr["sc"])
            b.wait("tensor", vv_ev)
            b.slot_free[si] = b.ckpt("tensor", "st")
            for kc in range(16):
                pslot, pi = b.pt_slot()
                b.wait("tensor", b.pt_last_read.get(pi))
                b.TRANS(pslot, vv[:, kc * 128:(kc + 1) * 128], b.identb, inc="st")
                tr_ev = b.tlast
                tr_last = tr_ev
                b.wait("vector", tr_ev)
                base = kc * 1056 + 2 * oc * 66
                dst = vtm[:, base:base + 132].rearrange("p (h z) -> p h z", z=66)[:, :, 0:64]
                src = pslot.rearrange("p (h z) -> p h z", z=64)
                cpev = b.VCOPY(dst, src, inc="sv")
                b.pt_last_read[pi] = cpev
                vtm_cp = cpev
            b.wait("scalar", b.ckpt("tensor"))
        b.vtm_ev = vtm_cp

    def ph2_attention(self):
        b = self
        xln = b.RA
        atm = b.AB(65.0, 16384)
        b.atm = atm
        atm3 = atm.rearrange("p (s z) -> p s z", z=1024)
        qg = b.AB(97.0, 2048)
        kg = b.AB(101.0, 2048)
        pex = [b.AB(105.0, 1024), b.AB(107.0, 1024)]
        ez = b.AB(109.0, 1024)
        dvE = b.AF(111.0, 256)
        dvO = b.AF(112.0, 256)
        cmb = b.AF(113.0, 256)
        tmp4 = b.AF(114.0, 256)
        sqc = b.AB(115.0, 256)
        ssm = b.AF(115.55, 4)
        rst = b.AF(115.65, 4)
        rzE = b.AF(115.75, 4)
        rzO = b.AF(115.85, 4)

        b.wait("tensor", b.vtm_ev)
        b.wait("vector", b.ld["subw"])
        b.use_ring(4)
        prev = None
        for g in range(NCH):
            for which, dst in ((0, qg), (1, kg)):
                si = (2 * g + which) % 4
                sl = b.wq_slots[si]
                b.wait("gpsimd", b.slot_free[si])
                ldev = b.dma("gpsimd", sl.rearrange("p (c n) -> p c n", c=NCH),
                             b.wqkv3[:, :, which * D + g * 128: which * D + (g + 1) * 128])
                b.wait("tensor", ldev)
                for tt in range(4):
                    bk = tt % 2
                    b.use_bank(bk)
                    for c in range(NCH):
                        b.MM(b.bank(bk), sl[:, c * 128:(c + 1) * 128],
                             xln[c][:, tt * 512:(tt + 1) * 512], c == 0, c == NCH - 1,
                             inc=("st" if c == NCH - 1 else None))
                    mm_ev = b.tlast
                    b.wait("scalar", mm_ev)
                    ev = b.ACT(dst[:, tt * 512:(tt + 1) * 512], b.bank(bk), AF.Copy, inc="sc")
                    b.read_bank(bk, ev)
                b.slot_free[si] = b.ckpt("tensor", "st")
            qk_ev = ("sc", b.ctr["sc"])
            b.wait("tensor", qk_ev)

            for jl in range(2):
                h = 2 * g + jl
                for qt in range(4):
                    cur = {"pv": {}, "exp": {}}

                    def emit_pv(kc):
                        st_ = kc % 2
                        b.wait("tensor", cur["exp"][kc])
                        if kc == 0 and prev is not None:
                            b.wait("tensor", prev["evac"])
                        pvs = []
                        for half in range(2):
                            pvs.append(b.MM(
                                b.bank(4 + half)[0:65, :],
                                b.vtm[:, kc * 1056 + h * 66: kc * 1056 + h * 66 + 65],
                                pex[st_][:, half * 512:(half + 1) * 512],
                                kc == 0, kc == 15, inc="st"))
                        cur["pv"][kc] = pvs

                    for kc in range(16):
                        st_ = kc % 2
                        bk = 2 * st_
                        if kc >= 2:
                            b.wait("tensor", cur["exp"][kc - 2])
                        elif prev is not None:
                            b.wait("tensor", prev["exp"][14 + st_])
                        pack_evs = []
                        for i in range(2):
                            strip = 64 * jl + 32 * i
                            pack_evs.append(b.MM(
                                b.bank(bk + i, 0, 512),
                                kg[strip:strip + 32, kc * 128:(kc + 1) * 128],
                                qg[strip:strip + 32, qt * 512:(qt + 1) * 512],
                                True, True, tp=(strip, 0), inc="st"))
                        for pe_ in pack_evs:
                            b.wait("scalar", pe_)
                        if kc >= 2:
                            for pe_ in cur["pv"][kc - 2]:
                                b.wait("scalar", pe_)
                        elif prev is not None:
                            for pe_ in prev["pv"][14 + st_]:
                                b.wait("scalar", pe_)
                        b.ACT(pex[st_], b.ps[:, bk * 512:(bk + 2) * 512], AF.Exp,
                              scale=SCALING, inc="sc")
                        cur["exp"][kc] = ("sc", b.ctr["sc"])
                        if kc >= 1:
                            emit_pv(kc - 1)
                    emit_pv(15)
                    for pe_ in cur["pv"][15]:
                        b.wait("scalar", pe_)
                    b.SCOPY(ez[0:65, 0:512], b.bank(4)[0:65, :])
                    b.SCOPY(ez[0:65, 512:1024], b.bank(5)[0:65, :], inc="sc")
                    cur["evac"] = ("sc", b.ctr["sc"])
                    b.wait("tensor", cur["evac"])
                    if prev is not None:
                        b.wait("tensor", prev["div"])
                    for s in range(8):
                        half, s4 = s // 4, s % 4
                        dstp = b.pt[:, half * 1024 + s4 * 66: half * 1024 + s4 * 66 + 65]
                        b.TRANS(dstp, ez[0:65, s * 128:(s + 1) * 128],
                                b.identb[0:65, 0:65],
                                inc="st")
                    tr_ev = b.ckpt("tensor")
                    b.wait("vector", tr_ev)
                    ptE = b.pt[:, 0:264].rearrange("p (s z) -> p s z", z=66)
                    ptO = b.pt[:, 1024:1288].rearrange("p (s z) -> p s z", z=66)
                    b.RECIP(rzE, ptE[:, :, 64:65])
                    b.RECIP(rzO, ptO[:, :, 64:65])
                    b.TT(dvE.rearrange("p (s z) -> p s z", z=64), ptE[:, :, 0:64],
                         rzE.rearrange("p (s u) -> p s u", u=1).broadcast_to([128, 4, 64]),
                         OP.mult)
                    b.TT(dvO.rearrange("p (s z) -> p s z", z=64), ptO[:, :, 0:64],
                         rzO.rearrange("p (s u) -> p s u", u=1).broadcast_to([128, 4, 64]),
                         OP.mult, inc="sv")
                    cur["div"] = ("sv", b.ctr["sv"])
                    b.STT(cmb, dvO, b.mcol(37), dvE, OP.mult, OP.add, inc="sv")
                    b.wait("scalar", ("sv", b.ctr["sv"]))
                    b.ACT(sqc, cmb, AF.Square, inc="sc")
                    b.wait("vector", ("sc", b.ctr["sc"]))
                    b.RED(ssm.rearrange("p (s u) -> p s u", u=1),
                          sqc.rearrange("p (s z) -> p s z", z=64), OP.add, inc="sv")
                    b.wait("scalar", ("sv", b.ctr["sv"]))
                    b.ACT(rst, ssm, AF.Ln, scale=1.0 / 64.0, bias=1e-5)
                    b.ACT(rst, rst, AF.Exp, scale=-0.5,
                          bias=math.log(1.0 - LAMBDA_INIT), inc="sc")
                    b.wait("vector", ("sc", b.ctr["sc"]))
                    b.TT(tmp4.rearrange("p (s z) -> p s z", z=64),
                         cmb.rearrange("p (s z) -> p s z", z=64),
                         rst.rearrange("p (s u) -> p s u", u=1).broadcast_to([128, 4, 64]),
                         OP.mult)
                    b.TT(atm3[:, qt * 4:(qt + 1) * 4, h * 64:h * 64 + 64],
                         tmp4.rearrange("p (s z) -> p s z", z=64),
                         b.subw_s.rearrange("p (s z) -> p s z", z=64), OP.mult, inc="sv")
                    prev = cur
        b.wait("tensor", prev["div"])
        b.ckpt("tensor")
        b.use_ring(1)
        b.ckpt("vector", "sv")
        b.atm_ev = ("sv", b.ctr["sv"])
        b.read_bank(4, prev["evac"] if False else None)
        b.barrier()

    def ph3_attnout(self):
        b = self
        atm = b.atm
        afm = [b.AB(32.0 + c * 4.0, T) for c in range(NCH)]
        for fc in range(NCH):
            for tc in range(16):
                pslot, pi = b.pt_slot()
                b.wait("tensor", b.pt_last_read.get(pi))
                b.TRANS(pslot, atm[:, tc * 1024 + fc * 128: tc * 1024 + (fc + 1) * 128],
                        b.identb, inc="st")
                b.wait("vector", b.tlast)
                b.pt_last_read[pi] = b.VCOPY(afm[fc][:, tc * 128:(tc + 1) * 128], pslot, inc="sv")
        b.ckpt("vector", "sv")
        afm_ev = ("sv", b.ctr["sv"])
        b.wait("tensor", afm_ev)
        wo3 = b.woT.rearrange("(c p) n -> p c n", p=128)

        def resid(oc, tt, bk_ap, mm_ev):
            b.wait("vector", mm_ev)
            return b.TT(b.xc(oc, tt * 512, (tt + 1) * 512), bk_ap,
                        b.xc(oc, tt * 512, (tt + 1) * 512), OP.add, inc="sv")

        b.proj_fm(wo3, 0, resid, afm, b.wq_slots, b.slot_free, NCH)
        b.barrier()

    def ph4_moe(self):
        b = self
        zn = b.RA
        slots = [b.AB(32.0, 8192), b.AB(48.0, 8192), b.AB(64.0, 8192)]
        G = b.AB(80.0, 16384)
        cbc = b.AB(112.0, 2048)
        tmp = b.AF(96.5, 512)
        cmbf = b.AF(123.0, 128)
        cmbb = b.AB(123.66, 128)
        crows = b.AB(123.9, 128)
        m4 = b.AF(116.5, 64)
        e4 = b.AF(116.75, 64)
        mx16 = b.AF(117.0, 16)
        ms16 = b.AF(117.1, 16)
        m1t = b.AF(117.2, 16)
        m2t = b.AF(117.3, 16)
        dnv = b.AF(117.4, 16)
        mi8 = b.AF(117.5, 128)
        pm4 = b.AF(118.0, 64)
        iE = b.AF(118.25, 64)
        iO = b.AF(118.5, 64)
        gw4 = b.AF(118.75, 64)
        wmm8 = b.AB(122.5, 128)

        ph4_gate = b.gate_t
        b.wait("sync", ("sv", b.ctr["sv"]))
        spill_ev = b.dma("sync", b.xspill.rearrange("(c p) t -> p c t", p=128),
                         b.xs_a.rearrange("p (c t) -> p c t", c=NCH))
        b.wait("vector", spill_ev)
        b.wait("gpsimd", ph4_gate)
        ld_wmm = b.dma("gpsimd", wmm8.rearrange("p (c n) -> p c n", c=NCH),
                       b.wmmT.rearrange("(c p) n -> p c n", p=128))

        def fin_ln2(tt, lo, mu, r, mur):
            for c in range(NCH):
                b.TT(tmp, b.xc(c, lo, lo + 512), r, OP.mult)
                b.TT(b.xc(c, lo, lo + 512), tmp, mur, OP.subtract)

        b.ln_stats([b.xc(c) for c in range(NCH)], sqkb=80.0, eps=1e-5,
                   fin=fin_ln2, base=116.5)
        b.barrier()

        w1r = b.w1T.rearrange("e (c p) f -> e p c f", p=128)
        w2r = b.w2T.rearrange("e (j p) d -> e p j d", p=128)
        loads = []
        for l in range(2):
            for e in range(8):
                loads.append((1, e, 0))
                loads.append((1, e, 1))
                loads.append((2, e, 0))
                loads.append((2, e, 1))
        load_evs = [None] * len(loads)
        seg_done = [None] * len(loads)
        wr_evs = [None] * 32
        emitted = [0]

        def pump(upto):
            while emitted[0] < min(upto, len(loads)):
                k = emitted[0]
                kind, e, hf = loads[k]
                l1 = k >= 32
                eng = ("sync" if k % 2 == 0 else "gpsimd") if l1 else "gpsimd"
                if k >= 3:
                    b.wait(eng, seg_done[k - 3])
                    if k - 3 < 32:
                        b.wait(eng, wr_evs[k - 3])
                else:
                    b.wait(eng, ph4_gate)
                si = k % 3
                scr = b.w1scr if kind == 1 else b.w2scr
                seg = (2 * e + hf) * 128
                if l1:
                    b.wait(eng, wr_evs[k - 32])
                    load_evs[k] = b.dma(eng, slots[si], scr[seg:seg + 128, :])
                elif kind == 1:
                    load_evs[k] = b.dma(eng, slots[si].rearrange("p (c f) -> p c f", c=NCH),
                                        w1r[e][:, :, hf * 1024:(hf + 1) * 1024])
                else:
                    load_evs[k] = b.dma(eng, slots[si].rearrange("p (j d) -> p j d", j=8),
                                        w2r[e][:, hf * 8:(hf + 1) * 8, :])
                if not l1:
                    b.wait("sync", load_evs[k])
                    wr_evs[k] = b.dma("sync", scr[seg:seg + 128, :], slots[si])
                emitted[0] += 1

        pump(3)
        kseg = 0

        for l in range(2):
            def fin_rms(tt, lo, mu, r, mur):
                for c in range(NCH):
                    b.STT(zn[c][:, lo:lo + 512], b.xc(c, lo, lo + 512),
                          b.colv("normw", c), r, OP.mult, OP.mult)

            b.ln_stats([b.xc(c) for c in range(NCH)], sqkb=104.5,
                       eps=1e-6, fin=fin_rms, rms=True, base=116.5)
            b.ckpt("vector", "sv")
            zn_ev = ("sv", b.ctr["sv"])
            b.wait("tensor", zn_ev)
            b.wait("scalar", zn_ev)

            b.wait("tensor", ld_wmm)
            b.use_bank(0)
            for tc in range(16):
                for c in range(NCH):
                    b.MM(b.bank(0)[:, tc * 12:tc * 12 + 12],
                         zn[c][:, tc * 128:(tc + 1) * 128],
                         wmm8[:, c * 16:c * 16 + 12], c == 0, c == NCH - 1,
                         inc=("st" if (c == NCH - 1 and tc == 15) else None))
            lg_ev = b.ckpt("tensor")
            b.wait("vector", lg_ev)
            L = b.bank(0)[:, 0:192].rearrange("p (tc n) -> p tc n", n=12)
            mac = L[:, :, 0:4]
            mic = L[:, :, 4:12]
            b.RED(mx16.rearrange("p (s u) -> p s u", u=1), mac, OP.max)
            b.TT(m4.rearrange("p (s z) -> p s z", z=4), mac,
                 mx16.rearrange("p (s u) -> p s u", u=1).broadcast_to([128, 16, 4]),
                 OP.subtract)
            b.VCOPY(mi8, mic, inc="sv")
            b.wait("scalar", ("sv", b.ctr["sv"]))
            b.ACT(m4, m4, AF.Exp, inc="sc")
            b.wait("vector", ("sc", b.ctr["sc"]))
            m43 = m4.rearrange("p (s z) -> p s z", z=4)
            b.RED(ms16.rearrange("p (s u) -> p s u", u=1), m43, OP.add)
            b.RED(m1t.rearrange("p (s u) -> p s u", u=1), m43, OP.max)
            b.TT(e4.rearrange("p (s z) -> p s z", z=4), m43,
                 m1t.rearrange("p (s u) -> p s u", u=1).broadcast_to([128, 16, 4]),
                 OP.is_ge)
            b.STT(e4, e4, -1e30, m4, OP.mult, OP.add)
            b.RED(m2t.rearrange("p (s u) -> p s u", u=1),
                  e4.rearrange("p (s z) -> p s z", z=4), OP.max)
            b.TT(dnv, m1t, m2t, OP.add)
            b.STT(dnv, ms16, 1e-9, dnv, OP.mult, OP.add)
            b.RECIP(dnv, dnv)
            b.TT(iE.rearrange("p (s z) -> p s z", z=4), m43,
                 m2t.rearrange("p (s u) -> p s u", u=1).broadcast_to([128, 16, 4]),
                 OP.is_ge)
            b.TT(gw4, m4, iE, OP.mult)
            b.TT(gw4.rearrange("p (s z) -> p s z", z=4),
                 gw4.rearrange("p (s z) -> p s z", z=4),
                 dnv.rearrange("p (s u) -> p s u", u=1).broadcast_to([128, 16, 4]),
                 OP.mult)
            mi4 = mi8.rearrange("p (s g j) -> p s g j", g=4, j=2)
            b.TT(pm4.rearrange("p (s g) -> p s g", g=4), mi4[:, :, :, 0],
                 mi4[:, :, :, 1], OP.max)
            pmB = pm4.rearrange("p (s g) -> p s g", g=4)
            b.TT(iE.rearrange("p (s g) -> p s g", g=4), mi4[:, :, :, 0], pmB, OP.is_ge)
            b.TT(iO.rearrange("p (s g) -> p s g", g=4), mi4[:, :, :, 1], pmB, OP.is_ge)
            cm4 = cmbf.rearrange("p (s g j) -> p s g j", g=4, j=2)
            b.TT(cm4[:, :, :, 0], iE.rearrange("p (s g) -> p s g", g=4),
                 gw4.rearrange("p (s g) -> p s g", g=4), OP.mult)
            ev = b.TT(cm4[:, :, :, 1], iO.rearrange("p (s g) -> p s g", g=4),
                      gw4.rearrange("p (s g) -> p s g", g=4), OP.mult, inc="sv")
            b.read_bank(0, ev)
            b.VCOPY(cmbb, cmbf, inc="sv")
            cmb_ev = ("sv", b.ctr["sv"])
            b.wait("tensor", cmb_ev)
            if True:
                pslot, pi = b.pt_slot()
                b.wait("tensor", b.pt_last_read.get(pi))
                b.wait("tensor", b.pt_last_read.get((pi + 2) % 4))
                b.TRANS(pslot, cmbb, b.identb, inc="st")
                b.wait("vector", b.tlast)
                crows_ev = b.VCOPY(crows, pslot, inc="sv")
                b.pt_last_read[pi] = crows_ev
                b.wait("sync", crows_ev)
                crd_ev = b.dma("sync", b.crow_d.rearrange("tc e j -> (tc e) j"), crows)

            c8 = b.AF(80.0, 2048)[0:8, :]
            mask8 = b.AF(88.0, 2048)[0:8, :]
            pos8 = b.AF(96.0, 2048)[0:8, :]
            ones8 = b.AF(104.0, 2048)[0:8, :]
            first8 = ones8
            s12r = b.AF(112.0, 2048)[0:1, :]
            s1r = b.AF(80.0, 2048)[0:1, :]
            s2r = b.AF(88.0, 2048)[0:1, :]
            ltri_s = b.AF(127.65, 16)
            s1f = b.AF(124.0, 16)
            s2f = b.AF(124.0625, 16)
            c1f = b.AF(124.125, 16)
            c2f = b.AF(124.1875, 16)
            s1u = s1f.bitcast(mybir.dt.uint32)
            s2u = s2f.bitcast(mybir.dt.uint32)
            tmA = b.AF(118.5, 128)
            tmB = b.AF(119.0, 128)
            tmC = b.AF(119.5, 128)

            if True:
                b.wait("gpsimd", crd_ev)
                c8ld = b.dma("gpsimd", c8.rearrange("e (tc j) -> e tc j", j=128),
                             b.crow_d.rearrange("tc e j -> e tc j"))
                b.wait("vector", c8ld)
                b.wait("vector", b.ld["ltri"])
                b.MEMSET(ones8, 1.0)
                b.TS(mask8, c8, 0.0, None, OP.is_gt)
                b.SCAN(pos8, ones8, mask8, inc="sv")
                b.wait("tensor", ("sv", b.ctr["sv"]))
                b.wait("tensor", b.ld["identf"])
                for tt in range(4):
                    b.use_bank(1)
                    sev = b.MM(b.bank(1)[0:8, :], ltri_s[0:8, 0:8],
                               mask8[:, tt * 512:(tt + 1) * 512], True, True, inc="st")
                    b.wait("vector", sev)
                    ev = b.TS(first8[:, tt * 512:(tt + 1) * 512], b.bank(1)[0:8, :],
                              0.5, None, OP.is_lt, inc="sv")
                    b.read_bank(1, ev)
                b.TT(first8, first8, mask8, OP.mult)
                b.TS(c8, pos8, 1.0, ltri_s[0:8, 8:9], OP.subtract, OP.add)
                b.TT(c8, c8, mask8, OP.mult)
                b.TT(mask8, c8, first8, OP.mult, inc="sv")
                b.wait("tensor", ("sv", b.ctr["sv"]))
                for tt in range(4):
                    b.use_bank(2)
                    e1 = b.MM(b.bank(2)[0:1, :], b.onesf[0:8, 0:1],
                              c8[:, tt * 512:(tt + 1) * 512], True, True, inc="st")
                    b.use_bank(3)
                    e2 = b.MM(b.bank(3)[0:1, :], b.onesf[0:8, 0:1],
                              mask8[:, tt * 512:(tt + 1) * 512], True, True, inc="st")
                    b.wait("vector", e1)
                    b.wait("vector", e2)
                    ev1 = b.TS(s12r[:, tt * 512:(tt + 1) * 512], b.bank(2)[0:1, :],
                               1.0, None, OP.mult, inc="sv")
                    ev2 = b.TS(s1r[:, tt * 512:(tt + 1) * 512], b.bank(3)[0:1, :],
                               1.0, None, OP.mult, inc="sv")
                    b.read_bank(2, ev1)
                    b.read_bank(3, ev2)
                b.TT(s2r, s12r, s1r, OP.subtract, inc="sv")
                b.wait("tensor", ("sv", b.ctr["sv"]))
                for ri, row in ((0, s1r), (1, s2r)):
                    dstf = s1f if ri == 0 else s2f
                    bk = 4 + ri
                    b.use_bank(bk)
                    for tc in range(16):
                        b.MM(b.bank(bk)[:, tc:tc + 1], row[:, tc * 128:(tc + 1) * 128],
                             b.onesf[0:1, 0:1], True, True, inc="st")
                    tre = b.tlast
                    b.wait("vector", tre)
                    ev = b.VCOPY(dstf, b.bank(bk)[:, 0:16], inc="sv")
                    b.read_bank(bk, ev)
                b.VCOPY(s1u, s1f)
                b.VCOPY(s2u, s2f)
                cm3 = cmbf.rearrange("p (s e) -> p s e", e=8)
                tA = tmA.rearrange("p (s e) -> p s e", e=8)
                tB = tmB.rearrange("p (s e) -> p s e", e=8)
                tC = tmC.rearrange("p (s e) -> p s e", e=8)
                b.TS(tmA.rearrange("p (s e) -> p s e", e=8), cm3, 0.0, None, OP.is_gt)
                b.VCOPY(tmB, tmA)
                for k in (1, 2, 4):
                    b.VCOPY(tmC, tmB)
                    b.TT(tB[:, :, k:8], tC[:, :, k:8], tC[:, :, 0:8 - k], OP.add)
                b.TT(tmB, tmB, tmA, OP.subtract)
                b.TS(tmB, tmB, 0.5, None, OP.is_lt)
                b.TT(tmA, tmA, tmB, OP.mult)
                b.TT(tB, cm3, tA, OP.mult)
                b.RED(c1f.rearrange("p (s u) -> p s u", u=1), tB, OP.add)
                b.RED(c2f.rearrange("p (s u) -> p s u", u=1), cm3, OP.add)
                b.TT(c2f, c2f, c1f, OP.subtract, inc="sv")
                idx_ev = ("sv", b.ctr["sv"])

            ztlA = b.AB(116.0, 1024)
            ztlB = b.AB(114.0, 1024)
            scat_evs = []
            ztl_free = [(), ()]

            def igather(out_ap, in_ap, idx_ap, gate_evs=()):
                i = b.dma_ring["gpsimd"]
                b.dma_ring["gpsimd"] = (i + 1) % 8
                sm = "sg%d" % i
                if sm in b.dma_last:
                    b.wait("gpsimd", (sm, b.dma_last[sm]))
                for gev in gate_evs:
                    b.wait("gpsimd", gev)
                def mk(e, out=out_ap, in_=in_ap, idx=idx_ap):
                    if idx[1] == "out":
                        return e.indirect_dma_start(
                            out=out, out_offset=bass.IndirectOffsetOnAxis(idx[0], 0),
                            in_=in_, in_offset=None)
                    return e.indirect_dma_start(
                        out=out, out_offset=None,
                        in_=in_, in_offset=bass.IndirectOffsetOnAxis(idx[0], 0))
                ev = b.emit("gpsimd", mk, inc=sm, n=16)
                b.dma_last[sm] = ev[1]
                return ev

            if True:
                for tc in range(16):
                    ztl = (ztlA, ztlB)[tc % 2]
                    evac_evs = []
                    for c in range(NCH):
                        pslot, pi = b.pt_slot()
                        b.wait("tensor", b.pt_last_read.get(pi))
                        b.wait("tensor", b.pt_last_read.get((pi + 2) % 4))
                        if c == 0:
                            for fe in ztl_free[tc % 2]:
                                b.wait("tensor", fe)
                        b.TRANS(pslot, zn[c][:, tc * 128:(tc + 1) * 128], b.identb, inc="st")
                        tre = b.tlast
                        eng = "vector" if c % 2 == 0 else "scalar"
                        CP = b.VCOPY if eng == "vector" else b.SCOPY
                        sem = "sv" if eng == "vector" else "sc"
                        b.wait(eng, tre)
                        for fe in ztl_free[tc % 2]:
                            b.wait(eng, fe)
                        ev = CP(ztl[:, c * 128:(c + 1) * 128], pslot, inc=sem)
                        b.pt_last_read[pi] = ev
                        evac_evs.append(ev)
                    ev1 = igather(b.zslots, ztl, (s1u[:, tc:tc + 1], "out"),
                                  gate_evs=list(evac_evs) + [idx_ev])
                    ev2 = igather(b.zslots, ztl, (s2u[:, tc:tc + 1], "out"),
                                  gate_evs=[idx_ev])
                    scat_evs += [ev1, ev2]
                    ztl_free[tc % 2] = (ev1, ev2)

            zge = b.AB(104.0, 5120)
            G = b.AB(80.0, 10240)
            ysb = [b.AB(118.0, 1024), b.AB(120.0, 1024)]
            ywr_evs = []
            ysb_busy = [None, None]
            zld_free = [(), ()]
            last_y_ckpt = None
            if True:
                for e in range(8):
                    for k in range(5):
                        ztl = (ztlA, ztlB)[k % 2]
                        if e == 0 and k < 2:
                            for sev_ in scat_evs:
                                b.wait("gpsimd", sev_)
                        for fe in zld_free[k % 2]:
                            b.wait("gpsimd", fe)
                        if e == 0:
                            for fe in ztl_free[k % 2]:
                                b.wait("gpsimd", fe)
                        zl = b.dma("gpsimd", ztl,
                                   b.zslots[e * 656 + k * 128: e * 656 + (k + 1) * 128, :])
                        b.wait("tensor", zl)
                        kevs = []
                        for c in range(NCH):
                            pslot, pi = b.pt_slot()
                            b.wait("tensor", b.pt_last_read.get(pi))
                            b.wait("tensor", b.pt_last_read.get((pi + 2) % 4))
                            b.TRANS(pslot, ztl[:, c * 128:(c + 1) * 128], b.identb, inc="st")
                            tre = b.tlast
                            eng = "vector" if c % 2 == 0 else "scalar"
                            CP = b.VCOPY if eng == "vector" else b.SCOPY
                            sem = "sv" if eng == "vector" else "sc"
                            b.wait(eng, tre)
                            dst = zge[:, c * 640 + k * 128: c * 640 + (k + 1) * 128]
                            ev = CP(dst, pslot, inc=sem)
                            b.pt_last_read[pi] = ev
                            kevs.append(ev)
                        zld_free[k % 2] = tuple(kevs)
                    zge_evs = zld_free[0] + zld_free[1]

                    for hf in range(2):
                        kH = kseg; kseg += 1
                        pump(kH + 3)
                        sl = slots[kH % 3]
                        b.wait("tensor", load_evs[kH])
                        for gev in zge_evs:
                            b.wait("tensor", gev)
                        if last_y_ckpt is not None:
                            b.wait("scalar", last_y_ckpt)
                        for fl in range(8):
                            f = hf * 8 + fl
                            for nt in range(2):
                                nlo, nsz = (0, 512) if nt == 0 else (512, 128)
                                bk = (fl * 2 + nt) % 2
                                b.use_bank(bk)
                                for c in range(NCH):
                                    b.MM(b.bank(bk, 0, nsz),
                                         sl[:, c * 1024 + fl * 128: c * 1024 + (fl + 1) * 128],
                                         zge[:, c * 640 + nlo: c * 640 + nlo + nsz],
                                         c == 0, c == NCH - 1,
                                         inc=("st" if c == NCH - 1 else None))
                                mm_ev = b.tlast
                                b.wait("scalar", mm_ev)
                                gsl = G[:, f * 640 + nlo: f * 640 + nlo + nsz]
                                silu_ev = b.ACT(gsl, b.bank(bk, 0, nsz), AF.Silu, inc="sc")
                                b.read_bank(bk, silu_ev)
                        seg_done[kH] = b.ckpt("tensor", "st")
                    kY0 = kseg; kseg += 1
                    kY1 = kseg; kseg += 1
                    pump(kY0 + 3)
                    slA = slots[kY0 % 3]
                    slB = slots[kY1 % 3]
                    b.wait("tensor", load_evs[kY0])
                    b.wait("tensor", load_evs[kY1])
                    b.wait("tensor", ("sc", b.ctr["sc"]))
                    for st6 in range(5):
                        ypar = st6 % 2
                        yevs = []
                        for dn in range(2):
                            bk = 2 + (st6 * 2 + dn) % 4
                            b.use_bank(bk)
                            for f in range(16):
                                w2sl = slA if f < 8 else slB
                                fl = f % 8
                                b.MM(b.bank(bk),
                                     G[:, f * 640 + st6 * 128: f * 640 + (st6 + 1) * 128],
                                     w2sl[:, fl * 1024 + dn * 512: fl * 1024 + (dn + 1) * 512],
                                     f == 0, f == 15,
                                     inc=("st" if f == 15 else None))
                            mm_ev = b.tlast
                            eng = "vector" if dn == 0 else "scalar"
                            CP = b.VCOPY if eng == "vector" else b.SCOPY
                            sem = "sv" if eng == "vector" else "sc"
                            b.wait(eng, mm_ev)
                            if ysb_busy[ypar] is not None:
                                b.wait(eng, ysb_busy[ypar])
                            ev = CP(ysb[ypar][:, dn * 512:(dn + 1) * 512], b.bank(bk),
                                    inc=sem)
                            b.read_bank(bk, ev)
                            yevs.append(ev)
                        for yev_ in yevs:
                            b.wait("gpsimd", yev_)
                        wev = b.dma(
                            "gpsimd",
                            b.yslots[e * 656 + st6 * 128: e * 656 + (st6 + 1) * 128, :],
                            ysb[ypar])
                        ywr_evs.append(wev)
                        ysb_busy[ypar] = wev
                    last_y_ckpt = b.ckpt("tensor", "st")
                    seg_done[kY0] = seg_done[kY1] = last_y_ckpt
                    pump(kY1 + 3)

            gbuf = [(b.AB(80.0, 1024), b.AB(84.0, 1024)),
                    (b.AB(104.0, 1024), b.AB(108.0, 1024))]
            zd = [b.AF(88.0, 1024), b.AF(96.0, 1024)]
            zdb = [b.AB(92.0, 1024), b.AB(112.0, 1024)]
            zadd_done = [None, None]
            if True:
                def issue_gather(tc):
                    par = tc % 2
                    gbufA, gbufB = gbuf[par]
                    gevs = []
                    for sidx, gb in ((s1u, gbufA), (s2u, gbufB)):
                        gate = list(ywr_evs) if tc == 0 else []
                        if zadd_done[par] is not None:
                            gate.append(zadd_done[par])
                        gevs.append(igather(gb, b.yslots, (sidx[:, tc:tc + 1], "in"),
                                            gate_evs=gate))
                    return gevs

                pend = {0: issue_gather(0)}
                ywr_evs = []
                pend[1] = issue_gather(1)
                for tc in range(16):
                    par = tc % 2
                    gbufA, gbufB = gbuf[par]
                    for gev in pend[tc]:
                        b.wait("vector", gev)
                    b.TS(zd[par], gbufA, c1f[:, tc:tc + 1], None, OP.mult)
                    b.STT(zd[par], gbufB, c2f[:, tc:tc + 1], zd[par], OP.mult, OP.add)
                    b.VCOPY(zdb[par], zd[par], inc="sv")
                    zdb_ev = ("sv", b.ctr["sv"])
                    b.wait("tensor", zdb_ev)
                    for c in range(NCH):
                        pslot, pi = b.pt_slot()
                        b.wait("tensor", b.pt_last_read.get(pi))
                        b.wait("tensor", b.pt_last_read.get((pi + 2) % 4))
                        b.TRANS(pslot, zdb[par][:, c * 128:(c + 1) * 128], b.identb,
                                inc="st")
                        tre = b.tlast
                        b.wait("vector", tre)
                        ev = b.TT(b.xc(c, tc * 128, (tc + 1) * 128), pslot,
                                  b.xc(c, tc * 128, (tc + 1) * 128), OP.add, inc="sv")
                        b.pt_last_read[pi] = ev
                    zadd_done[par] = ("sv", b.ctr["sv"])
                    if tc + 2 < 16:
                        pend[tc + 2] = issue_gather(tc + 2)
            b.barrier()
            b.barrier()

        stage = b.arf[:, int(32.0 * 256):int(32.0 * 256) + 8192]
        b.wait("sync", seg_done[-1])
        for wave in range(2):
            rev = b.dma("sync", stage.rearrange("p (c t) -> p c t", c=4),
                        b.xspill.rearrange("(c p) t -> p c t", p=128)[:, wave * 4:(wave + 1) * 4, :])
            b.wait("vector", rev)
            for c4 in range(4):
                c = wave * 4 + c4
                b.TT(b.xc(c), stage[:, c4 * 2048:(c4 + 1) * 2048], b.xc(c), OP.add, inc="sv")
            if wave == 0:
                b.wait("sync", ("sv", b.ctr["sv"]))
        b.barrier()


    def ph5_rwkv(self):
        b = self
        xln3 = b.RA
        xsh = [b.AF(32.0 + c * 8.0, 512 * 4) for c in range(NCH)]
        tmp = b.AF(96.5, 512)
        tmp2 = b.AF(112.5, 512)
        wsh = b.AB(98.5, 8192)
        tmp2 = b.AF(114.5, 512)
        b.wait("gpsimd", b.gate_t)
        ld_wsh = b.dma("gpsimd", wsh.rearrange("p (c n) -> p c n", c=NCH),
                       b.wshiftT.rearrange("(c p) n -> p c n", p=128))

        def fin_ln3(tt, lo, mu, r, mur):
            for c in range(NCH):
                b.TT(tmp, b.xc(c, lo, lo + 512), r, OP.mult)
                b.TT(xln3[c][:, lo:lo + 512], tmp, mur, OP.subtract)

        b.ln_stats([b.xc(c) for c in range(NCH)], sqkb=64.5, eps=1e-5,
                   fin=fin_ln3, base=116.5)
        b.ckpt("vector", "sv")
        xln3_ev = ("sv", b.ctr["sv"])
        b.wait("tensor", xln3_ev)
        b.wait("tensor", ld_wsh)

        for oc in range(NCH):
            for tt in range(4):
                bk = tt % 2
                b.use_bank(bk)
                rot = (tt * 512 + 1024) % 2048
                for c in range(NCH):
                    b.MM(b.bank(bk), wsh[:, c * 1024 + oc * 128: c * 1024 + (oc + 1) * 128],
                         xln3[c][:, rot:rot + 512], c == 0, c == NCH - 1,
                         inc=("st" if c == NCH - 1 else None))
                mm_ev = b.tlast
                b.wait("vector", mm_ev)
                b.TS(tmp2, xln3[oc][:, tt * 512:(tt + 1) * 512],
                     b.GC[:, NCH + oc:NCH + oc + 1], None, OP.mult)
                ev = b.STT(xsh[oc][:, tt * 512:(tt + 1) * 512], b.bank(bk),
                           b.GC[:, oc:oc + 1], tmp2, OP.mult, OP.add, inc="sv")
                b.read_bank(bk, ev)
        b.ckpt("vector", "sv")
        xsh_ev = ("sv", b.ctr["sv"])
        b.wait("scalar", xsh_ev)
        b.wait("tensor", xsh_ev)

        def fin_lni(tt, lo, mu, r, mur):
            for c in range(NCH):
                b.TT(tmp, xsh[c][:, lo:lo + 512], r, OP.mult)
                b.TT(tmp2, tmp, mur, OP.subtract)
                b.TS(xln3[c][:, lo:lo + 512], tmp2, b.colv("lnw", c),
                     b.colv("lnb", c), OP.mult, OP.add)

        b.ln_stats(xsh, sqkb=104.5, eps=1e-5, fin=fin_lni, base=116.5)
        b.ckpt("vector", "sv")
        xn_ev = ("sv", b.ctr["sv"])
        b.wait("tensor", xn_ev)

        shift_done = b.ckpt("tensor", "st")
        wkv_s = b.AB(122.0, 1024)
        b.wait("gpsimd", shift_done)
        ld_wkv = b.dma("gpsimd", wkv_s.rearrange("p (c n) -> p c n", c=NCH),
                       b.wkvT.rearrange("(c p) n -> p c n", p=128))
        wout_s = b.AB(100.5, 1024)
        ld_wout = b.dma("gpsimd", wout_s[0:64, :], b.woutT)
        wkvsb = b.AF(104.5, 2048)
        ssg = b.AF(112.5, 512)
        b.wait("tensor", ld_wkv)
        for tt in range(4):
            bk = 2 + tt % 2
            b.use_bank(bk)
            for c in range(NCH):
                b.MM(b.bank(bk), wkv_s[:, c * 128:(c + 1) * 128],
                     xln3[c][:, tt * 512:(tt + 1) * 512], c == 0, c == NCH - 1,
                     inc=("st" if c == NCH - 1 else None))
            mm_ev = b.tlast
            b.wait("scalar", mm_ev)
            b.ACT(ssg[0:64, :], b.bank(bk)[0:64, :], AF.Sigmoid)
            b.ACT(ssg[0:64, :], ssg[0:64, :], AF.Exp, scale=b.mcol(43)[0:64, :], inc="sc")
            b.wait("vector", ("sc", b.ctr["sc"]))
            ev = b.TT(wkvsb[0:64, tt * 512:(tt + 1) * 512], ssg[0:64, :],
                      b.bank(bk)[64:128, :], OP.mult, inc="sv")
            b.read_bank(bk, ev)
            b.wait("scalar", ("sv", b.ctr["sv"]))
        states = b.AF(32.0, 2048)
        wrow = b.AF(40.0, 2048)
        statesbf = b.AB(96.5, 2048)
        b.TS(wrow[0:64, :], wkvsb[0:64, :], 0.0, b.mcol(41)[0:64, :],
             OP.mult, OP.add, inc="sv")
        b.SCAN(states[0:64, :], wrow[0:64, :], wkvsb[0:64, :])
        b.VCOPY(statesbf[0:64, :], states[0:64, :], inc="sv")
        st_ev = ("sv", b.ctr["sv"])
        b.wait("tensor", st_ev)
        b.wait("tensor", ld_wout)
        for oc in range(NCH):
            for tt in range(4):
                bk = tt % 2
                b.use_bank(bk)
                b.MM(b.bank(bk), wout_s[0:64, oc * 128:(oc + 1) * 128],
                     statesbf[0:64, tt * 512:(tt + 1) * 512], True, True, inc="st")
                mm_ev = b.tlast
                b.wait("vector", mm_ev)
                ev = b.TT(b.xc(oc, tt * 512, (tt + 1) * 512), b.bank(bk),
                          b.xc(oc, tt * 512, (tt + 1) * 512), OP.add, inc="sv")
                b.read_bank(bk, ev)
        b.barrier()

    def ph6_lffn(self):
        b = self
        xl = b.RA
        mu = b.AF(116.5, 512)
        r = b.AF(118.5, 512)
        mur = b.AF(120.5, 512)
        var = b.AF(114.5, 512)
        r2 = b.AF(110.5, 512)
        mur2 = b.AF(112.5, 512)
        tmp = b.AF(96.5, 512)
        tmp2 = b.AF(98.5, 512)
        srow = b.AF(48.0, 2048)

        b.wait("scalar", b.ld["b1l"])
        b.wait("scalar", b.ld["b1h"])
        for tt in range(4):
            lo = tt * 512
            sq_evs = []
            for c in range(NCH):
                sq = b.AB(64.0 + c * 1.0, 512)
                sq_evs.append((sq, b.ACT(sq, b.xc(c, lo, lo + 512), AF.Square, inc="sc")))
            b.use_bank(0)
            b.use_bank(1)
            sum_ev = None
            for c in range(NCH):
                sum_ev = b.MM(b.bank(0), b.onesf, b.xc(c, lo, lo + 512),
                              c == 0, c == NCH - 1,
                              inc=("st" if c == NCH - 1 else None))
            for c in range(NCH):
                b.wait("tensor", sq_evs[c][1])
                b.MM(b.bank(1), b.onesb, sq_evs[c][0], c == 0, c == NCH - 1,
                     inc=("st" if c == NCH - 1 else None))
            mm_ev = b.tlast
            b.wait("vector", sum_ev)
            b.wait("vector", mm_ev)
            b.TS(mu, b.bank(0), 1.0 / D, None, OP.mult)
            b.TT(var, mu, mu, OP.mult)
            b.TS(tmp, b.bank(1), 1.0 / D, None, OP.mult)
            b.TT(var, tmp, var, OP.subtract, inc="sv")
            var_ev = ("sv", b.ctr["sv"])
            b.wait("scalar", var_ev)
            b.ACT(r, var, AF.Ln, bias=1e-5)
            b.ACT(r, r, AF.Exp, scale=-0.5)
            b.ACT(r2, var, AF.Ln)
            b.ACT(r2, r2, AF.Exp, scale=-0.5, inc="sc")
            r_ev = ("sc", b.ctr["sc"])
            b.wait("vector", r_ev)
            b.TT(mur, mu, r, OP.mult)
            b.TT(mur2, mu, r2, OP.mult, inc="sv")
            bank_free = b.ckpt("vector", "sv")
            b.read_bank(0, bank_free)
            b.read_bank(1, bank_free)
            b.use_bank(2)
            for c in range(NCH):
                b.TT(tmp, b.xc(c, lo, lo + 512), r, OP.mult)
                ev = b.TT(tmp2, tmp, mur, OP.subtract, inc="sv")
                b.wait("tensor", ev)
                b.MM(b.bank(2)[0:1, :], b.colv("rtok", c), tmp2, c == 0, c == NCH - 1,
                     inc="st")
                b.wait("vector", b.tlast)
                b.TT(tmp, b.xc(c, lo, lo + 512), r2, OP.mult)
                b.TT(tmp2, tmp, mur2, OP.subtract)
                b.TS(xl[c][:, lo:lo + 512], tmp2, b.colv("gl", c), None, OP.mult, inc="sv")
            sev = b.tlast
            b.wait("scalar", sev)
            scop = b.SCOPY(srow[0:1, lo:lo + 512], b.bank(2)[0:1, :], inc="sc")
            b.read_bank(2, scop)
            b.wait("vector", ("sc", b.ctr["sc"]))
        b.ckpt("vector", "sv")
        xl_ev = ("sv", b.ctr["sv"])
        b.ckpt("scalar", "sc")
        srow_ev = ("sc", b.ctr["sc"])
        b.wait("sync", srow_ev)

        s_tm = b.AF(123.0, 16)
        b.wait("sync", xl_ev)
        bev = b.dma("sync", b.srow_d, srow[0:1, :])
        b.wait("sync", bev)
        def _stm(e, out=s_tm, in_=b.srow_d.rearrange("u (tc p) -> (u p) tc", p=128)):
            return e.dma_start(out=out, in_=in_, allow_slow_non_contiguous=True)
        i = b.dma_ring["sync"]; b.dma_ring["sync"] = (i + 1) % 8
        _sm = "ss%d" % i
        if _sm in b.dma_last:
            b.wait("sync", (_sm, b.dma_last[_sm]))
        bev2 = b.emit("sync", _stm, inc=_sm, n=16)
        b.dma_last[_sm] = bev2[1]
        b.wait("vector", bev2)

        theta = b.mcol(38)
        step = b.mcol(39)
        dcol = b.mcol(40)
        trial = b.mcol(44)
        pc = b.mcol(45)
        gsc = b.AF(123.2, 16)
        b.RED(pc, s_tm, OP.max, absv=True, inc="sv")
        b.wait("tensor", ("sv", b.ctr["sv"]))
        b.use_bank(3)
        b.MM(b.bank(3)[:, 0:1], b.onesf, pc, True, True, inc="st")
        b.wait("vector", b.tlast)
        b.TS(theta, b.bank(3)[:, 0:1], -1.0, None, OP.mult)
        ev = b.TS(step, b.bank(3)[:, 0:1], 1.0, None, OP.mult, inc="sv")
        b.read_bank(3, ev)
        for it in range(48):
            b.TT(trial, theta, step, OP.add)
            b.TS(gsc, s_tm, trial, None, OP.is_ge)
            b.RED(pc, gsc, OP.add, inc="sv")
            b.wait("tensor", ("sv", b.ctr["sv"]))
            b.use_bank(3)
            b.MM(b.bank(3)[:, 0:1], b.onesf, pc, True, True, inc="st")
            b.wait("vector", b.tlast)
            b.TS(dcol, b.bank(3)[:, 0:1], 1024.0, None, OP.is_ge, inc="sv")
            b.read_bank(3, ("sv", b.ctr["sv"]))
            b.STT(theta, dcol, step, theta, OP.mult, OP.add)
            b.TS(step, step, 0.5, None, OP.mult)
        b.ckpt("vector", "sv")
        th_ev = ("sv", b.ctr["sv"])

        w1l_s = b.AB(96.5, 4096)
        b.wait("gpsimd", b.gate_t)
        b.wait("gpsimd", xl_ev)
        ld_w1l = b.dma("gpsimd", w1l_s.rearrange("p (c n) -> p c n", c=NCH),
                       b.w1l.rearrange("(c p) n -> p c n", p=128))
        w2l_s = b.AB(104.5, 4096)
        ld_w2l = b.dma("gpsimd", w2l_s.rearrange("p (c n) -> p c n", c=4),
                       b.w2l.rearrange("(c p) n -> p c n", p=128))
        Gl = b.AB(64.0, 8192)
        b.wait("tensor", xl_ev)
        b.wait("tensor", ld_w1l)
        for hc in range(4):
            for tt in range(4):
                bk = tt % 2
                b.use_bank(bk)
                for c in range(NCH):
                    b.MM(b.bank(bk), w1l_s[:, c * 512 + hc * 128: c * 512 + (hc + 1) * 128],
                         xl[c][:, tt * 512:(tt + 1) * 512], c == 0, c == NCH - 1,
                         inc=("st" if c == NCH - 1 else None))
                mm_ev = b.tlast
                b.wait("scalar", mm_ev)
                ev = b.ACT(Gl[:, hc * 2048 + tt * 512: hc * 2048 + (tt + 1) * 512],
                           b.bank(bk), AF.Gelu, bias=b.MISC[:, hc:hc + 1], inc="sc")
                b.read_bank(bk, ev)
        w1l_done = b.ckpt("tensor", "st")
        gl_ev = ("sc", b.ctr["sc"])
        b.wait("tensor", gl_ev)
        b.wait("tensor", ld_w2l)
        tmpl = b.AF(121.0, 512)
        for oc in range(NCH):
            for tt in range(4):
                bk = 2 + tt % 2
                b.use_bank(bk)
                for hc in range(4):
                    b.MM(b.bank(bk), w2l_s[:, hc * 1024 + oc * 128: hc * 1024 + (oc + 1) * 128],
                         Gl[:, hc * 2048 + tt * 512: hc * 2048 + (tt + 1) * 512],
                         hc == 0, hc == 3, inc=("st" if hc == 3 else None))
                mm_ev = b.tlast
                b.wait("vector", mm_ev)
                b.TS(tmpl, b.bank(bk), b.colv("b2l", oc), None, OP.add)
                ev = b.TT(b.xc(oc, tt * 512, (tt + 1) * 512), tmpl,
                          b.xc(oc, tt * 512, (tt + 1) * 512), OP.add, inc="sv")
                b.read_bank(bk, ev)
        light_done = b.ckpt("tensor", "st")

        mrow = b.AF(112.5, 2048)
        mrowb = b.AB(120.5, 2048)
        b.wait("vector", th_ev)
        b.TS(mrow[0:1, :], srow[0:1, :], theta[0:1, :], None, OP.is_ge)
        b.VCOPY(mrowb[0:1, :], mrow[0:1, :], inc="sv")
        mrow_ev = ("sv", b.ctr["sv"])
        b.wait("tensor", mrow_ev)
        b.wait("tensor", light_done)
        maskb = b.AF(96.5, 2048)
        for tt in range(4):
            b.use_bank(0)
            mevs = []
            for i in range(4):
                tc = tt * 4 + i
                mevs.append(b.MM(b.bank(0)[:, i * 128:(i + 1) * 128], b.onesb[0:1, :],
                                 mrowb[0:1, tc * 128:(tc + 1) * 128], True, True,
                                 inc="st"))
            for me_ in mevs:
                b.wait("scalar", me_)
            mev = b.ACT(maskb[:, tt * 512:(tt + 1) * 512], b.bank(0), AF.Copy, inc="sc")
            b.read_bank(0, mev)
        mask_ev = ("sc", b.ctr["sc"])
        b.wait("vector", mask_ev)

        slots = [b.AB(32.0, 8192), b.AB(48.0, 8192)]
        slot_free = [b.gate_t, mrow_ev]
        Gh = b.AB(64.0, 16384)
        w1r = b.w1h.rearrange("(c p) n -> p c n", p=128)
        w2r = b.w2h.rearrange("(j p) d -> p j d", p=128)
        b.wait("gpsimd", bev)
        nload = 0
        for g in range(4):
            si = nload % 2
            nload += 1
            b.wait("gpsimd", slot_free[si])
            ldev = b.dma("gpsimd", slots[si].rearrange("p (c n) -> p c n", c=NCH),
                         w1r[:, :, g * 1024:(g + 1) * 1024])
            b.wait("vector", ldev)
            for c in range(NCH):
                b.TS(slots[si][:, c * 1024:(c + 1) * 1024],
                     slots[si][:, c * 1024:(c + 1) * 1024],
                     b.GC[:, 2 * NCH + c:2 * NCH + c + 1], None, OP.mult,
                     inc=("sv" if c == NCH - 1 else None))
            rat_ev = ("sv", b.ctr["sv"])
            b.wait("tensor", rat_ev)
            for hcl in range(8):
                hc = g * 8 + hcl
                for tt in range(4):
                    bk = (hcl * 4 + tt) % 2
                    b.use_bank(bk)
                    for c in range(NCH):
                        b.MM(b.bank(bk),
                             slots[si][:, c * 1024 + hcl * 128: c * 1024 + (hcl + 1) * 128],
                             xl[c][:, tt * 512:(tt + 1) * 512], c == 0, c == NCH - 1,
                             inc=("st" if c == NCH - 1 else None))
                    mm_ev = b.tlast
                    b.wait("scalar", mm_ev)
                    ev = b.ACT(Gh[:, hcl * 2048 + tt * 512: hcl * 2048 + (tt + 1) * 512],
                               b.bank(bk), AF.Gelu, bias=b.MISC[:, 4 + hc:5 + hc],
                               inc="sc")
                    b.read_bank(bk, ev)
            slot_free[si] = b.ckpt("tensor", "st")
            gh_ev = ("sc", b.ctr["sc"])
            si = nload % 2
            nload += 1
            b.wait("gpsimd", slot_free[si])
            ldev = b.dma("gpsimd", slots[si].rearrange("p (j d) -> p j d", j=8),
                         w2r[:, g * 8:(g + 1) * 8, :])
            b.wait("tensor", ldev)
            b.wait("tensor", gh_ev)
            for dchunk in range(8):
                for tt in range(4):
                    bk = 2 + (dchunk * 4 + tt) % 4
                    b.use_bank(bk)
                    for j in range(8):
                        b.MM(b.bank(bk),
                             slots[si][:, j * 1024 + dchunk * 128: j * 1024 + (dchunk + 1) * 128],
                             Gh[:, j * 2048 + tt * 512: j * 2048 + (tt + 1) * 512],
                             j == 0, j == 7, inc=("st" if j == 7 else None))
                    mm_ev = b.tlast
                    b.wait("vector", mm_ev)
                    if g == 0:
                        b.TS(tmpl, b.bank(bk), b.colv("b2h", dchunk), None, OP.add)
                        b.TT(tmpl, tmpl, maskb[:, tt * 512:(tt + 1) * 512], OP.mult)
                    else:
                        b.TT(tmpl, b.bank(bk), maskb[:, tt * 512:(tt + 1) * 512],
                             OP.mult)
                    ev = b.TT(b.xc(dchunk, tt * 512, (tt + 1) * 512), tmpl,
                              b.xc(dchunk, tt * 512, (tt + 1) * 512), OP.add, inc="sv")
                    b.read_bank(bk, ev)
            slot_free[si] = b.ckpt("tensor", "st")
        b.barrier()

    def ph7_store(self):
        b = self
        b.wait("sync", ("sv", b.ctr["sv"]))
        b.wait("sync", b.tlast)
        ev = b.dma("sync", b.y_out.rearrange("(c p) t -> p c t", p=128),
                   b.xs_a.rearrange("p (c t) -> p c t", c=NCH), sem="so")
        b.wait("gpsimd", ev)

    def build(self):
        self.ph0_loads()
        self.ph1_lambda_ln1()
        self.ph1b_v()
        self.ph2_attention()
        self.ph3_attnout()
        self.ph4_moe()
        self.ph5_rwkv()
        self.ph6_lffn()
        self.ph7_store()
        nc = self.nc
        with nc.Block() as block:
            for eng in ENGINES:
                dec = getattr(block, eng)
                ops = self.stream[eng]
                def make(ops):
                    def _(e):
                        for op in ops:
                            op(e)
                    return _
                dec(make(ops))
        self.es.close()
        return nc


_CACHE = {}


def build_program():
    if "nc" not in _CACHE:
        _CACHE["nc"] = MK().build()
    return _CACHE["nc"]


def kernel(**inputs):
    g = lambda n: np.asarray(inputs[n], dtype=np.float32)
    cT = np.ascontiguousarray

    wqkvT = cT(np.concatenate([g("L0_attn_Wq").T, g("L0_attn_Wk").T,
                               g("L0_attn_Wv").T], axis=1))
    woT = cT(g("L0_attn_Wo").T)
    lam = np.zeros((1, 128), np.float32)
    lam[0, 0:32] = g("L0_attn_lambda_q1")
    lam[0, 32:64] = g("L0_attn_lambda_k1")
    lam[0, 64:96] = g("L0_attn_lambda_q2")
    lam[0, 96:128] = g("L0_attn_lambda_k2")
    sublnw4 = cT(np.tile(np.broadcast_to(g("L0_attn_subln_w"), (128, 64)), (1, 4)))
    w1T = cT(g("L0_ffn_w1").transpose(0, 2, 1))
    w2T = cT(g("L0_ffn_w2").transpose(0, 2, 1))
    wmm = np.zeros((D, 16), np.float32)
    wmm[:, 0:4] = g("L0_ffn_w_macro").T
    wmm[:, 4:12] = g("L0_ffn_w_micro").reshape(8, D).T
    wmmT = cT(wmm)
    wshiftT = cT(g("L1_attn_W_shift").T)
    wkvT = cT(np.concatenate([g("L1_attn_W_key").T, g("L1_attn_W_value").T], axis=1))
    woutT = cT(g("L1_attn_W_output").T)
    tdec = cT(g("L1_attn_time_decay").reshape(64, 1))
    tfirst = cT(g("L1_attn_time_first").reshape(64, 1))
    w1l = cT(g("L1_ffn_w1_light"))
    w2l = cT(g("L1_ffn_w2_light"))
    w1h = cT(g("L1_ffn_w1_heavy"))
    w2h = cT(g("L1_ffn_w2_heavy"))
    b1l = cT(np.broadcast_to(g("L1_ffn_b1_light").reshape(4, 128).T, (128, 4)))
    b1h = cT(np.broadcast_to(g("L1_ffn_b1_heavy").reshape(32, 128).T, (128, 32)))
    colpack = np.zeros((128, 9 * NCH), np.float32)
    for i, v in enumerate([g("L0_ffn_norm_w"), g("L1_attn_shift_gate"),
                           g("L1_attn_ln_w"), g("L1_attn_ln_b"),
                           g("L1_ffn_routing_token"), g("L1_ffn_gamma_light"),
                           g("L1_ffn_gamma_heavy"), g("L1_ffn_b2_light"),
                           g("L1_ffn_b2_heavy")]):
        colpack[:, i * NCH:(i + 1) * NCH] = v.reshape(NCH, 128).T
    colpack = cT(colpack)
    eye = np.eye(128, dtype=np.float32)
    ltri = np.zeros((128, 16), np.float32)
    for m in range(8):
        ltri[:m, m] = 1.0
    ltri[0:8, 8] = np.arange(8, dtype=np.float32) * 656.0
    ltri = cT(ltri)
    constv = cT(np.broadcast_to(np.array([0.0, 1.0, -0.5, 1e-5, 1e-6,
                                          SCALING, 1.0 / 64.0,
                                          math.log(1.0 - LAMBDA_INIT)],
                                         np.float32), (128, 8)))
    onesin = np.ones((128, 128), np.float32)

    x = g("L0_attn_x")
    nc = build_program()
    in_maps = []
    for bb in range(8):
        in_maps.append(dict(
            x=cT(x[bb].T), wqkvT=wqkvT, woT=woT, lam=lam, sublnw4=sublnw4,
            constv=constv,
            w1T=w1T, w2T=w2T, wmmT=wmmT, wshiftT=wshiftT, wkvT=wkvT,
            woutT=woutT, tdec=tdec, tfirst=tfirst, w1l=w1l, b1l=b1l, w2l=w2l,
            w1h=w1h, b1h=b1h, w2h=w2h, colpack=colpack, eye=eye, onesin=onesin,
            ltri=ltri,
        ))
    res = run_bass_kernel_spmd(nc, in_maps, core_ids=list(range(8)))
    out = np.empty((8, T, D), np.float32)
    for bb in range(8):
        out[bb] = np.asarray(res.results[bb]["y"], np.float32).T
    return out


# version 11
# speedup vs baseline: 1.2204x; 1.2204x over previous
import math
import numpy as np
import concourse.bass as bass
import concourse.mybir as mybir
from concourse.bass_utils import run_bass_kernel_spmd
from concourse.alu_op_type import AluOpType as OP

F32 = mybir.dt.float32
BF16 = mybir.dt.bfloat16
AF = mybir.ActivationFunctionType
AX = mybir.AxisListType.X

T = 2048
D = 1024
NCH = 8
LAMBDA_INIT = 0.8 - 0.6 * math.exp(-0.3 * 12)
SCALING = 32.0 ** -0.5

ENGINES = ("sync", "tensor", "vector", "scalar", "gpsimd")


class MK:
    def __init__(self):
        self.nc = bass.Bass("TRN2", target_bir_lowering=False)
        from contextlib import ExitStack
        self.es = ExitStack()
        nc = self.nc
        self.stream = {e: [] for e in ENGINES}
        self.sem = {}
        self.ctr = {}
        names = ["sv", "sc", "so"]
        names += ["st%d" % i for i in range(8)]
        names += ["sg%d" % i for i in range(8)] + ["ss%d" % i for i in range(8)]
        for name in names:
            self.sem[name] = self.es.enter_context(nc.semaphore(name))
            self.ctr[name] = 0
        self.dma_ring = {"gpsimd": 0, "sync": 0}
        self.dma_last = {}
        self.t_ring = 0
        self.tring_n = 1
        self.tlast = None

        def din(name, shape):
            return nc.dram_tensor(name, shape, F32, kind="ExternalInput").ap()

        self.x_in = din("x", [D, T])
        self.wqkvT = din("wqkvT", [D, 3 * D])
        self.woT = din("woT", [D, D])
        self.lam = din("lam", [1, 128])
        self.sublnw4 = din("sublnw4", [128, 256])
        self.w1T = din("w1T", [8, D, 2048])
        self.w2T = din("w2T", [8, 2048, D])
        self.wmmT = din("wmmT", [D, 16])
        self.wshiftT = din("wshiftT", [D, D])
        self.wkvT = din("wkvT", [D, 128])
        self.woutT = din("woutT", [64, D])
        self.tdec = din("tdec", [64, 1])
        self.tfirst = din("tfirst", [64, 1])
        self.w1l = din("w1l", [D, 512])
        self.b1l = din("b1l", [128, 4])
        self.w2l = din("w2l", [512, D])
        self.w1h = din("w1h", [D, 4096])
        self.b1h = din("b1h", [128, 32])
        self.w2h = din("w2h", [4096, D])
        self.colpack = din("colpack", [128, 9 * NCH])
        self.ltri = din("ltri", [128, 16])
        self.constv = din("constv", [128, 8])
        self.eye = din("eye", [128, 128])
        self.onesin = din("onesin", [128, 128])

        self.y_out = nc.dram_tensor("y", [D, T], F32, kind="ExternalOutput").ap()
        self.xspill = nc.dram_tensor("xspill", [D, T], F32, kind="Internal").ap()
        self.srow_d = nc.dram_tensor("srow_d", [1, T], F32, kind="Internal").ap()
        self.crow_d = nc.dram_tensor("crow_d", [16, 8, 128], BF16, kind="Internal").ap()
        self.zslots = nc.dram_tensor("zslots", [5248, 1024], BF16, kind="Internal").ap()
        self.yslots = nc.dram_tensor("yslots", [5248, 1024], BF16, kind="Internal").ap()
        self.w1scr = nc.dram_tensor("w1scr", [2048, 8192], BF16, kind="Internal").ap()
        self.w2scr = nc.dram_tensor("w2scr", [2048, 8192], BF16, kind="Internal").ap()
        self.rb2 = nc.dram_tensor("rb2", [4, T], F32, kind="Internal").ap()

        self.xs_t = self.es.enter_context(nc.sbuf_tensor("xs", [128, NCH * T], F32))
        self.ar_t = self.es.enter_context(nc.sbuf_tensor("ar", [128, 65536], BF16))
        self.xs_a = self.xs_t.ap()
        self.ar = self.ar_t.ap()
        self.arf = self.ar.bitcast(F32)

        self.PS = self.es.enter_context(nc.psum_tensor("PS", [128, 3072], F32))
        self.PT = self.es.enter_context(nc.psum_tensor("PT", [128, 2048], BF16))
        self.ps = self.PS.ap()
        self.pt = self.PT.ap()

        self.identb = self.AB(124.5, 128)
        self.identf = self.AF(127.15, 128)
        self.onesf = self.AF(124.75, 128)
        self.onesb = self.AB(125.25, 128)
        self.subw_s = self.AF(125.5, 256)
        self.COLS = {}
        for i, n in enumerate(["normw", "sgate", "lnw", "lnb", "rtok", "gl", "gh",
                               "b2l", "b2h"]):
            self.COLS[n] = self.AF(126.5 + i * 0.03125, NCH)
        self.GC = self.AF(126.8, 3 * NCH)
        self.MISC = self.AF(126.9, 64)
        self.CONSTS = [0.0, 1.0, -0.5, 1e-5, 1e-6, SCALING, 1.0 / 64.0,
                       math.log(1.0 - LAMBDA_INIT)]
        cv = self.MISC[:, 46:54]
        self.cv = cv
        for i, v in enumerate(self.CONSTS):
            nc.const_aps.register(F32, v, cv[:, i:i + 1])
        self.RA = [self.AB(c * 4.0, T) for c in range(NCH)]
        self.ld = {}
        self.bank_last_read = {}
        self.pt_last_read = {}
        self.pt_idx = 0

    def AB(self, kb, cols):
        o = int(round(kb * 512))
        assert o + cols <= 65536, (kb, cols)
        return self.ar[:, o:o + cols]

    def AF(self, kb, cols):
        o = int(round(kb * 256))
        assert o + cols <= 32768, (kb, cols)
        return self.arf[:, o:o + cols]

    def xc(self, c, lo=0, hi=T):
        return self.xs_a[:, c * T + lo:c * T + hi]

    def bank(self, i, lo=0, hi=512):
        return self.ps[:, i * 512 + lo:i * 512 + hi]

    def colv(self, name, c=0):
        return self.COLS[name][:, c:c + 1]

    def gcv(self, which, c):
        return self.GC[:, which * NCH + c:which * NCH + c + 1]

    def mcol(self, i):
        return self.MISC[:, i:i + 1]

    OWN = {"vector": "sv", "scalar": "sc", "tensor": "st"}

    def emit(self, eng, fn, inc=None, n=1):
        own = self.OWN.get(eng)
        if eng in ("vector", "scalar"):
            assert inc in (None, own), (eng, inc)
            prev = self.ctr[own]
            h = self.sem[own]
            if prev > 0:
                self.stream[eng].append(
                    lambda e, h=h, prev=prev: e.wait_ge(h, prev))
            self.ctr[own] += 1
            def op(e, fn=fn, h=h):
                fn(e).then_inc(h, 1)
            self.stream[eng].append(op)
            return (own, self.ctr[own])
        if inc is None:
            self.stream[eng].append(fn)
            return None
        if eng == "tensor":
            assert inc == "st"
            inc = "st%d" % (self.t_ring % self.tring_n)
            self.t_ring += 1
        h = self.sem[inc]
        if eng == "tensor" and self.ctr[inc] > 0:
            prev = self.ctr[inc]
            self.stream[eng].append(lambda e, h=h, prev=prev: e.wait_ge(h, prev))
        self.ctr[inc] += n
        def op2(e, fn=fn, h=h, n=n):
            fn(e).then_inc(h, n)
        self.stream[eng].append(op2)
        ev = (inc, self.ctr[inc])
        if eng == "tensor":
            self.tlast = ev
        return ev

    def ckpt(self, eng, sem=None):
        if eng in ("vector", "scalar"):
            return (self.OWN[eng], self.ctr[self.OWN[eng]])
        for i in range(8):
            s = "st%d" % i
            if self.ctr[s] > 0:
                self.wait("tensor", (s, self.ctr[s]))
        return self.emit(eng, lambda e: e.nop(), inc="st")

    def wait(self, eng, ev):
        if ev is None:
            return
        sem, val = ev
        h = self.sem[sem]
        self.stream[eng].append(lambda e, h=h, val=val: e.wait_ge(h, val))

    def dma(self, eng, out, in_, sem=None):
        if sem is None:
            pfx = "sg" if eng == "gpsimd" else "ss"
            i = self.dma_ring[eng]
            self.dma_ring[eng] = (i + 1) % 8
            sem = "%s%d" % (pfx, i)
            if sem in self.dma_last:
                self.wait(eng, (sem, self.dma_last[sem]))
        ev = self.emit(eng, lambda e, out=out, in_=in_: e.dma_start(out=out, in_=in_), inc=sem, n=16)
        self.dma_last[sem] = ev[1]
        return ev

    def barrier(self):
        evs = [self.ckpt(e, s) for e, s in
               (("tensor", "st"), ("vector", "sv"), ("scalar", "sc"))]
        for e in ("tensor", "vector", "scalar"):
            for ev in evs:
                self.wait(e, ev)
        self.gate_t = evs[0]
        return evs[0]

    def use_ring(self, n):
        self.tring_n = n
        self.t_ring = 0

    def use_bank(self, i, ev=None):
        self.wait("tensor", self.bank_last_read.get(i))
        if ev is not None:
            self.bank_last_read[i] = ev

    def read_bank(self, i, ev):
        self.bank_last_read[i] = ev

    def TT(self, out, a, bb, op, inc=None):
        return self.emit("vector", lambda e, out=out, a=a, bb=bb, op=op:
                         e.tensor_tensor(out=out, in0=a, in1=bb, op=op), inc=inc)

    def TS(self, out, a, s1, s2, op0, op1=None, inc=None):
        if op1 is None:
            return self.emit("vector", lambda e, out=out, a=a, s1=s1, op0=op0:
                             e.tensor_scalar(out=out, in0=a, scalar1=s1, scalar2=None, op0=op0), inc=inc)
        return self.emit("vector", lambda e, out=out, a=a, s1=s1, s2=s2, op0=op0, op1=op1:
                         e.tensor_scalar(out=out, in0=a, scalar1=s1, scalar2=s2, op0=op0, op1=op1), inc=inc)

    def STT(self, out, a, s, bb, op0, op1, inc=None):
        return self.emit("vector", lambda e, out=out, a=a, s=s, bb=bb, op0=op0, op1=op1:
                         e.scalar_tensor_tensor(out=out, in0=a, scalar=s, in1=bb, op0=op0, op1=op1), inc=inc)

    def VCOPY(self, out, a, inc=None):
        return self.emit("vector", lambda e, out=out, a=a: e.tensor_copy(out, a), inc=inc)

    def RED(self, out, a, op, inc=None, absv=None):
        return self.emit("vector", lambda e, out=out, a=a, op=op, absv=absv:
                         e.tensor_reduce(out=out, in_=a, axis=AX, op=op, apply_absolute_value=absv), inc=inc)

    def RECIP(self, out, a, inc=None):
        return self.emit("vector", lambda e, out=out, a=a: e.reciprocal(out, a), inc=inc)

    def MEMSET(self, ap, v, inc=None):
        return self.emit("vector", lambda e, ap=ap, v=v: e.memset(ap, v), inc=inc)

    def SCAN(self, out, d0, d1, inc=None):
        return self.emit("vector", lambda e, out=out, d0=d0, d1=d1:
                         e.tensor_tensor_scan(out=out, data0=d0, data1=d1, initial=0.0,
                                              op0=OP.mult, op1=OP.add), inc=inc)

    def ACT(self, out, a, fn, bias=0.0, scale=1.0, accum=None, inc=None):
        return self.emit("scalar", lambda e, out=out, a=a, fn=fn, bias=bias, scale=scale, accum=accum:
                         e.activation(out=out, in_=a, func=fn, bias=bias, scale=scale, accum_out=accum), inc=inc)

    def SCOPY(self, out, a, inc=None):
        return self.emit("scalar", lambda e, out=out, a=a: e.copy(out, a), inc=inc)

    def MM(self, out, lhsT, rhs, start, stop, tp=None, inc=None):
        return self.emit("tensor", lambda e, out=out, lhsT=lhsT, rhs=rhs, start=start, stop=stop, tp=tp:
                         e.matmul(out, lhsT, rhs, start=start, stop=stop, tile_position=tp), inc=inc)

    def TRANS(self, out, in_, ident, inc=None):
        return self.emit("tensor", lambda e, out=out, in_=in_, ident=ident:
                         e.transpose(out, in_, ident), inc=inc)

    def pt_slot(self):
        i = self.pt_idx
        self.pt_idx += 1
        off = [0, 1024, 256, 1280][i % 4]
        return self.pt[:, off:off + 128], i % 4

    def ln_stats(self, srcs, sqkb, eps, fin, rms=False, base=110.5, extra=None):
        mu = self.AF(base, 512)
        r = self.AF(base + 2.0, 512)
        mur = self.AF(base + 4.0, 512)
        for tt in range(4):
            lo = tt * 512
            sq_evs = []
            for c in range(len(srcs)):
                sq = self.AB(sqkb + c * 1.0, 512)
                sq_evs.append((sq, self.ACT(sq, srcs[c][:, lo:lo + 512], AF.Square, inc="sc")))
            self.use_bank(0)
            self.use_bank(1)
            sum_ev = None
            if not rms:
                for c in range(NCH):
                    sum_ev = self.MM(self.bank(0), self.onesf, srcs[c][:, lo:lo + 512],
                                     c == 0, c == NCH - 1,
                                     inc=("st" if c == NCH - 1 else None))
            for c in range(NCH):
                self.wait("tensor", sq_evs[c][1])
                self.MM(self.bank(1), self.onesb, sq_evs[c][0], c == 0, c == NCH - 1,
                        inc=("st" if c == NCH - 1 else None))
            mm_ev = self.tlast
            self.wait("vector", sum_ev)
            self.wait("vector", mm_ev)
            if not rms:
                self.TS(mu, self.bank(0), 1.0 / D, None, OP.mult)
                self.TT(mur, mu, mu, OP.mult)
                self.TS(r, self.bank(1), 1.0 / D, None, OP.mult)
                self.TT(r, r, mur, OP.subtract, inc="sv")
            else:
                self.TS(r, self.bank(1), 1.0 / D, None, OP.mult, inc="sv")
            var_ev = ("sv", self.ctr["sv"])
            if extra is not None:
                extra(tt, lo, r, var_ev)
            self.wait("scalar", var_ev)
            self.ACT(r, r, AF.Ln, bias=eps)
            self.ACT(r, r, AF.Exp, scale=-0.5, inc="sc")
            r_ev = ("sc", self.ctr["sc"])
            self.wait("vector", r_ev)
            if not rms:
                self.TT(mur, mu, r, OP.mult)
            fin(tt, lo, mu, r, mur)
            free_ev = self.ckpt("vector", "sv")
            self.read_bank(0, free_ev)
            self.read_bank(1, free_ev)
        self.wait("scalar", ("sv", self.ctr["sv"]))

    def proj_fm(self, wsrc3, col0, out_cb, rhs_list, slots, slot_free, nouts,
                resid=False, evac="scalar"):
        act_evs = {}
        for oc in range(nouts):
            si = oc % 4
            sl = slots[si]
            self.wait("gpsimd", slot_free[si])
            ldev = self.dma("gpsimd", sl.rearrange("p (c n) -> p c n", c=NCH),
                            wsrc3[:, :, col0 + oc * 128: col0 + (oc + 1) * 128])
            self.wait("tensor", ldev)
            for tt in range(4):
                bk = tt % 2
                self.use_bank(bk)
                for c in range(NCH):
                    self.MM(self.bank(bk), sl[:, c * 128:(c + 1) * 128],
                            rhs_list[c][:, tt * 512:(tt + 1) * 512], c == 0, c == NCH - 1,
                            inc=("st" if c == NCH - 1 else None))
                mm_ev = self.tlast
                ev = out_cb(oc, tt, self.bank(bk), mm_ev)
                self.read_bank(bk, ev)
            slot_free[si] = self.ckpt("tensor", "st")
        return slot_free

    def ph0_loads(self):
        b = self
        b.ld["x"] = b.dma("sync", b.xs_a.rearrange("p (c t) -> p c t", c=NCH),
                          b.x_in.rearrange("(c p) t -> p c t", p=128))
        b.ld["eye_b"] = b.dma("gpsimd", b.identb, b.eye)
        b.ld["ones_f"] = b.dma("sync", b.onesf, b.onesin)
        b.ld["ones_b"] = b.dma("gpsimd", b.onesb, b.onesin)
        b.ld["subw"] = b.dma("sync", b.subw_s, b.sublnw4)
        b.ld["cols"] = b.dma("sync", b.arf[:, int(126.5 * 256):int(126.5 * 256) + 9 * NCH],
                             b.colpack)
        b.ld["b1l"] = b.dma("sync", b.MISC[:, 0:4], b.b1l)
        b.ld["b1h"] = b.dma("sync", b.MISC[:, 4:36], b.b1h)
        b.ld["constv"] = b.dma("sync", b.cv, b.constv)
        b.ld["ltri"] = b.dma("sync", b.AF(127.65, 16), b.ltri)
        b.ld["identf"] = b.dma("sync", b.identf, b.eye)
        b.ld["tdec"] = b.dma("sync", b.mcol(41)[0:64, :], b.tdec)
        b.ld["tf"] = b.dma("sync", b.mcol(42)[0:64, :], b.tfirst)

    def ph1_lambda_ln1(self):
        b = self
        lamrow = b.AF(110.5, 128)
        b.ld["lam"] = b.dma("sync", lamrow[0:1, :], b.lam)
        for ev in (b.ld["lam"], b.ld["ones_f"]):
            b.wait("vector", ev)
        b.wait("scalar", b.ld["constv"])
        b.wait("vector", b.ld["constv"])
        b.TT(lamrow[0:1, 0:32], lamrow[0:1, 0:32], lamrow[0:1, 32:64], OP.mult)
        b.TT(lamrow[0:1, 64:96], lamrow[0:1, 64:96], lamrow[0:1, 96:128], OP.mult)
        b.RED(lamrow[0:1, 32:33], lamrow[0:1, 0:32], OP.add)
        b.RED(lamrow[0:1, 96:97], lamrow[0:1, 64:96], OP.add, inc="sv")
        b.wait("scalar", ("sv", b.ctr["sv"]))
        b.ACT(lamrow[0:1, 33:34], lamrow[0:1, 32:33], AF.Exp)
        b.ACT(lamrow[0:1, 97:98], lamrow[0:1, 96:97], AF.Exp, inc="sc")
        b.wait("vector", ("sc", b.ctr["sc"]))
        b.TT(lamrow[0:1, 0:1], lamrow[0:1, 33:34], lamrow[0:1, 97:98], OP.subtract)
        b.TS(lamrow[0:1, 0:1], lamrow[0:1, 0:1], 1.0, LAMBDA_INIT, OP.mult, OP.add, inc="sv")
        b.wait("tensor", ("sv", b.ctr["sv"]))
        b.use_bank(0)
        b.MM(b.bank(0, 0, 1), b.onesf[0:1, :], lamrow[0:1, 0:1], True, True, inc="st")
        b.wait("vector", b.tlast)
        b.TS(b.mcol(36), b.bank(0, 0, 1), 1.0, None, OP.mult)
        ev = b.TS(b.mcol(37), b.bank(0, 0, 1), -1.0, None, OP.mult, inc="sv")
        b.read_bank(0, ev)

        b.wait("vector", b.ld["cols"])
        b.wait("scalar", b.ld["cols"])
        b.ACT(b.GC[:, 0:NCH], b.COLS["sgate"], AF.Sigmoid, inc="sc")
        b.wait("vector", ("sc", b.ctr["sc"]))
        b.TS(b.GC[:, NCH:2 * NCH], b.GC[:, 0:NCH], -1.0, 1.0, OP.mult, OP.add)
        b.RECIP(b.GC[:, 2 * NCH:3 * NCH], b.COLS["gl"])
        b.TT(b.GC[:, 2 * NCH:3 * NCH], b.COLS["gh"], b.GC[:, 2 * NCH:3 * NCH], OP.mult, inc="sv")
        b.wait("scalar", b.ld["tdec"])
        b.wait("scalar", b.ld["tf"])
        b.ACT(b.mcol(41)[0:64, :], b.mcol(41)[0:64, :], AF.Exp)
        b.ACT(b.mcol(43)[0:64, :], b.mcol(42)[0:64, :], AF.Exp, inc="sc")
        b.wait("vector", ("sc", b.ctr["sc"]))
        b.TS(b.mcol(43)[0:64, :], b.mcol(43)[0:64, :], -1.0, None, OP.mult, inc="sv")

        zt0 = b.AB(32.0, 1024)
        b.MEMSET(zt0, 0.0, inc="sv")
        z0ev = ("sv", b.ctr["sv"])
        b.wait("sync", z0ev)
        b.zinit_evs = []
        for k in range(41):
            b.zinit_evs.append(b.dma("sync", b.zslots[k * 128:(k + 1) * 128, :], zt0))
        for e in range(8):
            b.zinit_evs.append(b.dma("sync", b.yslots[e * 656 + 640:e * 656 + 656, :],
                                     zt0[0:16, :]))
        b.wait("scalar", b.ld["x"])
        b.wait("tensor", b.ld["x"])
        b.wait("tensor", b.ld["ones_b"])
        xln = b.RA
        tmp = b.AF(96.5, 512)

        def fin(tt, lo, mu, r, mur):
            for c in range(NCH):
                b.TT(tmp, b.xc(c, lo, lo + 512), r, OP.mult)
                b.TT(xln[c][:, lo:lo + 512], tmp, mur, OP.subtract)

        b.ln_stats([b.xc(c) for c in range(NCH)], sqkb=64.5, eps=1e-5, fin=fin)
        b.ckpt("vector", "sv")
        b.xln_ev = ("sv", b.ctr["sv"])

    def ph1b_v(self):
        b = self
        vtm = b.AB(32.0, 16896)
        b.vtm = vtm
        b.wait("tensor", b.ld["eye_b"])
        b.wait("vector", b.xln_ev)
        for zev in b.zinit_evs:
            b.wait("vector", zev)
        b.MEMSET(vtm, 1.0, inc="sv")
        vtm_ones = ("sv", b.ctr["sv"])
        b.wait("tensor", vtm_ones)
        b.wait("tensor", b.xln_ev)
        b.wait("scalar", b.xln_ev)
        b.wq_slots = [b.AB(116.5 + i * 2.0, 1024) for i in range(4)]
        b.slot_free = [None] * 4
        b.wqkv3 = b.wqkvT.rearrange("(c p) n -> p c n", p=128)
        vv = b.AB(104.5, 2048)
        xln = b.RA
        vtm_cp = None
        for oc in range(NCH):
            si = oc % 4
            sl = b.wq_slots[si]
            b.wait("gpsimd", b.slot_free[si])
            ldev = b.dma("gpsimd", sl.rearrange("p (c n) -> p c n", c=NCH),
                         b.wqkv3[:, :, 2 * D + oc * 128: 2 * D + (oc + 1) * 128])
            b.wait("tensor", ldev)
            tr_last = None
            for tt in range(4):
                bk = tt % 2
                self.use_bank(bk)
                for c in range(NCH):
                    b.MM(b.bank(bk), sl[:, c * 128:(c + 1) * 128],
                         xln[c][:, tt * 512:(tt + 1) * 512], c == 0, c == NCH - 1,
                         inc=("st" if c == NCH - 1 else None))
                mm_ev = b.tlast
                b.wait("scalar", mm_ev)
                ev = b.ACT(vv[:, tt * 512:(tt + 1) * 512], b.bank(bk), AF.Copy, inc="sc")
                b.read_bank(bk, ev)
            vv_ev = ("sc", b.ctr["sc"])
            b.wait("tensor", vv_ev)
            b.slot_free[si] = b.ckpt("tensor", "st")
            for kc in range(16):
                pslot, pi = b.pt_slot()
                b.wait("tensor", b.pt_last_read.get(pi))
                b.TRANS(pslot, vv[:, kc * 128:(kc + 1) * 128], b.identb, inc="st")
                tr_ev = b.tlast
                tr_last = tr_ev
                b.wait("vector", tr_ev)
                base = kc * 1056 + 2 * oc * 66
                dst = vtm[:, base:base + 132].rearrange("p (h z) -> p h z", z=66)[:, :, 0:64]
                src = pslot.rearrange("p (h z) -> p h z", z=64)
                cpev = b.VCOPY(dst, src, inc="sv")
                b.pt_last_read[pi] = cpev
                vtm_cp = cpev
            b.wait("scalar", b.ckpt("tensor"))
        b.vtm_ev = vtm_cp

    def ph2_attention(self):
        b = self
        xln = b.RA
        atm = b.AB(65.0, 16384)
        b.atm = atm
        atm3 = atm.rearrange("p (s z) -> p s z", z=1024)
        qg = b.AB(97.0, 2048)
        kg = b.AB(101.0, 2048)
        pex = [b.AB(105.0, 1024), b.AB(107.0, 1024), b.AB(109.0, 1024)]
        ez = b.AB(105.0, 1024)
        dvE = b.AF(111.0, 256)
        dvO = b.AF(112.0, 256)
        cmb = b.AF(113.0, 256)
        tmp4 = b.AF(114.0, 256)
        sqc = b.AB(115.0, 256)
        ssm = b.AF(115.55, 4)
        rst = b.AF(115.65, 4)
        rzE = b.AF(115.75, 4)
        rzO = b.AF(115.85, 4)

        b.wait("tensor", b.vtm_ev)
        b.wait("vector", b.ld["subw"])
        b.use_ring(4)
        prev = None
        for g in range(NCH):
            for which, dst in ((0, qg), (1, kg)):
                si = (2 * g + which) % 4
                sl = b.wq_slots[si]
                b.wait("gpsimd", b.slot_free[si])
                ldev = b.dma("gpsimd", sl.rearrange("p (c n) -> p c n", c=NCH),
                             b.wqkv3[:, :, which * D + g * 128: which * D + (g + 1) * 128])
                b.wait("tensor", ldev)
                for tt in range(4):
                    bk = tt % 2
                    b.use_bank(bk)
                    for c in range(NCH):
                        b.MM(b.bank(bk), sl[:, c * 128:(c + 1) * 128],
                             xln[c][:, tt * 512:(tt + 1) * 512], c == 0, c == NCH - 1,
                             inc=("st" if c == NCH - 1 else None))
                    mm_ev = b.tlast
                    b.wait("scalar", mm_ev)
                    ev = b.ACT(dst[:, tt * 512:(tt + 1) * 512], b.bank(bk), AF.Copy, inc="sc")
                    b.read_bank(bk, ev)
                b.slot_free[si] = b.ckpt("tensor", "st")
            qk_ev = ("sc", b.ctr["sc"])
            b.wait("tensor", qk_ev)

            for jl in range(2):
                h = 2 * g + jl
                for qt in range(4):
                    cur = {"pv": {}, "exp": {}}

                    def emit_pv(kc):
                        b.wait("tensor", cur["exp"][kc])
                        if kc == 0 and prev is not None:
                            b.wait("tensor", prev["evac"])
                        pvs = []
                        for half in range(2):
                            pvs.append(b.MM(
                                b.bank(4 + half)[0:65, :],
                                b.vtm[:, kc * 1056 + h * 66: kc * 1056 + h * 66 + 65],
                                pex[kc % 3][:, half * 512:(half + 1) * 512],
                                kc == 0, kc == 15, inc="st"))
                        cur["pv"][kc] = pvs

                    for kc in range(16):
                        st_ = kc % 2
                        bk = 2 * st_
                        if kc >= 2:
                            b.wait("tensor", cur["exp"][kc - 2])
                        elif prev is not None:
                            b.wait("tensor", prev["exp"][14 + st_])
                        pack_evs = []
                        for i in range(2):
                            strip = 64 * jl + 32 * i
                            pack_evs.append(b.MM(
                                b.bank(bk + i, 0, 512),
                                kg[strip:strip + 32, kc * 128:(kc + 1) * 128],
                                qg[strip:strip + 32, qt * 512:(qt + 1) * 512],
                                True, True, tp=(strip, 0), inc="st"))
                        for pe_ in pack_evs:
                            b.wait("scalar", pe_)
                        if kc >= 3:
                            for pe_ in cur["pv"][kc - 3]:
                                b.wait("scalar", pe_)
                        elif prev is not None:
                            for pe_ in prev["pv"][{0: 15, 1: 13, 2: 14}[kc]]:
                                b.wait("scalar", pe_)
                        if kc == 0 and prev is not None:
                            b.wait("scalar", prev["trev"])
                        b.ACT(pex[kc % 3], b.ps[:, bk * 512:(bk + 2) * 512], AF.Exp,
                              scale=SCALING, inc="sc")
                        cur["exp"][kc] = ("sc", b.ctr["sc"])
                        if kc >= 2:
                            emit_pv(kc - 2)
                    emit_pv(14)
                    emit_pv(15)
                    for pe_ in cur["pv"][15]:
                        b.wait("scalar", pe_)
                    b.SCOPY(ez[0:65, 0:512], b.bank(4)[0:65, :])
                    b.SCOPY(ez[0:65, 512:1024], b.bank(5)[0:65, :], inc="sc")
                    cur["evac"] = ("sc", b.ctr["sc"])
                    b.wait("tensor", cur["evac"])
                    if prev is not None:
                        b.wait("tensor", prev["div"])
                    for s in range(8):
                        half, s4 = s // 4, s % 4
                        dstp = b.pt[:, half * 1024 + s4 * 66: half * 1024 + s4 * 66 + 65]
                        b.TRANS(dstp, ez[0:65, s * 128:(s + 1) * 128],
                                b.identb[0:65, 0:65],
                                inc="st")
                    tr_ev = b.ckpt("tensor")
                    cur["trev"] = tr_ev
                    b.wait("vector", tr_ev)
                    ptE = b.pt[:, 0:264].rearrange("p (s z) -> p s z", z=66)
                    ptO = b.pt[:, 1024:1288].rearrange("p (s z) -> p s z", z=66)
                    b.RECIP(rzE, ptE[:, :, 64:65])
                    b.RECIP(rzO, ptO[:, :, 64:65])
                    b.TT(dvE.rearrange("p (s z) -> p s z", z=64), ptE[:, :, 0:64],
                         rzE.rearrange("p (s u) -> p s u", u=1).broadcast_to([128, 4, 64]),
                         OP.mult)
                    b.TT(dvO.rearrange("p (s z) -> p s z", z=64), ptO[:, :, 0:64],
                         rzO.rearrange("p (s u) -> p s u", u=1).broadcast_to([128, 4, 64]),
                         OP.mult, inc="sv")
                    cur["div"] = ("sv", b.ctr["sv"])
                    b.STT(cmb, dvO, b.mcol(37), dvE, OP.mult, OP.add, inc="sv")
                    b.wait("scalar", ("sv", b.ctr["sv"]))
                    b.ACT(sqc, cmb, AF.Square, inc="sc")
                    b.wait("vector", ("sc", b.ctr["sc"]))
                    b.RED(ssm.rearrange("p (s u) -> p s u", u=1),
                          sqc.rearrange("p (s z) -> p s z", z=64), OP.add, inc="sv")
                    b.wait("scalar", ("sv", b.ctr["sv"]))
                    b.ACT(rst, ssm, AF.Ln, scale=1.0 / 64.0, bias=1e-5)
                    b.ACT(rst, rst, AF.Exp, scale=-0.5,
                          bias=math.log(1.0 - LAMBDA_INIT), inc="sc")
                    b.wait("vector", ("sc", b.ctr["sc"]))
                    b.TT(tmp4.rearrange("p (s z) -> p s z", z=64),
                         cmb.rearrange("p (s z) -> p s z", z=64),
                         rst.rearrange("p (s u) -> p s u", u=1).broadcast_to([128, 4, 64]),
                         OP.mult)
                    b.TT(atm3[:, qt * 4:(qt + 1) * 4, h * 64:h * 64 + 64],
                         tmp4.rearrange("p (s z) -> p s z", z=64),
                         b.subw_s.rearrange("p (s z) -> p s z", z=64), OP.mult, inc="sv")
                    prev = cur
        b.wait("tensor", prev["div"])
        b.ckpt("tensor")
        b.use_ring(1)
        b.ckpt("vector", "sv")
        b.atm_ev = ("sv", b.ctr["sv"])
        b.read_bank(4, prev["evac"] if False else None)
        b.barrier()

    def ph3_attnout(self):
        b = self
        atm = b.atm
        afm = [b.AB(32.0 + c * 4.0, T) for c in range(NCH)]
        for fc in range(NCH):
            for tc in range(16):
                pslot, pi = b.pt_slot()
                b.wait("tensor", b.pt_last_read.get(pi))
                b.TRANS(pslot, atm[:, tc * 1024 + fc * 128: tc * 1024 + (fc + 1) * 128],
                        b.identb, inc="st")
                b.wait("vector", b.tlast)
                b.pt_last_read[pi] = b.VCOPY(afm[fc][:, tc * 128:(tc + 1) * 128], pslot, inc="sv")
        b.ckpt("vector", "sv")
        afm_ev = ("sv", b.ctr["sv"])
        b.wait("tensor", afm_ev)
        wo3 = b.woT.rearrange("(c p) n -> p c n", p=128)

        def resid(oc, tt, bk_ap, mm_ev):
            b.wait("vector", mm_ev)
            return b.TT(b.xc(oc, tt * 512, (tt + 1) * 512), bk_ap,
                        b.xc(oc, tt * 512, (tt + 1) * 512), OP.add, inc="sv")

        b.proj_fm(wo3, 0, resid, afm, b.wq_slots, b.slot_free, NCH)
        b.barrier()

    def ph4_moe(self):
        b = self
        zn = b.RA
        slots = [b.AB(32.0, 8192), b.AB(48.0, 8192), b.AB(64.0, 8192)]
        G = b.AB(80.0, 16384)
        cbc = b.AB(112.0, 2048)
        tmp = b.AF(96.5, 512)
        cmbf = b.AF(123.0, 128)
        cmbb = b.AB(123.66, 128)
        crows = b.AB(123.9, 128)
        m4 = b.AF(116.5, 64)
        e4 = b.AF(116.75, 64)
        mx16 = b.AF(117.0, 16)
        ms16 = b.AF(117.1, 16)
        m1t = b.AF(117.2, 16)
        m2t = b.AF(117.3, 16)
        dnv = b.AF(117.4, 16)
        mi8 = b.AF(117.5, 128)
        pm4 = b.AF(118.0, 64)
        iE = b.AF(118.25, 64)
        iO = b.AF(118.5, 64)
        gw4 = b.AF(118.75, 64)
        wmm8 = b.AB(122.5, 128)

        ph4_gate = b.gate_t
        b.wait("sync", ("sv", b.ctr["sv"]))
        spill_ev = b.dma("sync", b.xspill.rearrange("(c p) t -> p c t", p=128),
                         b.xs_a.rearrange("p (c t) -> p c t", c=NCH))
        b.wait("vector", spill_ev)
        b.wait("gpsimd", ph4_gate)
        ld_wmm = b.dma("gpsimd", wmm8.rearrange("p (c n) -> p c n", c=NCH),
                       b.wmmT.rearrange("(c p) n -> p c n", p=128))

        def fin_ln2(tt, lo, mu, r, mur):
            for c in range(NCH):
                b.TT(tmp, b.xc(c, lo, lo + 512), r, OP.mult)
                b.TT(b.xc(c, lo, lo + 512), tmp, mur, OP.subtract)

        b.ln_stats([b.xc(c) for c in range(NCH)], sqkb=80.0, eps=1e-5,
                   fin=fin_ln2, base=116.5)
        b.barrier()

        w1r = b.w1T.rearrange("e (c p) f -> e p c f", p=128)
        w2r = b.w2T.rearrange("e (j p) d -> e p j d", p=128)
        loads = []
        for l in range(2):
            for e in range(8):
                loads.append((1, e, 0))
                loads.append((1, e, 1))
                loads.append((2, e, 0))
                loads.append((2, e, 1))
        load_evs = [None] * len(loads)
        seg_done = [None] * len(loads)
        wr_evs = [None] * 32
        emitted = [0]

        def pump(upto):
            while emitted[0] < min(upto, len(loads)):
                k = emitted[0]
                kind, e, hf = loads[k]
                l1 = k >= 32
                eng = ("sync" if k % 2 == 0 else "gpsimd") if l1 else "gpsimd"
                if k >= 3:
                    b.wait(eng, seg_done[k - 3])
                    if k - 3 < 32:
                        b.wait(eng, wr_evs[k - 3])
                else:
                    b.wait(eng, ph4_gate)
                si = k % 3
                scr = b.w1scr if kind == 1 else b.w2scr
                seg = (2 * e + hf) * 128
                if l1:
                    b.wait(eng, wr_evs[k - 32])
                    load_evs[k] = b.dma(eng, slots[si], scr[seg:seg + 128, :])
                elif kind == 1:
                    load_evs[k] = b.dma(eng, slots[si].rearrange("p (c f) -> p c f", c=NCH),
                                        w1r[e][:, :, hf * 1024:(hf + 1) * 1024])
                else:
                    load_evs[k] = b.dma(eng, slots[si].rearrange("p (j d) -> p j d", j=8),
                                        w2r[e][:, hf * 8:(hf + 1) * 8, :])
                if not l1:
                    b.wait("sync", load_evs[k])
                    wr_evs[k] = b.dma("sync", scr[seg:seg + 128, :], slots[si])
                emitted[0] += 1

        pump(3)
        kseg = 0

        for l in range(2):
            def fin_rms(tt, lo, mu, r, mur):
                for c in range(NCH):
                    b.STT(zn[c][:, lo:lo + 512], b.xc(c, lo, lo + 512),
                          b.colv("normw", c), r, OP.mult, OP.mult)

            b.ln_stats([b.xc(c) for c in range(NCH)], sqkb=104.5,
                       eps=1e-6, fin=fin_rms, rms=True, base=116.5)
            b.ckpt("vector", "sv")
            zn_ev = ("sv", b.ctr["sv"])
            b.wait("tensor", zn_ev)
            b.wait("scalar", zn_ev)

            b.wait("tensor", ld_wmm)
            b.use_bank(0)
            for tc in range(16):
                for c in range(NCH):
                    b.MM(b.bank(0)[:, tc * 12:tc * 12 + 12],
                         zn[c][:, tc * 128:(tc + 1) * 128],
                         wmm8[:, c * 16:c * 16 + 12], c == 0, c == NCH - 1,
                         inc=("st" if (c == NCH - 1 and tc == 15) else None))
            lg_ev = b.ckpt("tensor")
            b.wait("vector", lg_ev)
            L = b.bank(0)[:, 0:192].rearrange("p (tc n) -> p tc n", n=12)
            mac = L[:, :, 0:4]
            mic = L[:, :, 4:12]
            b.RED(mx16.rearrange("p (s u) -> p s u", u=1), mac, OP.max)
            b.TT(m4.rearrange("p (s z) -> p s z", z=4), mac,
                 mx16.rearrange("p (s u) -> p s u", u=1).broadcast_to([128, 16, 4]),
                 OP.subtract)
            b.VCOPY(mi8, mic, inc="sv")
            b.wait("scalar", ("sv", b.ctr["sv"]))
            b.ACT(m4, m4, AF.Exp, inc="sc")
            b.wait("vector", ("sc", b.ctr["sc"]))
            m43 = m4.rearrange("p (s z) -> p s z", z=4)
            b.RED(ms16.rearrange("p (s u) -> p s u", u=1), m43, OP.add)
            b.RED(m1t.rearrange("p (s u) -> p s u", u=1), m43, OP.max)
            b.TT(e4.rearrange("p (s z) -> p s z", z=4), m43,
                 m1t.rearrange("p (s u) -> p s u", u=1).broadcast_to([128, 16, 4]),
                 OP.is_ge)
            b.STT(e4, e4, -1e30, m4, OP.mult, OP.add)
            b.RED(m2t.rearrange("p (s u) -> p s u", u=1),
                  e4.rearrange("p (s z) -> p s z", z=4), OP.max)
            b.TT(dnv, m1t, m2t, OP.add)
            b.STT(dnv, ms16, 1e-9, dnv, OP.mult, OP.add)
            b.RECIP(dnv, dnv)
            b.TT(iE.rearrange("p (s z) -> p s z", z=4), m43,
                 m2t.rearrange("p (s u) -> p s u", u=1).broadcast_to([128, 16, 4]),
                 OP.is_ge)
            b.TT(gw4, m4, iE, OP.mult)
            b.TT(gw4.rearrange("p (s z) -> p s z", z=4),
                 gw4.rearrange("p (s z) -> p s z", z=4),
                 dnv.rearrange("p (s u) -> p s u", u=1).broadcast_to([128, 16, 4]),
                 OP.mult)
            mi4 = mi8.rearrange("p (s g j) -> p s g j", g=4, j=2)
            b.TT(pm4.rearrange("p (s g) -> p s g", g=4), mi4[:, :, :, 0],
                 mi4[:, :, :, 1], OP.max)
            pmB = pm4.rearrange("p (s g) -> p s g", g=4)
            b.TT(iE.rearrange("p (s g) -> p s g", g=4), mi4[:, :, :, 0], pmB, OP.is_ge)
            b.TT(iO.rearrange("p (s g) -> p s g", g=4), mi4[:, :, :, 1], pmB, OP.is_ge)
            cm4 = cmbf.rearrange("p (s g j) -> p s g j", g=4, j=2)
            b.TT(cm4[:, :, :, 0], iE.rearrange("p (s g) -> p s g", g=4),
                 gw4.rearrange("p (s g) -> p s g", g=4), OP.mult)
            ev = b.TT(cm4[:, :, :, 1], iO.rearrange("p (s g) -> p s g", g=4),
                      gw4.rearrange("p (s g) -> p s g", g=4), OP.mult, inc="sv")
            b.read_bank(0, ev)
            b.VCOPY(cmbb, cmbf, inc="sv")
            cmb_ev = ("sv", b.ctr["sv"])
            b.wait("tensor", cmb_ev)
            if True:
                pslot, pi = b.pt_slot()
                b.wait("tensor", b.pt_last_read.get(pi))
                b.wait("tensor", b.pt_last_read.get((pi + 2) % 4))
                b.TRANS(pslot, cmbb, b.identb, inc="st")
                b.wait("vector", b.tlast)
                crows_ev = b.VCOPY(crows, pslot, inc="sv")
                b.pt_last_read[pi] = crows_ev
                b.wait("sync", crows_ev)
                crd_ev = b.dma("sync", b.crow_d.rearrange("tc e j -> (tc e) j"), crows)

            c8 = b.AF(80.0, 2048)[0:8, :]
            mask8 = b.AF(88.0, 2048)[0:8, :]
            pos8 = b.AF(96.0, 2048)[0:8, :]
            ones8 = b.AF(104.0, 2048)[0:8, :]
            first8 = ones8
            s12r = b.AF(112.0, 2048)[0:1, :]
            s1r = b.AF(80.0, 2048)[0:1, :]
            s2r = b.AF(88.0, 2048)[0:1, :]
            ltri_s = b.AF(127.65, 16)
            s1f = b.AF(124.0, 16)
            s2f = b.AF(124.0625, 16)
            c1f = b.AF(124.125, 16)
            c2f = b.AF(124.1875, 16)
            s1u = s1f.bitcast(mybir.dt.uint32)
            s2u = s2f.bitcast(mybir.dt.uint32)
            tmA = b.AF(118.5, 128)
            tmB = b.AF(119.0, 128)
            tmC = b.AF(119.5, 128)

            if True:
                b.wait("gpsimd", crd_ev)
                c8ld = b.dma("gpsimd", c8.rearrange("e (tc j) -> e tc j", j=128),
                             b.crow_d.rearrange("tc e j -> e tc j"))
                b.wait("vector", c8ld)
                b.wait("vector", b.ld["ltri"])
                b.MEMSET(ones8, 1.0)
                b.TS(mask8, c8, 0.0, None, OP.is_gt)
                b.SCAN(pos8, ones8, mask8, inc="sv")
                b.wait("tensor", ("sv", b.ctr["sv"]))
                b.wait("tensor", b.ld["identf"])
                for tt in range(4):
                    b.use_bank(1)
                    sev = b.MM(b.bank(1)[0:8, :], ltri_s[0:8, 0:8],
                               mask8[:, tt * 512:(tt + 1) * 512], True, True, inc="st")
                    b.wait("vector", sev)
                    ev = b.TS(first8[:, tt * 512:(tt + 1) * 512], b.bank(1)[0:8, :],
                              0.5, None, OP.is_lt, inc="sv")
                    b.read_bank(1, ev)
                b.TT(first8, first8, mask8, OP.mult)
                b.TS(c8, pos8, 1.0, ltri_s[0:8, 8:9], OP.subtract, OP.add)
                b.TT(c8, c8, mask8, OP.mult)
                b.TT(mask8, c8, first8, OP.mult, inc="sv")
                b.wait("tensor", ("sv", b.ctr["sv"]))
                for tt in range(4):
                    b.use_bank(2)
                    e1 = b.MM(b.bank(2)[0:1, :], b.onesf[0:8, 0:1],
                              c8[:, tt * 512:(tt + 1) * 512], True, True, inc="st")
                    b.use_bank(3)
                    e2 = b.MM(b.bank(3)[0:1, :], b.onesf[0:8, 0:1],
                              mask8[:, tt * 512:(tt + 1) * 512], True, True, inc="st")
                    b.wait("vector", e1)
                    b.wait("vector", e2)
                    ev1 = b.TS(s12r[:, tt * 512:(tt + 1) * 512], b.bank(2)[0:1, :],
                               1.0, None, OP.mult, inc="sv")
                    ev2 = b.TS(s1r[:, tt * 512:(tt + 1) * 512], b.bank(3)[0:1, :],
                               1.0, None, OP.mult, inc="sv")
                    b.read_bank(2, ev1)
                    b.read_bank(3, ev2)
                b.TT(s2r, s12r, s1r, OP.subtract, inc="sv")
                b.wait("tensor", ("sv", b.ctr["sv"]))
                for ri, row in ((0, s1r), (1, s2r)):
                    dstf = s1f if ri == 0 else s2f
                    bk = 4 + ri
                    b.use_bank(bk)
                    for tc in range(16):
                        b.MM(b.bank(bk)[:, tc:tc + 1], row[:, tc * 128:(tc + 1) * 128],
                             b.onesf[0:1, 0:1], True, True, inc="st")
                    tre = b.tlast
                    b.wait("vector", tre)
                    ev = b.VCOPY(dstf, b.bank(bk)[:, 0:16], inc="sv")
                    b.read_bank(bk, ev)
                b.VCOPY(s1u, s1f)
                b.VCOPY(s2u, s2f)
                cm3 = cmbf.rearrange("p (s e) -> p s e", e=8)
                tA = tmA.rearrange("p (s e) -> p s e", e=8)
                tB = tmB.rearrange("p (s e) -> p s e", e=8)
                tC = tmC.rearrange("p (s e) -> p s e", e=8)
                b.TS(tmA.rearrange("p (s e) -> p s e", e=8), cm3, 0.0, None, OP.is_gt)
                b.VCOPY(tmB, tmA)
                for k in (1, 2, 4):
                    b.VCOPY(tmC, tmB)
                    b.TT(tB[:, :, k:8], tC[:, :, k:8], tC[:, :, 0:8 - k], OP.add)
                b.TT(tmB, tmB, tmA, OP.subtract)
                b.TS(tmB, tmB, 0.5, None, OP.is_lt)
                b.TT(tmA, tmA, tmB, OP.mult)
                b.TT(tB, cm3, tA, OP.mult)
                b.RED(c1f.rearrange("p (s u) -> p s u", u=1), tB, OP.add)
                b.RED(c2f.rearrange("p (s u) -> p s u", u=1), cm3, OP.add)
                b.TT(c2f, c2f, c1f, OP.subtract, inc="sv")
                idx_ev = ("sv", b.ctr["sv"])

            ztlA = b.AB(116.0, 1024)
            ztlB = b.AB(114.0, 1024)
            scat_evs = []
            ztl_free = [(), ()]

            def igather(out_ap, in_ap, idx_ap, gate_evs=()):
                i = b.dma_ring["gpsimd"]
                b.dma_ring["gpsimd"] = (i + 1) % 8
                sm = "sg%d" % i
                if sm in b.dma_last:
                    b.wait("gpsimd", (sm, b.dma_last[sm]))
                for gev in gate_evs:
                    b.wait("gpsimd", gev)
                def mk(e, out=out_ap, in_=in_ap, idx=idx_ap):
                    if idx[1] == "out":
                        return e.indirect_dma_start(
                            out=out, out_offset=bass.IndirectOffsetOnAxis(idx[0], 0),
                            in_=in_, in_offset=None)
                    return e.indirect_dma_start(
                        out=out, out_offset=None,
                        in_=in_, in_offset=bass.IndirectOffsetOnAxis(idx[0], 0))
                ev = b.emit("gpsimd", mk, inc=sm, n=16)
                b.dma_last[sm] = ev[1]
                return ev

            if True:
                for tc in range(16):
                    ztl = (ztlA, ztlB)[tc % 2]
                    evac_evs = []
                    for c in range(NCH):
                        pslot, pi = b.pt_slot()
                        b.wait("tensor", b.pt_last_read.get(pi))
                        b.wait("tensor", b.pt_last_read.get((pi + 2) % 4))
                        if c == 0:
                            for fe in ztl_free[tc % 2]:
                                b.wait("tensor", fe)
                        b.TRANS(pslot, zn[c][:, tc * 128:(tc + 1) * 128], b.identb, inc="st")
                        tre = b.tlast
                        eng = "vector" if c % 2 == 0 else "scalar"
                        CP = b.VCOPY if eng == "vector" else b.SCOPY
                        sem = "sv" if eng == "vector" else "sc"
                        b.wait(eng, tre)
                        for fe in ztl_free[tc % 2]:
                            b.wait(eng, fe)
                        ev = CP(ztl[:, c * 128:(c + 1) * 128], pslot, inc=sem)
                        b.pt_last_read[pi] = ev
                        evac_evs.append(ev)
                    ev1 = igather(b.zslots, ztl, (s1u[:, tc:tc + 1], "out"),
                                  gate_evs=list(evac_evs) + [idx_ev])
                    ev2 = igather(b.zslots, ztl, (s2u[:, tc:tc + 1], "out"),
                                  gate_evs=[idx_ev])
                    scat_evs += [ev1, ev2]
                    ztl_free[tc % 2] = (ev1, ev2)

            zge = b.AB(104.0, 5120)
            G = b.AB(80.0, 10240)
            ysb = [b.AB(118.0, 1024), b.AB(120.0, 1024)]
            ywr_evs = []
            ysb_busy = [None, None]
            zld_free = [(), ()]
            last_y_ckpt = None
            if True:
                for e in range(8):
                    for k in range(5):
                        ztl = (ztlA, ztlB)[k % 2]
                        if e == 0 and k < 2:
                            for sev_ in scat_evs:
                                b.wait("gpsimd", sev_)
                        for fe in zld_free[k % 2]:
                            b.wait("gpsimd", fe)
                        if e == 0:
                            for fe in ztl_free[k % 2]:
                                b.wait("gpsimd", fe)
                        zl = b.dma("gpsimd", ztl,
                                   b.zslots[e * 656 + k * 128: e * 656 + (k + 1) * 128, :])
                        b.wait("tensor", zl)
                        kevs = []
                        for c in range(NCH):
                            pslot, pi = b.pt_slot()
                            b.wait("tensor", b.pt_last_read.get(pi))
                            b.wait("tensor", b.pt_last_read.get((pi + 2) % 4))
                            b.TRANS(pslot, ztl[:, c * 128:(c + 1) * 128], b.identb, inc="st")
                            tre = b.tlast
                            eng = "vector" if c % 2 == 0 else "scalar"
                            CP = b.VCOPY if eng == "vector" else b.SCOPY
                            sem = "sv" if eng == "vector" else "sc"
                            b.wait(eng, tre)
                            dst = zge[:, c * 640 + k * 128: c * 640 + (k + 1) * 128]
                            ev = CP(dst, pslot, inc=sem)
                            b.pt_last_read[pi] = ev
                            kevs.append(ev)
                        zld_free[k % 2] = tuple(kevs)
                    zge_evs = zld_free[0] + zld_free[1]

                    for hf in range(2):
                        kH = kseg; kseg += 1
                        pump(kH + 3)
                        sl = slots[kH % 3]
                        b.wait("tensor", load_evs[kH])
                        for gev in zge_evs:
                            b.wait("tensor", gev)
                        if last_y_ckpt is not None:
                            b.wait("scalar", last_y_ckpt)
                        for fl in range(8):
                            f = hf * 8 + fl
                            for nt in range(2):
                                nlo, nsz = (0, 512) if nt == 0 else (512, 128)
                                bk = (fl * 2 + nt) % 2
                                b.use_bank(bk)
                                for c in range(NCH):
                                    b.MM(b.bank(bk, 0, nsz),
                                         sl[:, c * 1024 + fl * 128: c * 1024 + (fl + 1) * 128],
                                         zge[:, c * 640 + nlo: c * 640 + nlo + nsz],
                                         c == 0, c == NCH - 1,
                                         inc=("st" if c == NCH - 1 else None))
                                mm_ev = b.tlast
                                b.wait("scalar", mm_ev)
                                gsl = G[:, f * 640 + nlo: f * 640 + nlo + nsz]
                                silu_ev = b.ACT(gsl, b.bank(bk, 0, nsz), AF.Silu, inc="sc")
                                b.read_bank(bk, silu_ev)
                        seg_done[kH] = b.ckpt("tensor", "st")
                    kY0 = kseg; kseg += 1
                    kY1 = kseg; kseg += 1
                    pump(kY0 + 3)
                    slA = slots[kY0 % 3]
                    slB = slots[kY1 % 3]
                    b.wait("tensor", load_evs[kY0])
                    b.wait("tensor", load_evs[kY1])
                    b.wait("tensor", ("sc", b.ctr["sc"]))
                    for st6 in range(5):
                        ypar = st6 % 2
                        yevs = []
                        for dn in range(2):
                            bk = 2 + (st6 * 2 + dn) % 4
                            b.use_bank(bk)
                            for f in range(16):
                                w2sl = slA if f < 8 else slB
                                fl = f % 8
                                b.MM(b.bank(bk),
                                     G[:, f * 640 + st6 * 128: f * 640 + (st6 + 1) * 128],
                                     w2sl[:, fl * 1024 + dn * 512: fl * 1024 + (dn + 1) * 512],
                                     f == 0, f == 15,
                                     inc=("st" if f == 15 else None))
                            mm_ev = b.tlast
                            eng = "vector" if dn == 0 else "scalar"
                            CP = b.VCOPY if eng == "vector" else b.SCOPY
                            sem = "sv" if eng == "vector" else "sc"
                            b.wait(eng, mm_ev)
                            if ysb_busy[ypar] is not None:
                                b.wait(eng, ysb_busy[ypar])
                            ev = CP(ysb[ypar][:, dn * 512:(dn + 1) * 512], b.bank(bk),
                                    inc=sem)
                            b.read_bank(bk, ev)
                            yevs.append(ev)
                        for yev_ in yevs:
                            b.wait("gpsimd", yev_)
                        wev = b.dma(
                            "gpsimd",
                            b.yslots[e * 656 + st6 * 128: e * 656 + (st6 + 1) * 128, :],
                            ysb[ypar])
                        ywr_evs.append(wev)
                        ysb_busy[ypar] = wev
                    last_y_ckpt = b.ckpt("tensor", "st")
                    seg_done[kY0] = seg_done[kY1] = last_y_ckpt
                    pump(kY1 + 3)

            gbuf = [(b.AB(80.0, 1024), b.AB(84.0, 1024)),
                    (b.AB(104.0, 1024), b.AB(108.0, 1024))]
            zd = [b.AF(88.0, 1024), b.AF(96.0, 1024)]
            zdb = [b.AB(92.0, 1024), b.AB(112.0, 1024)]
            zadd_done = [None, None]
            if True:
                def issue_gather(tc):
                    par = tc % 2
                    gbufA, gbufB = gbuf[par]
                    gevs = []
                    for sidx, gb in ((s1u, gbufA), (s2u, gbufB)):
                        gate = list(ywr_evs) if tc == 0 else []
                        if zadd_done[par] is not None:
                            gate.append(zadd_done[par])
                        gevs.append(igather(gb, b.yslots, (sidx[:, tc:tc + 1], "in"),
                                            gate_evs=gate))
                    return gevs

                pend = {0: issue_gather(0)}
                ywr_evs = []
                pend[1] = issue_gather(1)
                for tc in range(16):
                    par = tc % 2
                    gbufA, gbufB = gbuf[par]
                    for gev in pend[tc]:
                        b.wait("vector", gev)
                    b.TS(zd[par], gbufA, c1f[:, tc:tc + 1], None, OP.mult)
                    b.STT(zd[par], gbufB, c2f[:, tc:tc + 1], zd[par], OP.mult, OP.add)
                    b.VCOPY(zdb[par], zd[par], inc="sv")
                    zdb_ev = ("sv", b.ctr["sv"])
                    b.wait("tensor", zdb_ev)
                    for c in range(NCH):
                        pslot, pi = b.pt_slot()
                        b.wait("tensor", b.pt_last_read.get(pi))
                        b.wait("tensor", b.pt_last_read.get((pi + 2) % 4))
                        b.TRANS(pslot, zdb[par][:, c * 128:(c + 1) * 128], b.identb,
                                inc="st")
                        tre = b.tlast
                        b.wait("vector", tre)
                        ev = b.TT(b.xc(c, tc * 128, (tc + 1) * 128), pslot,
                                  b.xc(c, tc * 128, (tc + 1) * 128), OP.add, inc="sv")
                        b.pt_last_read[pi] = ev
                    zadd_done[par] = ("sv", b.ctr["sv"])
                    if tc + 2 < 16:
                        pend[tc + 2] = issue_gather(tc + 2)
            b.barrier()
            b.barrier()

        stage = b.arf[:, int(32.0 * 256):int(32.0 * 256) + 8192]
        b.wait("sync", seg_done[-1])
        for wave in range(2):
            rev = b.dma("sync", stage.rearrange("p (c t) -> p c t", c=4),
                        b.xspill.rearrange("(c p) t -> p c t", p=128)[:, wave * 4:(wave + 1) * 4, :])
            b.wait("vector", rev)
            for c4 in range(4):
                c = wave * 4 + c4
                b.TT(b.xc(c), stage[:, c4 * 2048:(c4 + 1) * 2048], b.xc(c), OP.add, inc="sv")
            if wave == 0:
                b.wait("sync", ("sv", b.ctr["sv"]))
        b.barrier()


    def ph5_rwkv(self):
        b = self
        xln3 = b.RA
        xsh = [b.AF(32.0 + c * 8.0, 512 * 4) for c in range(NCH)]
        tmp = b.AF(96.5, 512)
        tmp2 = b.AF(112.5, 512)
        wsh = b.AB(98.5, 8192)
        tmp2 = b.AF(114.5, 512)
        b.wait("gpsimd", b.gate_t)
        ld_wsh = b.dma("gpsimd", wsh.rearrange("p (c n) -> p c n", c=NCH),
                       b.wshiftT.rearrange("(c p) n -> p c n", p=128))

        def fin_ln3(tt, lo, mu, r, mur):
            for c in range(NCH):
                b.TT(tmp, b.xc(c, lo, lo + 512), r, OP.mult)
                b.TT(xln3[c][:, lo:lo + 512], tmp, mur, OP.subtract)

        b.ln_stats([b.xc(c) for c in range(NCH)], sqkb=64.5, eps=1e-5,
                   fin=fin_ln3, base=116.5)
        b.ckpt("vector", "sv")
        xln3_ev = ("sv", b.ctr["sv"])
        b.wait("tensor", xln3_ev)
        b.wait("tensor", ld_wsh)

        for oc in range(NCH):
            for tt in range(4):
                bk = tt % 2
                b.use_bank(bk)
                rot = (tt * 512 + 1024) % 2048
                for c in range(NCH):
                    b.MM(b.bank(bk), wsh[:, c * 1024 + oc * 128: c * 1024 + (oc + 1) * 128],
                         xln3[c][:, rot:rot + 512], c == 0, c == NCH - 1,
                         inc=("st" if c == NCH - 1 else None))
                mm_ev = b.tlast
                b.wait("vector", mm_ev)
                b.TS(tmp2, xln3[oc][:, tt * 512:(tt + 1) * 512],
                     b.GC[:, NCH + oc:NCH + oc + 1], None, OP.mult)
                ev = b.STT(xsh[oc][:, tt * 512:(tt + 1) * 512], b.bank(bk),
                           b.GC[:, oc:oc + 1], tmp2, OP.mult, OP.add, inc="sv")
                b.read_bank(bk, ev)
        b.ckpt("vector", "sv")
        xsh_ev = ("sv", b.ctr["sv"])
        b.wait("scalar", xsh_ev)
        b.wait("tensor", xsh_ev)

        def fin_lni(tt, lo, mu, r, mur):
            for c in range(NCH):
                b.TT(tmp, xsh[c][:, lo:lo + 512], r, OP.mult)
                b.TT(tmp2, tmp, mur, OP.subtract)
                b.TS(xln3[c][:, lo:lo + 512], tmp2, b.colv("lnw", c),
                     b.colv("lnb", c), OP.mult, OP.add)

        b.ln_stats(xsh, sqkb=104.5, eps=1e-5, fin=fin_lni, base=116.5)
        b.ckpt("vector", "sv")
        xn_ev = ("sv", b.ctr["sv"])
        b.wait("tensor", xn_ev)

        shift_done = b.ckpt("tensor", "st")
        wkv_s = b.AB(122.0, 1024)
        b.wait("gpsimd", shift_done)
        ld_wkv = b.dma("gpsimd", wkv_s.rearrange("p (c n) -> p c n", c=NCH),
                       b.wkvT.rearrange("(c p) n -> p c n", p=128))
        wout_s = b.AB(100.5, 1024)
        ld_wout = b.dma("gpsimd", wout_s[0:64, :], b.woutT)
        wkvsb = b.AF(104.5, 2048)
        ssg = b.AF(112.5, 512)
        b.wait("tensor", ld_wkv)
        for tt in range(4):
            bk = 2 + tt % 2
            b.use_bank(bk)
            for c in range(NCH):
                b.MM(b.bank(bk), wkv_s[:, c * 128:(c + 1) * 128],
                     xln3[c][:, tt * 512:(tt + 1) * 512], c == 0, c == NCH - 1,
                     inc=("st" if c == NCH - 1 else None))
            mm_ev = b.tlast
            b.wait("scalar", mm_ev)
            b.ACT(ssg[0:64, :], b.bank(bk)[0:64, :], AF.Sigmoid)
            b.ACT(ssg[0:64, :], ssg[0:64, :], AF.Exp, scale=b.mcol(43)[0:64, :], inc="sc")
            b.wait("vector", ("sc", b.ctr["sc"]))
            ev = b.TT(wkvsb[0:64, tt * 512:(tt + 1) * 512], ssg[0:64, :],
                      b.bank(bk)[64:128, :], OP.mult, inc="sv")
            b.read_bank(bk, ev)
            b.wait("scalar", ("sv", b.ctr["sv"]))
        states = b.AF(32.0, 2048)
        wrow = b.AF(40.0, 2048)
        statesbf = b.AB(96.5, 2048)
        b.TS(wrow[0:64, :], wkvsb[0:64, :], 0.0, b.mcol(41)[0:64, :],
             OP.mult, OP.add, inc="sv")
        b.SCAN(states[0:64, :], wrow[0:64, :], wkvsb[0:64, :])
        b.VCOPY(statesbf[0:64, :], states[0:64, :], inc="sv")
        st_ev = ("sv", b.ctr["sv"])
        b.wait("tensor", st_ev)
        b.wait("tensor", ld_wout)
        for oc in range(NCH):
            for tt in range(4):
                bk = tt % 2
                b.use_bank(bk)
                b.MM(b.bank(bk), wout_s[0:64, oc * 128:(oc + 1) * 128],
                     statesbf[0:64, tt * 512:(tt + 1) * 512], True, True, inc="st")
                mm_ev = b.tlast
                b.wait("vector", mm_ev)
                ev = b.TT(b.xc(oc, tt * 512, (tt + 1) * 512), b.bank(bk),
                          b.xc(oc, tt * 512, (tt + 1) * 512), OP.add, inc="sv")
                b.read_bank(bk, ev)
        b.barrier()

    def ph6_lffn(self):
        b = self
        xl = b.RA
        mu = b.AF(116.5, 512)
        r = b.AF(118.5, 512)
        mur = b.AF(120.5, 512)
        var = b.AF(114.5, 512)
        r2 = b.AF(110.5, 512)
        mur2 = b.AF(112.5, 512)
        tmp = b.AF(96.5, 512)
        tmp2 = b.AF(98.5, 512)
        srow = b.AF(48.0, 2048)

        b.wait("scalar", b.ld["b1l"])
        b.wait("scalar", b.ld["b1h"])
        for tt in range(4):
            lo = tt * 512
            sq_evs = []
            for c in range(NCH):
                sq = b.AB(64.0 + c * 1.0, 512)
                sq_evs.append((sq, b.ACT(sq, b.xc(c, lo, lo + 512), AF.Square, inc="sc")))
            b.use_bank(0)
            b.use_bank(1)
            sum_ev = None
            for c in range(NCH):
                sum_ev = b.MM(b.bank(0), b.onesf, b.xc(c, lo, lo + 512),
                              c == 0, c == NCH - 1,
                              inc=("st" if c == NCH - 1 else None))
            for c in range(NCH):
                b.wait("tensor", sq_evs[c][1])
                b.MM(b.bank(1), b.onesb, sq_evs[c][0], c == 0, c == NCH - 1,
                     inc=("st" if c == NCH - 1 else None))
            mm_ev = b.tlast
            b.wait("vector", sum_ev)
            b.wait("vector", mm_ev)
            b.TS(mu, b.bank(0), 1.0 / D, None, OP.mult)
            b.TT(var, mu, mu, OP.mult)
            b.TS(tmp, b.bank(1), 1.0 / D, None, OP.mult)
            b.TT(var, tmp, var, OP.subtract, inc="sv")
            var_ev = ("sv", b.ctr["sv"])
            b.wait("scalar", var_ev)
            b.ACT(r, var, AF.Ln, bias=1e-5)
            b.ACT(r, r, AF.Exp, scale=-0.5)
            b.ACT(r2, var, AF.Ln)
            b.ACT(r2, r2, AF.Exp, scale=-0.5, inc="sc")
            r_ev = ("sc", b.ctr["sc"])
            b.wait("vector", r_ev)
            b.TT(mur, mu, r, OP.mult)
            b.TT(mur2, mu, r2, OP.mult, inc="sv")
            bank_free = b.ckpt("vector", "sv")
            b.read_bank(0, bank_free)
            b.read_bank(1, bank_free)
            b.use_bank(2)
            for c in range(NCH):
                b.TT(tmp, b.xc(c, lo, lo + 512), r, OP.mult)
                ev = b.TT(tmp2, tmp, mur, OP.subtract, inc="sv")
                b.wait("tensor", ev)
                b.MM(b.bank(2)[0:1, :], b.colv("rtok", c), tmp2, c == 0, c == NCH - 1,
                     inc="st")
                b.wait("vector", b.tlast)
                b.TT(tmp, b.xc(c, lo, lo + 512), r2, OP.mult)
                b.TT(tmp2, tmp, mur2, OP.subtract)
                b.TS(xl[c][:, lo:lo + 512], tmp2, b.colv("gl", c), None, OP.mult, inc="sv")
            sev = b.tlast
            b.wait("scalar", sev)
            scop = b.SCOPY(srow[0:1, lo:lo + 512], b.bank(2)[0:1, :], inc="sc")
            b.read_bank(2, scop)
            b.wait("vector", ("sc", b.ctr["sc"]))
        b.ckpt("vector", "sv")
        xl_ev = ("sv", b.ctr["sv"])
        b.ckpt("scalar", "sc")
        srow_ev = ("sc", b.ctr["sc"])
        b.wait("sync", srow_ev)

        s_tm = b.AF(123.0, 16)
        b.wait("sync", xl_ev)
        bev = b.dma("sync", b.srow_d, srow[0:1, :])
        b.wait("sync", bev)
        def _stm(e, out=s_tm, in_=b.srow_d.rearrange("u (tc p) -> (u p) tc", p=128)):
            return e.dma_start(out=out, in_=in_, allow_slow_non_contiguous=True)
        i = b.dma_ring["sync"]; b.dma_ring["sync"] = (i + 1) % 8
        _sm = "ss%d" % i
        if _sm in b.dma_last:
            b.wait("sync", (_sm, b.dma_last[_sm]))
        bev2 = b.emit("sync", _stm, inc=_sm, n=16)
        b.dma_last[_sm] = bev2[1]
        b.wait("vector", bev2)

        theta = b.mcol(38)
        step = b.mcol(39)
        dcol = b.mcol(40)
        trial = b.mcol(44)
        pc = b.mcol(45)
        gsc = b.AF(123.2, 16)
        b.RED(pc, s_tm, OP.max, absv=True, inc="sv")
        b.wait("tensor", ("sv", b.ctr["sv"]))
        b.use_bank(3)
        b.MM(b.bank(3)[:, 0:1], b.onesf, pc, True, True, inc="st")
        b.wait("vector", b.tlast)
        b.TS(theta, b.bank(3)[:, 0:1], -1.0, None, OP.mult)
        ev = b.TS(step, b.bank(3)[:, 0:1], 1.0, None, OP.mult, inc="sv")
        b.read_bank(3, ev)
        for it in range(48):
            b.TT(trial, theta, step, OP.add)
            b.TS(gsc, s_tm, trial, None, OP.is_ge)
            b.RED(pc, gsc, OP.add, inc="sv")
            b.wait("tensor", ("sv", b.ctr["sv"]))
            b.use_bank(3)
            b.MM(b.bank(3)[:, 0:1], b.onesf, pc, True, True, inc="st")
            b.wait("vector", b.tlast)
            b.TS(dcol, b.bank(3)[:, 0:1], 1024.0, None, OP.is_ge, inc="sv")
            b.read_bank(3, ("sv", b.ctr["sv"]))
            b.STT(theta, dcol, step, theta, OP.mult, OP.add)
            b.TS(step, step, 0.5, None, OP.mult)
        b.ckpt("vector", "sv")
        th_ev = ("sv", b.ctr["sv"])

        w1l_s = b.AB(96.5, 4096)
        b.wait("gpsimd", b.gate_t)
        b.wait("gpsimd", xl_ev)
        ld_w1l = b.dma("gpsimd", w1l_s.rearrange("p (c n) -> p c n", c=NCH),
                       b.w1l.rearrange("(c p) n -> p c n", p=128))
        w2l_s = b.AB(104.5, 4096)
        ld_w2l = b.dma("gpsimd", w2l_s.rearrange("p (c n) -> p c n", c=4),
                       b.w2l.rearrange("(c p) n -> p c n", p=128))
        Gl = b.AB(64.0, 8192)
        b.wait("tensor", xl_ev)
        b.wait("tensor", ld_w1l)
        for hc in range(4):
            for tt in range(4):
                bk = tt % 2
                b.use_bank(bk)
                for c in range(NCH):
                    b.MM(b.bank(bk), w1l_s[:, c * 512 + hc * 128: c * 512 + (hc + 1) * 128],
                         xl[c][:, tt * 512:(tt + 1) * 512], c == 0, c == NCH - 1,
                         inc=("st" if c == NCH - 1 else None))
                mm_ev = b.tlast
                b.wait("scalar", mm_ev)
                ev = b.ACT(Gl[:, hc * 2048 + tt * 512: hc * 2048 + (tt + 1) * 512],
                           b.bank(bk), AF.Gelu, bias=b.MISC[:, hc:hc + 1], inc="sc")
                b.read_bank(bk, ev)
        w1l_done = b.ckpt("tensor", "st")
        gl_ev = ("sc", b.ctr["sc"])
        b.wait("tensor", gl_ev)
        b.wait("tensor", ld_w2l)
        tmpl = b.AF(121.0, 512)
        for oc in range(NCH):
            for tt in range(4):
                bk = 2 + tt % 2
                b.use_bank(bk)
                for hc in range(4):
                    b.MM(b.bank(bk), w2l_s[:, hc * 1024 + oc * 128: hc * 1024 + (oc + 1) * 128],
                         Gl[:, hc * 2048 + tt * 512: hc * 2048 + (tt + 1) * 512],
                         hc == 0, hc == 3, inc=("st" if hc == 3 else None))
                mm_ev = b.tlast
                b.wait("vector", mm_ev)
                b.TS(tmpl, b.bank(bk), b.colv("b2l", oc), None, OP.add)
                ev = b.TT(b.xc(oc, tt * 512, (tt + 1) * 512), tmpl,
                          b.xc(oc, tt * 512, (tt + 1) * 512), OP.add, inc="sv")
                b.read_bank(bk, ev)
        light_done = b.ckpt("tensor", "st")

        mrow = b.AF(112.5, 2048)
        mrowb = b.AB(120.5, 2048)
        b.wait("vector", th_ev)
        b.TS(mrow[0:1, :], srow[0:1, :], theta[0:1, :], None, OP.is_ge)
        b.VCOPY(mrowb[0:1, :], mrow[0:1, :], inc="sv")
        mrow_ev = ("sv", b.ctr["sv"])
        b.wait("tensor", mrow_ev)
        b.wait("tensor", light_done)
        maskb = b.AF(96.5, 2048)
        for tt in range(4):
            b.use_bank(0)
            mevs = []
            for i in range(4):
                tc = tt * 4 + i
                mevs.append(b.MM(b.bank(0)[:, i * 128:(i + 1) * 128], b.onesb[0:1, :],
                                 mrowb[0:1, tc * 128:(tc + 1) * 128], True, True,
                                 inc="st"))
            for me_ in mevs:
                b.wait("scalar", me_)
            mev = b.ACT(maskb[:, tt * 512:(tt + 1) * 512], b.bank(0), AF.Copy, inc="sc")
            b.read_bank(0, mev)
        mask_ev = ("sc", b.ctr["sc"])
        b.wait("vector", mask_ev)

        slots = [b.AB(32.0, 8192), b.AB(48.0, 8192)]
        slot_free = [b.gate_t, mrow_ev]
        Gh = b.AB(64.0, 16384)
        w1r = b.w1h.rearrange("(c p) n -> p c n", p=128)
        w2r = b.w2h.rearrange("(j p) d -> p j d", p=128)
        b.wait("gpsimd", bev)
        nload = 0
        for g in range(4):
            si = nload % 2
            nload += 1
            b.wait("gpsimd", slot_free[si])
            ldev = b.dma("gpsimd", slots[si].rearrange("p (c n) -> p c n", c=NCH),
                         w1r[:, :, g * 1024:(g + 1) * 1024])
            b.wait("vector", ldev)
            for c in range(NCH):
                b.TS(slots[si][:, c * 1024:(c + 1) * 1024],
                     slots[si][:, c * 1024:(c + 1) * 1024],
                     b.GC[:, 2 * NCH + c:2 * NCH + c + 1], None, OP.mult,
                     inc=("sv" if c == NCH - 1 else None))
            rat_ev = ("sv", b.ctr["sv"])
            b.wait("tensor", rat_ev)
            for hcl in range(8):
                hc = g * 8 + hcl
                for tt in range(4):
                    bk = (hcl * 4 + tt) % 2
                    b.use_bank(bk)
                    for c in range(NCH):
                        b.MM(b.bank(bk),
                             slots[si][:, c * 1024 + hcl * 128: c * 1024 + (hcl + 1) * 128],
                             xl[c][:, tt * 512:(tt + 1) * 512], c == 0, c == NCH - 1,
                             inc=("st" if c == NCH - 1 else None))
                    mm_ev = b.tlast
                    b.wait("scalar", mm_ev)
                    ev = b.ACT(Gh[:, hcl * 2048 + tt * 512: hcl * 2048 + (tt + 1) * 512],
                               b.bank(bk), AF.Gelu, bias=b.MISC[:, 4 + hc:5 + hc],
                               inc="sc")
                    b.read_bank(bk, ev)
            slot_free[si] = b.ckpt("tensor", "st")
            gh_ev = ("sc", b.ctr["sc"])
            si = nload % 2
            nload += 1
            b.wait("gpsimd", slot_free[si])
            ldev = b.dma("gpsimd", slots[si].rearrange("p (j d) -> p j d", j=8),
                         w2r[:, g * 8:(g + 1) * 8, :])
            b.wait("tensor", ldev)
            b.wait("tensor", gh_ev)
            for dchunk in range(8):
                for tt in range(4):
                    bk = 2 + (dchunk * 4 + tt) % 4
                    b.use_bank(bk)
                    for j in range(8):
                        b.MM(b.bank(bk),
                             slots[si][:, j * 1024 + dchunk * 128: j * 1024 + (dchunk + 1) * 128],
                             Gh[:, j * 2048 + tt * 512: j * 2048 + (tt + 1) * 512],
                             j == 0, j == 7, inc=("st" if j == 7 else None))
                    mm_ev = b.tlast
                    b.wait("vector", mm_ev)
                    if g == 0:
                        b.TS(tmpl, b.bank(bk), b.colv("b2h", dchunk), None, OP.add)
                        b.TT(tmpl, tmpl, maskb[:, tt * 512:(tt + 1) * 512], OP.mult)
                    else:
                        b.TT(tmpl, b.bank(bk), maskb[:, tt * 512:(tt + 1) * 512],
                             OP.mult)
                    ev = b.TT(b.xc(dchunk, tt * 512, (tt + 1) * 512), tmpl,
                              b.xc(dchunk, tt * 512, (tt + 1) * 512), OP.add, inc="sv")
                    b.read_bank(bk, ev)
            slot_free[si] = b.ckpt("tensor", "st")
        b.barrier()

    def ph7_store(self):
        b = self
        b.wait("sync", ("sv", b.ctr["sv"]))
        b.wait("sync", b.tlast)
        ev = b.dma("sync", b.y_out.rearrange("(c p) t -> p c t", p=128),
                   b.xs_a.rearrange("p (c t) -> p c t", c=NCH), sem="so")
        b.wait("gpsimd", ev)

    def build(self):
        self.ph0_loads()
        self.ph1_lambda_ln1()
        self.ph1b_v()
        self.ph2_attention()
        self.ph3_attnout()
        self.ph4_moe()
        self.ph5_rwkv()
        self.ph6_lffn()
        self.ph7_store()
        nc = self.nc
        with nc.Block() as block:
            for eng in ENGINES:
                dec = getattr(block, eng)
                ops = self.stream[eng]
                def make(ops):
                    def _(e):
                        for op in ops:
                            op(e)
                    return _
                dec(make(ops))
        self.es.close()
        return nc


_CACHE = {}


def build_program():
    if "nc" not in _CACHE:
        _CACHE["nc"] = MK().build()
    return _CACHE["nc"]


def kernel(**inputs):
    g = lambda n: np.asarray(inputs[n], dtype=np.float32)
    cT = np.ascontiguousarray

    wqkvT = cT(np.concatenate([g("L0_attn_Wq").T, g("L0_attn_Wk").T,
                               g("L0_attn_Wv").T], axis=1))
    woT = cT(g("L0_attn_Wo").T)
    lam = np.zeros((1, 128), np.float32)
    lam[0, 0:32] = g("L0_attn_lambda_q1")
    lam[0, 32:64] = g("L0_attn_lambda_k1")
    lam[0, 64:96] = g("L0_attn_lambda_q2")
    lam[0, 96:128] = g("L0_attn_lambda_k2")
    sublnw4 = cT(np.tile(np.broadcast_to(g("L0_attn_subln_w"), (128, 64)), (1, 4)))
    w1T = cT(g("L0_ffn_w1").transpose(0, 2, 1))
    w2T = cT(g("L0_ffn_w2").transpose(0, 2, 1))
    wmm = np.zeros((D, 16), np.float32)
    wmm[:, 0:4] = g("L0_ffn_w_macro").T
    wmm[:, 4:12] = g("L0_ffn_w_micro").reshape(8, D).T
    wmmT = cT(wmm)
    wshiftT = cT(g("L1_attn_W_shift").T)
    wkvT = cT(np.concatenate([g("L1_attn_W_key").T, g("L1_attn_W_value").T], axis=1))
    woutT = cT(g("L1_attn_W_output").T)
    tdec = cT(g("L1_attn_time_decay").reshape(64, 1))
    tfirst = cT(g("L1_attn_time_first").reshape(64, 1))
    w1l = cT(g("L1_ffn_w1_light"))
    w2l = cT(g("L1_ffn_w2_light"))
    w1h = cT(g("L1_ffn_w1_heavy"))
    w2h = cT(g("L1_ffn_w2_heavy"))
    b1l = cT(np.broadcast_to(g("L1_ffn_b1_light").reshape(4, 128).T, (128, 4)))
    b1h = cT(np.broadcast_to(g("L1_ffn_b1_heavy").reshape(32, 128).T, (128, 32)))
    colpack = np.zeros((128, 9 * NCH), np.float32)
    for i, v in enumerate([g("L0_ffn_norm_w"), g("L1_attn_shift_gate"),
                           g("L1_attn_ln_w"), g("L1_attn_ln_b"),
                           g("L1_ffn_routing_token"), g("L1_ffn_gamma_light"),
                           g("L1_ffn_gamma_heavy"), g("L1_ffn_b2_light"),
                           g("L1_ffn_b2_heavy")]):
        colpack[:, i * NCH:(i + 1) * NCH] = v.reshape(NCH, 128).T
    colpack = cT(colpack)
    eye = np.eye(128, dtype=np.float32)
    ltri = np.zeros((128, 16), np.float32)
    for m in range(8):
        ltri[:m, m] = 1.0
    ltri[0:8, 8] = np.arange(8, dtype=np.float32) * 656.0
    ltri = cT(ltri)
    constv = cT(np.broadcast_to(np.array([0.0, 1.0, -0.5, 1e-5, 1e-6,
                                          SCALING, 1.0 / 64.0,
                                          math.log(1.0 - LAMBDA_INIT)],
                                         np.float32), (128, 8)))
    onesin = np.ones((128, 128), np.float32)

    x = g("L0_attn_x")
    nc = build_program()
    in_maps = []
    for bb in range(8):
        in_maps.append(dict(
            x=cT(x[bb].T), wqkvT=wqkvT, woT=woT, lam=lam, sublnw4=sublnw4,
            constv=constv,
            w1T=w1T, w2T=w2T, wmmT=wmmT, wshiftT=wshiftT, wkvT=wkvT,
            woutT=woutT, tdec=tdec, tfirst=tfirst, w1l=w1l, b1l=b1l, w2l=w2l,
            w1h=w1h, b1h=b1h, w2h=w2h, colpack=colpack, eye=eye, onesin=onesin,
            ltri=ltri,
        ))
    res = run_bass_kernel_spmd(nc, in_maps, core_ids=list(range(8)))
    out = np.empty((8, T, D), np.float32)
    for bb in range(8):
        out[bb] = np.asarray(res.results[bb]["y"], np.float32).T
    return out


# version 12
# speedup vs baseline: 1.2220x; 1.0013x over previous
import math
import numpy as np
import concourse.bass as bass
import concourse.mybir as mybir
from concourse.bass_utils import run_bass_kernel_spmd
from concourse.alu_op_type import AluOpType as OP

F32 = mybir.dt.float32
BF16 = mybir.dt.bfloat16
AF = mybir.ActivationFunctionType
AX = mybir.AxisListType.X

T = 2048
D = 1024
NCH = 8
LAMBDA_INIT = 0.8 - 0.6 * math.exp(-0.3 * 12)
SCALING = 32.0 ** -0.5

ENGINES = ("sync", "tensor", "vector", "scalar", "gpsimd")


class MK:
    def __init__(self):
        self.nc = bass.Bass("TRN2", target_bir_lowering=False)
        from contextlib import ExitStack
        self.es = ExitStack()
        nc = self.nc
        self.stream = {e: [] for e in ENGINES}
        self.sem = {}
        self.ctr = {}
        names = ["sv", "sc", "so"]
        names += ["st%d" % i for i in range(8)]
        names += ["sg%d" % i for i in range(8)] + ["ss%d" % i for i in range(8)]
        for name in names:
            self.sem[name] = self.es.enter_context(nc.semaphore(name))
            self.ctr[name] = 0
        self.dma_ring = {"gpsimd": 0, "sync": 0}
        self.dma_last = {}
        self.t_ring = 0
        self.tring_n = 1
        self.tlast = None

        def din(name, shape):
            return nc.dram_tensor(name, shape, F32, kind="ExternalInput").ap()

        self.x_in = din("x", [D, T])
        self.wqkvT = din("wqkvT", [D, 3 * D])
        self.woT = din("woT", [D, D])
        self.lam = din("lam", [1, 128])
        self.sublnw4 = din("sublnw4", [128, 256])
        self.w1T = din("w1T", [8, D, 2048])
        self.w2T = din("w2T", [8, 2048, D])
        self.wmmT = din("wmmT", [D, 16])
        self.wshiftT = din("wshiftT", [D, D])
        self.wkvT = din("wkvT", [D, 128])
        self.woutT = din("woutT", [64, D])
        self.tdec = din("tdec", [64, 1])
        self.tfirst = din("tfirst", [64, 1])
        self.w1l = din("w1l", [D, 512])
        self.b1l = din("b1l", [128, 4])
        self.w2l = din("w2l", [512, D])
        self.w1h = din("w1h", [D, 4096])
        self.b1h = din("b1h", [128, 32])
        self.w2h = din("w2h", [4096, D])
        self.colpack = din("colpack", [128, 9 * NCH])
        self.ltri = din("ltri", [128, 16])
        self.constv = din("constv", [128, 8])
        self.eye = din("eye", [128, 128])
        self.onesin = din("onesin", [128, 128])

        self.y_out = nc.dram_tensor("y", [D, T], F32, kind="ExternalOutput").ap()
        self.xspill = nc.dram_tensor("xspill", [D, T], F32, kind="Internal").ap()
        self.srow_d = nc.dram_tensor("srow_d", [1, T], F32, kind="Internal").ap()
        self.crow_d = nc.dram_tensor("crow_d", [16, 8, 128], BF16, kind="Internal").ap()
        self.zslots = nc.dram_tensor("zslots", [5248, 1024], BF16, kind="Internal").ap()
        self.yslots = nc.dram_tensor("yslots", [5248, 1024], BF16, kind="Internal").ap()
        self.w1scr = nc.dram_tensor("w1scr", [2048, 8192], BF16, kind="Internal").ap()
        self.w2scr = nc.dram_tensor("w2scr", [2048, 8192], BF16, kind="Internal").ap()
        self.rb2 = nc.dram_tensor("rb2", [4, T], F32, kind="Internal").ap()

        self.xs_t = self.es.enter_context(nc.sbuf_tensor("xs", [128, NCH * T], F32))
        self.ar_t = self.es.enter_context(nc.sbuf_tensor("ar", [128, 65536], BF16))
        self.xs_a = self.xs_t.ap()
        self.ar = self.ar_t.ap()
        self.arf = self.ar.bitcast(F32)

        self.PS = self.es.enter_context(nc.psum_tensor("PS", [128, 3072], F32))
        self.PT = self.es.enter_context(nc.psum_tensor("PT", [128, 2048], BF16))
        self.ps = self.PS.ap()
        self.pt = self.PT.ap()

        self.identb = self.AB(124.5, 128)
        self.identf = self.AF(127.15, 128)
        self.onesf = self.AF(124.75, 128)
        self.onesb = self.AB(125.25, 128)
        self.subw_s = self.AF(125.5, 256)
        self.COLS = {}
        for i, n in enumerate(["normw", "sgate", "lnw", "lnb", "rtok", "gl", "gh",
                               "b2l", "b2h"]):
            self.COLS[n] = self.AF(126.5 + i * 0.03125, NCH)
        self.GC = self.AF(126.8, 3 * NCH)
        self.MISC = self.AF(126.9, 64)
        self.CONSTS = [0.0, 1.0, -0.5, 1e-5, 1e-6, SCALING, 1.0 / 64.0,
                       math.log(1.0 - LAMBDA_INIT)]
        cv = self.MISC[:, 46:54]
        self.cv = cv
        for i, v in enumerate(self.CONSTS):
            nc.const_aps.register(F32, v, cv[:, i:i + 1])
        self.RA = [self.AB(c * 4.0, T) for c in range(NCH)]
        self.ld = {}
        self.bank_last_read = {}
        self.pt_last_read = {}
        self.pt_idx = 0

    def AB(self, kb, cols):
        o = int(round(kb * 512))
        assert o + cols <= 65536, (kb, cols)
        return self.ar[:, o:o + cols]

    def AF(self, kb, cols):
        o = int(round(kb * 256))
        assert o + cols <= 32768, (kb, cols)
        return self.arf[:, o:o + cols]

    def xc(self, c, lo=0, hi=T):
        return self.xs_a[:, c * T + lo:c * T + hi]

    def bank(self, i, lo=0, hi=512):
        return self.ps[:, i * 512 + lo:i * 512 + hi]

    def colv(self, name, c=0):
        return self.COLS[name][:, c:c + 1]

    def gcv(self, which, c):
        return self.GC[:, which * NCH + c:which * NCH + c + 1]

    def mcol(self, i):
        return self.MISC[:, i:i + 1]

    OWN = {"vector": "sv", "scalar": "sc", "tensor": "st"}

    def emit(self, eng, fn, inc=None, n=1):
        own = self.OWN.get(eng)
        if eng in ("vector", "scalar"):
            assert inc in (None, own), (eng, inc)
            prev = self.ctr[own]
            h = self.sem[own]
            if prev > 0:
                self.stream[eng].append(
                    lambda e, h=h, prev=prev: e.wait_ge(h, prev))
            self.ctr[own] += 1
            def op(e, fn=fn, h=h):
                fn(e).then_inc(h, 1)
            self.stream[eng].append(op)
            return (own, self.ctr[own])
        if inc is None:
            self.stream[eng].append(fn)
            return None
        if eng == "tensor":
            assert inc == "st"
            inc = "st%d" % (self.t_ring % self.tring_n)
            self.t_ring += 1
        h = self.sem[inc]
        if eng == "tensor" and self.ctr[inc] > 0:
            prev = self.ctr[inc]
            self.stream[eng].append(lambda e, h=h, prev=prev: e.wait_ge(h, prev))
        self.ctr[inc] += n
        def op2(e, fn=fn, h=h, n=n):
            fn(e).then_inc(h, n)
        self.stream[eng].append(op2)
        ev = (inc, self.ctr[inc])
        if eng == "tensor":
            self.tlast = ev
        return ev

    def ckpt(self, eng, sem=None):
        if eng in ("vector", "scalar"):
            return (self.OWN[eng], self.ctr[self.OWN[eng]])
        for i in range(8):
            s = "st%d" % i
            if self.ctr[s] > 0:
                self.wait("tensor", (s, self.ctr[s]))
        return self.emit(eng, lambda e: e.nop(), inc="st")

    def wait(self, eng, ev):
        if ev is None:
            return
        sem, val = ev
        h = self.sem[sem]
        self.stream[eng].append(lambda e, h=h, val=val: e.wait_ge(h, val))

    def dma(self, eng, out, in_, sem=None):
        if sem is None:
            pfx = "sg" if eng == "gpsimd" else "ss"
            i = self.dma_ring[eng]
            self.dma_ring[eng] = (i + 1) % 8
            sem = "%s%d" % (pfx, i)
            if sem in self.dma_last:
                self.wait(eng, (sem, self.dma_last[sem]))
        ev = self.emit(eng, lambda e, out=out, in_=in_: e.dma_start(out=out, in_=in_), inc=sem, n=16)
        self.dma_last[sem] = ev[1]
        return ev

    def barrier(self):
        evs = [self.ckpt(e, s) for e, s in
               (("tensor", "st"), ("vector", "sv"), ("scalar", "sc"))]
        for e in ("tensor", "vector", "scalar"):
            for ev in evs:
                self.wait(e, ev)
        self.gate_t = evs[0]
        return evs[0]

    def use_ring(self, n):
        self.tring_n = n
        self.t_ring = 0

    def use_bank(self, i, ev=None):
        self.wait("tensor", self.bank_last_read.get(i))
        if ev is not None:
            self.bank_last_read[i] = ev

    def read_bank(self, i, ev):
        self.bank_last_read[i] = ev

    def TT(self, out, a, bb, op, inc=None):
        return self.emit("vector", lambda e, out=out, a=a, bb=bb, op=op:
                         e.tensor_tensor(out=out, in0=a, in1=bb, op=op), inc=inc)

    def TS(self, out, a, s1, s2, op0, op1=None, inc=None):
        if op1 is None:
            return self.emit("vector", lambda e, out=out, a=a, s1=s1, op0=op0:
                             e.tensor_scalar(out=out, in0=a, scalar1=s1, scalar2=None, op0=op0), inc=inc)
        return self.emit("vector", lambda e, out=out, a=a, s1=s1, s2=s2, op0=op0, op1=op1:
                         e.tensor_scalar(out=out, in0=a, scalar1=s1, scalar2=s2, op0=op0, op1=op1), inc=inc)

    def STT(self, out, a, s, bb, op0, op1, inc=None):
        return self.emit("vector", lambda e, out=out, a=a, s=s, bb=bb, op0=op0, op1=op1:
                         e.scalar_tensor_tensor(out=out, in0=a, scalar=s, in1=bb, op0=op0, op1=op1), inc=inc)

    def VCOPY(self, out, a, inc=None):
        return self.emit("vector", lambda e, out=out, a=a: e.tensor_copy(out, a), inc=inc)

    def RED(self, out, a, op, inc=None, absv=None):
        return self.emit("vector", lambda e, out=out, a=a, op=op, absv=absv:
                         e.tensor_reduce(out=out, in_=a, axis=AX, op=op, apply_absolute_value=absv), inc=inc)

    def RECIP(self, out, a, inc=None):
        return self.emit("vector", lambda e, out=out, a=a: e.reciprocal(out, a), inc=inc)

    def MEMSET(self, ap, v, inc=None):
        return self.emit("vector", lambda e, ap=ap, v=v: e.memset(ap, v), inc=inc)

    def SCAN(self, out, d0, d1, inc=None):
        return self.emit("vector", lambda e, out=out, d0=d0, d1=d1:
                         e.tensor_tensor_scan(out=out, data0=d0, data1=d1, initial=0.0,
                                              op0=OP.mult, op1=OP.add), inc=inc)

    def ACT(self, out, a, fn, bias=0.0, scale=1.0, accum=None, inc=None):
        return self.emit("scalar", lambda e, out=out, a=a, fn=fn, bias=bias, scale=scale, accum=accum:
                         e.activation(out=out, in_=a, func=fn, bias=bias, scale=scale, accum_out=accum), inc=inc)

    def SCOPY(self, out, a, inc=None):
        return self.emit("scalar", lambda e, out=out, a=a: e.copy(out, a), inc=inc)

    def MM(self, out, lhsT, rhs, start, stop, tp=None, inc=None):
        return self.emit("tensor", lambda e, out=out, lhsT=lhsT, rhs=rhs, start=start, stop=stop, tp=tp:
                         e.matmul(out, lhsT, rhs, start=start, stop=stop, tile_position=tp), inc=inc)

    def TRANS(self, out, in_, ident, inc=None):
        return self.emit("tensor", lambda e, out=out, in_=in_, ident=ident:
                         e.transpose(out, in_, ident), inc=inc)

    def pt_slot(self):
        i = self.pt_idx
        self.pt_idx += 1
        off = [0, 1024, 256, 1280][i % 4]
        return self.pt[:, off:off + 128], i % 4

    def ln_stats(self, srcs, sqkb, eps, fin, rms=False, base=110.5, extra=None):
        mu = self.AF(base, 512)
        r = self.AF(base + 2.0, 512)
        mur = self.AF(base + 4.0, 512)
        for tt in range(4):
            lo = tt * 512
            sq_evs = []
            for c in range(len(srcs)):
                sq = self.AB(sqkb + c * 1.0, 512)
                sq_evs.append((sq, self.ACT(sq, srcs[c][:, lo:lo + 512], AF.Square, inc="sc")))
            self.use_bank(0)
            self.use_bank(1)
            sum_ev = None
            if not rms:
                for c in range(NCH):
                    sum_ev = self.MM(self.bank(0), self.onesf, srcs[c][:, lo:lo + 512],
                                     c == 0, c == NCH - 1,
                                     inc=("st" if c == NCH - 1 else None))
            for c in range(NCH):
                self.wait("tensor", sq_evs[c][1])
                self.MM(self.bank(1), self.onesb, sq_evs[c][0], c == 0, c == NCH - 1,
                        inc=("st" if c == NCH - 1 else None))
            mm_ev = self.tlast
            self.wait("vector", sum_ev)
            self.wait("vector", mm_ev)
            if not rms:
                self.TS(mu, self.bank(0), 1.0 / D, None, OP.mult)
                self.TT(mur, mu, mu, OP.mult)
                self.TS(r, self.bank(1), 1.0 / D, None, OP.mult)
                self.TT(r, r, mur, OP.subtract, inc="sv")
            else:
                self.TS(r, self.bank(1), 1.0 / D, None, OP.mult, inc="sv")
            var_ev = ("sv", self.ctr["sv"])
            if extra is not None:
                extra(tt, lo, r, var_ev)
            self.wait("scalar", var_ev)
            self.ACT(r, r, AF.Ln, bias=eps)
            self.ACT(r, r, AF.Exp, scale=-0.5, inc="sc")
            r_ev = ("sc", self.ctr["sc"])
            self.wait("vector", r_ev)
            if not rms:
                self.TT(mur, mu, r, OP.mult)
            fin(tt, lo, mu, r, mur)
            free_ev = self.ckpt("vector", "sv")
            self.read_bank(0, free_ev)
            self.read_bank(1, free_ev)
        self.wait("scalar", ("sv", self.ctr["sv"]))

    def proj_fm(self, wsrc3, col0, out_cb, rhs_list, slots, slot_free, nouts,
                resid=False, evac="scalar"):
        act_evs = {}
        for oc in range(nouts):
            si = oc % 4
            sl = slots[si]
            self.wait("gpsimd", slot_free[si])
            ldev = self.dma("gpsimd", sl.rearrange("p (c n) -> p c n", c=NCH),
                            wsrc3[:, :, col0 + oc * 128: col0 + (oc + 1) * 128])
            self.wait("tensor", ldev)
            for tt in range(4):
                bk = tt % 2
                self.use_bank(bk)
                for c in range(NCH):
                    self.MM(self.bank(bk), sl[:, c * 128:(c + 1) * 128],
                            rhs_list[c][:, tt * 512:(tt + 1) * 512], c == 0, c == NCH - 1,
                            inc=("st" if c == NCH - 1 else None))
                mm_ev = self.tlast
                ev = out_cb(oc, tt, self.bank(bk), mm_ev)
                self.read_bank(bk, ev)
            slot_free[si] = self.ckpt("tensor", "st")
        return slot_free

    def ph0_loads(self):
        b = self
        b.ld["x"] = b.dma("sync", b.xs_a.rearrange("p (c t) -> p c t", c=NCH),
                          b.x_in.rearrange("(c p) t -> p c t", p=128))
        b.ld["eye_b"] = b.dma("gpsimd", b.identb, b.eye)
        b.ld["ones_f"] = b.dma("sync", b.onesf, b.onesin)
        b.ld["ones_b"] = b.dma("gpsimd", b.onesb, b.onesin)
        b.ld["subw"] = b.dma("sync", b.subw_s, b.sublnw4)
        b.ld["cols"] = b.dma("sync", b.arf[:, int(126.5 * 256):int(126.5 * 256) + 9 * NCH],
                             b.colpack)
        b.ld["b1l"] = b.dma("sync", b.MISC[:, 0:4], b.b1l)
        b.ld["b1h"] = b.dma("sync", b.MISC[:, 4:36], b.b1h)
        b.ld["constv"] = b.dma("sync", b.cv, b.constv)
        b.ld["ltri"] = b.dma("sync", b.AF(127.65, 16), b.ltri)
        b.ld["identf"] = b.dma("sync", b.identf, b.eye)
        b.ld["tdec"] = b.dma("sync", b.mcol(41)[0:64, :], b.tdec)
        b.ld["tf"] = b.dma("sync", b.mcol(42)[0:64, :], b.tfirst)

    def ph1_lambda_ln1(self):
        b = self
        lamrow = b.AF(110.5, 128)
        b.ld["lam"] = b.dma("sync", lamrow[0:1, :], b.lam)
        for ev in (b.ld["lam"], b.ld["ones_f"]):
            b.wait("vector", ev)
        b.wait("scalar", b.ld["constv"])
        b.wait("vector", b.ld["constv"])
        b.TT(lamrow[0:1, 0:32], lamrow[0:1, 0:32], lamrow[0:1, 32:64], OP.mult)
        b.TT(lamrow[0:1, 64:96], lamrow[0:1, 64:96], lamrow[0:1, 96:128], OP.mult)
        b.RED(lamrow[0:1, 32:33], lamrow[0:1, 0:32], OP.add)
        b.RED(lamrow[0:1, 96:97], lamrow[0:1, 64:96], OP.add, inc="sv")
        b.wait("scalar", ("sv", b.ctr["sv"]))
        b.ACT(lamrow[0:1, 33:34], lamrow[0:1, 32:33], AF.Exp)
        b.ACT(lamrow[0:1, 97:98], lamrow[0:1, 96:97], AF.Exp, inc="sc")
        b.wait("vector", ("sc", b.ctr["sc"]))
        b.TT(lamrow[0:1, 0:1], lamrow[0:1, 33:34], lamrow[0:1, 97:98], OP.subtract)
        b.TS(lamrow[0:1, 0:1], lamrow[0:1, 0:1], 1.0, LAMBDA_INIT, OP.mult, OP.add, inc="sv")
        b.wait("tensor", ("sv", b.ctr["sv"]))
        b.use_bank(0)
        b.MM(b.bank(0, 0, 1), b.onesf[0:1, :], lamrow[0:1, 0:1], True, True, inc="st")
        b.wait("vector", b.tlast)
        b.TS(b.mcol(36), b.bank(0, 0, 1), 1.0, None, OP.mult)
        ev = b.TS(b.mcol(37), b.bank(0, 0, 1), -1.0, None, OP.mult, inc="sv")
        b.read_bank(0, ev)

        b.wait("vector", b.ld["cols"])
        b.wait("scalar", b.ld["cols"])
        b.ACT(b.GC[:, 0:NCH], b.COLS["sgate"], AF.Sigmoid, inc="sc")
        b.wait("vector", ("sc", b.ctr["sc"]))
        b.TS(b.GC[:, NCH:2 * NCH], b.GC[:, 0:NCH], -1.0, 1.0, OP.mult, OP.add)
        b.RECIP(b.GC[:, 2 * NCH:3 * NCH], b.COLS["gl"])
        b.TT(b.GC[:, 2 * NCH:3 * NCH], b.COLS["gh"], b.GC[:, 2 * NCH:3 * NCH], OP.mult, inc="sv")
        b.wait("scalar", b.ld["tdec"])
        b.wait("scalar", b.ld["tf"])
        b.ACT(b.mcol(41)[0:64, :], b.mcol(41)[0:64, :], AF.Exp)
        b.ACT(b.mcol(43)[0:64, :], b.mcol(42)[0:64, :], AF.Exp, inc="sc")
        b.wait("vector", ("sc", b.ctr["sc"]))
        b.TS(b.mcol(43)[0:64, :], b.mcol(43)[0:64, :], -1.0, None, OP.mult, inc="sv")

        zt0 = b.AB(32.0, 1024)
        b.MEMSET(zt0, 0.0, inc="sv")
        z0ev = ("sv", b.ctr["sv"])
        b.wait("sync", z0ev)
        b.zinit_evs = []
        for k in range(41):
            b.zinit_evs.append(b.dma("sync", b.zslots[k * 128:(k + 1) * 128, :], zt0))
        for e in range(8):
            b.zinit_evs.append(b.dma("sync", b.yslots[e * 656 + 640:e * 656 + 656, :],
                                     zt0[0:16, :]))
        b.wait("scalar", b.ld["x"])
        b.wait("tensor", b.ld["x"])
        b.wait("tensor", b.ld["ones_b"])
        xln = b.RA
        tmp = b.AF(96.5, 512)

        def fin(tt, lo, mu, r, mur):
            for c in range(NCH):
                b.TT(tmp, b.xc(c, lo, lo + 512), r, OP.mult)
                b.TT(xln[c][:, lo:lo + 512], tmp, mur, OP.subtract)

        b.ln_stats([b.xc(c) for c in range(NCH)], sqkb=64.5, eps=1e-5, fin=fin)
        b.ckpt("vector", "sv")
        b.xln_ev = ("sv", b.ctr["sv"])

    def ph1b_v(self):
        b = self
        vtm = b.AB(32.0, 16896)
        b.vtm = vtm
        b.wait("tensor", b.ld["eye_b"])
        b.wait("vector", b.xln_ev)
        for zev in b.zinit_evs:
            b.wait("vector", zev)
        b.MEMSET(vtm, 1.0, inc="sv")
        vtm_ones = ("sv", b.ctr["sv"])
        b.wait("tensor", vtm_ones)
        b.wait("tensor", b.xln_ev)
        b.wait("scalar", b.xln_ev)
        b.wq_slots = [b.AB(116.5 + i * 2.0, 1024) for i in range(4)]
        b.slot_free = [None] * 4
        b.wqkv3 = b.wqkvT.rearrange("(c p) n -> p c n", p=128)
        vv = b.AB(104.5, 2048)
        xln = b.RA
        vtm_cp = None
        for oc in range(NCH):
            si = oc % 4
            sl = b.wq_slots[si]
            b.wait("gpsimd", b.slot_free[si])
            ldev = b.dma("gpsimd", sl.rearrange("p (c n) -> p c n", c=NCH),
                         b.wqkv3[:, :, 2 * D + oc * 128: 2 * D + (oc + 1) * 128])
            b.wait("tensor", ldev)
            tr_last = None
            for tt in range(4):
                bk = tt % 2
                self.use_bank(bk)
                for c in range(NCH):
                    b.MM(b.bank(bk), sl[:, c * 128:(c + 1) * 128],
                         xln[c][:, tt * 512:(tt + 1) * 512], c == 0, c == NCH - 1,
                         inc=("st" if c == NCH - 1 else None))
                mm_ev = b.tlast
                b.wait("scalar", mm_ev)
                ev = b.ACT(vv[:, tt * 512:(tt + 1) * 512], b.bank(bk), AF.Copy, inc="sc")
                b.read_bank(bk, ev)
            vv_ev = ("sc", b.ctr["sc"])
            b.wait("tensor", vv_ev)
            b.slot_free[si] = b.ckpt("tensor", "st")
            for kc in range(16):
                pslot, pi = b.pt_slot()
                b.wait("tensor", b.pt_last_read.get(pi))
                b.TRANS(pslot, vv[:, kc * 128:(kc + 1) * 128], b.identb, inc="st")
                tr_ev = b.tlast
                tr_last = tr_ev
                b.wait("vector", tr_ev)
                base = kc * 1056 + 2 * oc * 66
                dst = vtm[:, base:base + 132].rearrange("p (h z) -> p h z", z=66)[:, :, 0:64]
                src = pslot.rearrange("p (h z) -> p h z", z=64)
                cpev = b.VCOPY(dst, src, inc="sv")
                b.pt_last_read[pi] = cpev
                vtm_cp = cpev
            b.wait("scalar", b.ckpt("tensor"))
        b.vtm_ev = vtm_cp

    def ph2_attention(self):
        b = self
        xln = b.RA
        atm = b.AB(65.0, 16384)
        b.atm = atm
        atm3 = atm.rearrange("p (s z) -> p s z", z=1024)
        qg = b.AB(97.0, 2048)
        kg = b.AB(101.0, 2048)
        pex = [b.AB(105.0, 1024), b.AB(107.0, 1024), b.AB(109.0, 1024)]
        ez = b.AB(105.0, 1024)
        dvE = b.AF(111.0, 256)
        dvO = b.AF(112.0, 256)
        cmb = b.AF(113.0, 256)
        tmp4 = b.AF(114.0, 256)
        sqc = b.AB(115.0, 256)
        ssm = b.AF(115.55, 4)
        rst = b.AF(115.65, 4)
        rzE = b.AF(115.75, 4)
        rzO = b.AF(115.85, 4)

        b.wait("tensor", b.vtm_ev)
        b.wait("vector", b.ld["subw"])
        b.use_ring(4)
        prev = None
        for g in range(NCH):
            for which, dst in ((0, qg), (1, kg)):
                si = (2 * g + which) % 4
                sl = b.wq_slots[si]
                b.wait("gpsimd", b.slot_free[si])
                ldev = b.dma("gpsimd", sl.rearrange("p (c n) -> p c n", c=NCH),
                             b.wqkv3[:, :, which * D + g * 128: which * D + (g + 1) * 128])
                b.wait("tensor", ldev)
                for tt in range(4):
                    bk = tt % 2
                    b.use_bank(bk)
                    for c in range(NCH):
                        b.MM(b.bank(bk), sl[:, c * 128:(c + 1) * 128],
                             xln[c][:, tt * 512:(tt + 1) * 512], c == 0, c == NCH - 1,
                             inc=("st" if c == NCH - 1 else None))
                    mm_ev = b.tlast
                    b.wait("vector", mm_ev)
                    ev = b.VCOPY(dst[:, tt * 512:(tt + 1) * 512], b.bank(bk), inc="sv")
                    b.read_bank(bk, ev)
                b.slot_free[si] = b.ckpt("tensor", "st")
            qk_ev = ("sv", b.ctr["sv"])
            b.wait("tensor", qk_ev)

            for jl in range(2):
                h = 2 * g + jl
                for qt in range(4):
                    cur = {"pv": {}, "exp": {}}

                    def emit_pv(kc):
                        b.wait("tensor", cur["exp"][kc])
                        if kc == 0 and prev is not None:
                            b.wait("tensor", prev["evac"])
                        pvs = []
                        for half in range(2):
                            pvs.append(b.MM(
                                b.bank(4 + half)[0:65, :],
                                b.vtm[:, kc * 1056 + h * 66: kc * 1056 + h * 66 + 65],
                                pex[kc % 3][:, half * 512:(half + 1) * 512],
                                kc == 0, kc == 15, inc="st"))
                        cur["pv"][kc] = pvs

                    for kc in range(16):
                        st_ = kc % 2
                        bk = 2 * st_
                        if kc >= 2:
                            b.wait("tensor", cur["exp"][kc - 2])
                        elif prev is not None:
                            b.wait("tensor", prev["exp"][14 + st_])
                        pack_evs = []
                        for i in range(2):
                            strip = 64 * jl + 32 * i
                            pack_evs.append(b.MM(
                                b.bank(bk + i, 0, 512),
                                kg[strip:strip + 32, kc * 128:(kc + 1) * 128],
                                qg[strip:strip + 32, qt * 512:(qt + 1) * 512],
                                True, True, tp=(strip, 0), inc="st"))
                        for pe_ in pack_evs:
                            b.wait("scalar", pe_)
                        if kc >= 3:
                            for pe_ in cur["pv"][kc - 3]:
                                b.wait("scalar", pe_)
                        elif prev is not None:
                            for pe_ in prev["pv"][{0: 15, 1: 13, 2: 14}[kc]]:
                                b.wait("scalar", pe_)
                        if kc == 0 and prev is not None:
                            b.wait("scalar", prev["trev"])
                        b.ACT(pex[kc % 3], b.ps[:, bk * 512:(bk + 2) * 512], AF.Exp,
                              scale=SCALING, inc="sc")
                        cur["exp"][kc] = ("sc", b.ctr["sc"])
                        if kc >= 2:
                            emit_pv(kc - 2)
                    emit_pv(14)
                    emit_pv(15)
                    for pe_ in cur["pv"][15]:
                        b.wait("vector", pe_)
                    b.VCOPY(ez[0:65, 0:512], b.bank(4)[0:65, :])
                    b.VCOPY(ez[0:65, 512:1024], b.bank(5)[0:65, :], inc="sv")
                    cur["evac"] = ("sv", b.ctr["sv"])
                    b.wait("tensor", cur["evac"])
                    if prev is not None:
                        b.wait("tensor", prev["div"])
                    for s in range(8):
                        half, s4 = s // 4, s % 4
                        dstp = b.pt[:, half * 1024 + s4 * 66: half * 1024 + s4 * 66 + 65]
                        b.TRANS(dstp, ez[0:65, s * 128:(s + 1) * 128],
                                b.identb[0:65, 0:65],
                                inc="st")
                    tr_ev = b.ckpt("tensor")
                    cur["trev"] = tr_ev
                    b.wait("vector", tr_ev)
                    ptE = b.pt[:, 0:264].rearrange("p (s z) -> p s z", z=66)
                    ptO = b.pt[:, 1024:1288].rearrange("p (s z) -> p s z", z=66)
                    b.RECIP(rzE, ptE[:, :, 64:65])
                    b.RECIP(rzO, ptO[:, :, 64:65])
                    b.TT(dvE.rearrange("p (s z) -> p s z", z=64), ptE[:, :, 0:64],
                         rzE.rearrange("p (s u) -> p s u", u=1).broadcast_to([128, 4, 64]),
                         OP.mult)
                    b.TT(dvO.rearrange("p (s z) -> p s z", z=64), ptO[:, :, 0:64],
                         rzO.rearrange("p (s u) -> p s u", u=1).broadcast_to([128, 4, 64]),
                         OP.mult, inc="sv")
                    cur["div"] = ("sv", b.ctr["sv"])
                    b.STT(cmb, dvO, b.mcol(37), dvE, OP.mult, OP.add, inc="sv")
                    b.TT(sqc, cmb, cmb, OP.mult)
                    b.RED(ssm.rearrange("p (s u) -> p s u", u=1),
                          sqc.rearrange("p (s z) -> p s z", z=64), OP.add, inc="sv")
                    b.wait("scalar", ("sv", b.ctr["sv"]))
                    b.ACT(rst, ssm, AF.Ln, scale=1.0 / 64.0, bias=1e-5)
                    b.ACT(rst, rst, AF.Exp, scale=-0.5,
                          bias=math.log(1.0 - LAMBDA_INIT), inc="sc")
                    b.wait("vector", ("sc", b.ctr["sc"]))
                    b.TT(tmp4.rearrange("p (s z) -> p s z", z=64),
                         cmb.rearrange("p (s z) -> p s z", z=64),
                         rst.rearrange("p (s u) -> p s u", u=1).broadcast_to([128, 4, 64]),
                         OP.mult)
                    b.TT(atm3[:, qt * 4:(qt + 1) * 4, h * 64:h * 64 + 64],
                         tmp4.rearrange("p (s z) -> p s z", z=64),
                         b.subw_s.rearrange("p (s z) -> p s z", z=64), OP.mult, inc="sv")
                    prev = cur
        b.wait("tensor", prev["div"])
        b.ckpt("tensor")
        b.use_ring(1)
        b.ckpt("vector", "sv")
        b.atm_ev = ("sv", b.ctr["sv"])
        b.read_bank(4, prev["evac"] if False else None)
        b.barrier()

    def ph3_attnout(self):
        b = self
        atm = b.atm
        afm = [b.AB(32.0 + c * 4.0, T) for c in range(NCH)]
        for fc in range(NCH):
            for tc in range(16):
                pslot, pi = b.pt_slot()
                b.wait("tensor", b.pt_last_read.get(pi))
                b.TRANS(pslot, atm[:, tc * 1024 + fc * 128: tc * 1024 + (fc + 1) * 128],
                        b.identb, inc="st")
                b.wait("vector", b.tlast)
                b.pt_last_read[pi] = b.VCOPY(afm[fc][:, tc * 128:(tc + 1) * 128], pslot, inc="sv")
        b.ckpt("vector", "sv")
        afm_ev = ("sv", b.ctr["sv"])
        b.wait("tensor", afm_ev)
        wo3 = b.woT.rearrange("(c p) n -> p c n", p=128)

        def resid(oc, tt, bk_ap, mm_ev):
            b.wait("vector", mm_ev)
            return b.TT(b.xc(oc, tt * 512, (tt + 1) * 512), bk_ap,
                        b.xc(oc, tt * 512, (tt + 1) * 512), OP.add, inc="sv")

        b.proj_fm(wo3, 0, resid, afm, b.wq_slots, b.slot_free, NCH)
        b.barrier()

    def ph4_moe(self):
        b = self
        zn = b.RA
        slots = [b.AB(32.0, 8192), b.AB(48.0, 8192), b.AB(64.0, 8192)]
        G = b.AB(80.0, 16384)
        cbc = b.AB(112.0, 2048)
        tmp = b.AF(96.5, 512)
        cmbf = b.AF(123.0, 128)
        cmbb = b.AB(123.66, 128)
        crows = b.AB(123.9, 128)
        m4 = b.AF(116.5, 64)
        e4 = b.AF(116.75, 64)
        mx16 = b.AF(117.0, 16)
        ms16 = b.AF(117.1, 16)
        m1t = b.AF(117.2, 16)
        m2t = b.AF(117.3, 16)
        dnv = b.AF(117.4, 16)
        mi8 = b.AF(117.5, 128)
        pm4 = b.AF(118.0, 64)
        iE = b.AF(118.25, 64)
        iO = b.AF(118.5, 64)
        gw4 = b.AF(118.75, 64)
        wmm8 = b.AB(122.5, 128)

        ph4_gate = b.gate_t
        b.wait("sync", ("sv", b.ctr["sv"]))
        spill_ev = b.dma("sync", b.xspill.rearrange("(c p) t -> p c t", p=128),
                         b.xs_a.rearrange("p (c t) -> p c t", c=NCH))
        b.wait("vector", spill_ev)
        b.wait("gpsimd", ph4_gate)
        ld_wmm = b.dma("gpsimd", wmm8.rearrange("p (c n) -> p c n", c=NCH),
                       b.wmmT.rearrange("(c p) n -> p c n", p=128))

        def fin_ln2(tt, lo, mu, r, mur):
            for c in range(NCH):
                b.TT(tmp, b.xc(c, lo, lo + 512), r, OP.mult)
                b.TT(b.xc(c, lo, lo + 512), tmp, mur, OP.subtract)

        b.ln_stats([b.xc(c) for c in range(NCH)], sqkb=80.0, eps=1e-5,
                   fin=fin_ln2, base=116.5)
        b.barrier()

        w1r = b.w1T.rearrange("e (c p) f -> e p c f", p=128)
        w2r = b.w2T.rearrange("e (j p) d -> e p j d", p=128)
        loads = []
        for l in range(2):
            for e in range(8):
                loads.append((1, e, 0))
                loads.append((1, e, 1))
                loads.append((2, e, 0))
                loads.append((2, e, 1))
        load_evs = [None] * len(loads)
        seg_done = [None] * len(loads)
        wr_evs = [None] * 32
        emitted = [0]

        def pump(upto):
            while emitted[0] < min(upto, len(loads)):
                k = emitted[0]
                kind, e, hf = loads[k]
                l1 = k >= 32
                eng = ("sync" if k % 2 == 0 else "gpsimd") if l1 else "gpsimd"
                if k >= 3:
                    b.wait(eng, seg_done[k - 3])
                    if k - 3 < 32:
                        b.wait(eng, wr_evs[k - 3])
                else:
                    b.wait(eng, ph4_gate)
                si = k % 3
                scr = b.w1scr if kind == 1 else b.w2scr
                seg = (2 * e + hf) * 128
                if l1:
                    b.wait(eng, wr_evs[k - 32])
                    load_evs[k] = b.dma(eng, slots[si], scr[seg:seg + 128, :])
                elif kind == 1:
                    load_evs[k] = b.dma(eng, slots[si].rearrange("p (c f) -> p c f", c=NCH),
                                        w1r[e][:, :, hf * 1024:(hf + 1) * 1024])
                else:
                    load_evs[k] = b.dma(eng, slots[si].rearrange("p (j d) -> p j d", j=8),
                                        w2r[e][:, hf * 8:(hf + 1) * 8, :])
                if not l1:
                    b.wait("sync", load_evs[k])
                    wr_evs[k] = b.dma("sync", scr[seg:seg + 128, :], slots[si])
                emitted[0] += 1

        pump(3)
        kseg = 0

        for l in range(2):
            def fin_rms(tt, lo, mu, r, mur):
                for c in range(NCH):
                    b.STT(zn[c][:, lo:lo + 512], b.xc(c, lo, lo + 512),
                          b.colv("normw", c), r, OP.mult, OP.mult)

            b.ln_stats([b.xc(c) for c in range(NCH)], sqkb=104.5,
                       eps=1e-6, fin=fin_rms, rms=True, base=116.5)
            b.ckpt("vector", "sv")
            zn_ev = ("sv", b.ctr["sv"])
            b.wait("tensor", zn_ev)
            b.wait("scalar", zn_ev)

            b.wait("tensor", ld_wmm)
            b.use_bank(0)
            for tc in range(16):
                for c in range(NCH):
                    b.MM(b.bank(0)[:, tc * 12:tc * 12 + 12],
                         zn[c][:, tc * 128:(tc + 1) * 128],
                         wmm8[:, c * 16:c * 16 + 12], c == 0, c == NCH - 1,
                         inc=("st" if (c == NCH - 1 and tc == 15) else None))
            lg_ev = b.ckpt("tensor")
            b.wait("vector", lg_ev)
            L = b.bank(0)[:, 0:192].rearrange("p (tc n) -> p tc n", n=12)
            mac = L[:, :, 0:4]
            mic = L[:, :, 4:12]
            b.RED(mx16.rearrange("p (s u) -> p s u", u=1), mac, OP.max)
            b.TT(m4.rearrange("p (s z) -> p s z", z=4), mac,
                 mx16.rearrange("p (s u) -> p s u", u=1).broadcast_to([128, 16, 4]),
                 OP.subtract)
            b.VCOPY(mi8, mic, inc="sv")
            b.wait("scalar", ("sv", b.ctr["sv"]))
            b.ACT(m4, m4, AF.Exp, inc="sc")
            b.wait("vector", ("sc", b.ctr["sc"]))
            m43 = m4.rearrange("p (s z) -> p s z", z=4)
            b.RED(ms16.rearrange("p (s u) -> p s u", u=1), m43, OP.add)
            b.RED(m1t.rearrange("p (s u) -> p s u", u=1), m43, OP.max)
            b.TT(e4.rearrange("p (s z) -> p s z", z=4), m43,
                 m1t.rearrange("p (s u) -> p s u", u=1).broadcast_to([128, 16, 4]),
                 OP.is_ge)
            b.STT(e4, e4, -1e30, m4, OP.mult, OP.add)
            b.RED(m2t.rearrange("p (s u) -> p s u", u=1),
                  e4.rearrange("p (s z) -> p s z", z=4), OP.max)
            b.TT(dnv, m1t, m2t, OP.add)
            b.STT(dnv, ms16, 1e-9, dnv, OP.mult, OP.add)
            b.RECIP(dnv, dnv)
            b.TT(iE.rearrange("p (s z) -> p s z", z=4), m43,
                 m2t.rearrange("p (s u) -> p s u", u=1).broadcast_to([128, 16, 4]),
                 OP.is_ge)
            b.TT(gw4, m4, iE, OP.mult)
            b.TT(gw4.rearrange("p (s z) -> p s z", z=4),
                 gw4.rearrange("p (s z) -> p s z", z=4),
                 dnv.rearrange("p (s u) -> p s u", u=1).broadcast_to([128, 16, 4]),
                 OP.mult)
            mi4 = mi8.rearrange("p (s g j) -> p s g j", g=4, j=2)
            b.TT(pm4.rearrange("p (s g) -> p s g", g=4), mi4[:, :, :, 0],
                 mi4[:, :, :, 1], OP.max)
            pmB = pm4.rearrange("p (s g) -> p s g", g=4)
            b.TT(iE.rearrange("p (s g) -> p s g", g=4), mi4[:, :, :, 0], pmB, OP.is_ge)
            b.TT(iO.rearrange("p (s g) -> p s g", g=4), mi4[:, :, :, 1], pmB, OP.is_ge)
            cm4 = cmbf.rearrange("p (s g j) -> p s g j", g=4, j=2)
            b.TT(cm4[:, :, :, 0], iE.rearrange("p (s g) -> p s g", g=4),
                 gw4.rearrange("p (s g) -> p s g", g=4), OP.mult)
            ev = b.TT(cm4[:, :, :, 1], iO.rearrange("p (s g) -> p s g", g=4),
                      gw4.rearrange("p (s g) -> p s g", g=4), OP.mult, inc="sv")
            b.read_bank(0, ev)
            b.VCOPY(cmbb, cmbf, inc="sv")
            cmb_ev = ("sv", b.ctr["sv"])
            b.wait("tensor", cmb_ev)
            if True:
                pslot, pi = b.pt_slot()
                b.wait("tensor", b.pt_last_read.get(pi))
                b.wait("tensor", b.pt_last_read.get((pi + 2) % 4))
                b.TRANS(pslot, cmbb, b.identb, inc="st")
                b.wait("vector", b.tlast)
                crows_ev = b.VCOPY(crows, pslot, inc="sv")
                b.pt_last_read[pi] = crows_ev
                b.wait("sync", crows_ev)
                crd_ev = b.dma("sync", b.crow_d.rearrange("tc e j -> (tc e) j"), crows)

            c8 = b.AF(80.0, 2048)[0:8, :]
            mask8 = b.AF(88.0, 2048)[0:8, :]
            pos8 = b.AF(96.0, 2048)[0:8, :]
            ones8 = b.AF(104.0, 2048)[0:8, :]
            first8 = ones8
            s12r = b.AF(112.0, 2048)[0:1, :]
            s1r = b.AF(80.0, 2048)[0:1, :]
            s2r = b.AF(88.0, 2048)[0:1, :]
            ltri_s = b.AF(127.65, 16)
            s1f = b.AF(124.0, 16)
            s2f = b.AF(124.0625, 16)
            c1f = b.AF(124.125, 16)
            c2f = b.AF(124.1875, 16)
            s1u = s1f.bitcast(mybir.dt.uint32)
            s2u = s2f.bitcast(mybir.dt.uint32)
            tmA = b.AF(118.5, 128)
            tmB = b.AF(119.0, 128)
            tmC = b.AF(119.5, 128)

            if True:
                b.wait("gpsimd", crd_ev)
                c8ld = b.dma("gpsimd", c8.rearrange("e (tc j) -> e tc j", j=128),
                             b.crow_d.rearrange("tc e j -> e tc j"))
                b.wait("vector", c8ld)
                b.wait("vector", b.ld["ltri"])
                b.MEMSET(ones8, 1.0)
                b.TS(mask8, c8, 0.0, None, OP.is_gt)
                b.SCAN(pos8, ones8, mask8, inc="sv")
                b.wait("tensor", ("sv", b.ctr["sv"]))
                b.wait("tensor", b.ld["identf"])
                for tt in range(4):
                    b.use_bank(1)
                    sev = b.MM(b.bank(1)[0:8, :], ltri_s[0:8, 0:8],
                               mask8[:, tt * 512:(tt + 1) * 512], True, True, inc="st")
                    b.wait("vector", sev)
                    ev = b.TS(first8[:, tt * 512:(tt + 1) * 512], b.bank(1)[0:8, :],
                              0.5, None, OP.is_lt, inc="sv")
                    b.read_bank(1, ev)
                b.TT(first8, first8, mask8, OP.mult)
                b.TS(c8, pos8, 1.0, ltri_s[0:8, 8:9], OP.subtract, OP.add)
                b.TT(c8, c8, mask8, OP.mult)
                b.TT(mask8, c8, first8, OP.mult, inc="sv")
                b.wait("tensor", ("sv", b.ctr["sv"]))
                for tt in range(4):
                    b.use_bank(2)
                    e1 = b.MM(b.bank(2)[0:1, :], b.onesf[0:8, 0:1],
                              c8[:, tt * 512:(tt + 1) * 512], True, True, inc="st")
                    b.use_bank(3)
                    e2 = b.MM(b.bank(3)[0:1, :], b.onesf[0:8, 0:1],
                              mask8[:, tt * 512:(tt + 1) * 512], True, True, inc="st")
                    b.wait("vector", e1)
                    b.wait("vector", e2)
                    ev1 = b.TS(s12r[:, tt * 512:(tt + 1) * 512], b.bank(2)[0:1, :],
                               1.0, None, OP.mult, inc="sv")
                    ev2 = b.TS(s1r[:, tt * 512:(tt + 1) * 512], b.bank(3)[0:1, :],
                               1.0, None, OP.mult, inc="sv")
                    b.read_bank(2, ev1)
                    b.read_bank(3, ev2)
                b.TT(s2r, s12r, s1r, OP.subtract, inc="sv")
                b.wait("tensor", ("sv", b.ctr["sv"]))
                for ri, row in ((0, s1r), (1, s2r)):
                    dstf = s1f if ri == 0 else s2f
                    bk = 4 + ri
                    b.use_bank(bk)
                    for tc in range(16):
                        b.MM(b.bank(bk)[:, tc:tc + 1], row[:, tc * 128:(tc + 1) * 128],
                             b.onesf[0:1, 0:1], True, True, inc="st")
                    tre = b.tlast
                    b.wait("vector", tre)
                    ev = b.VCOPY(dstf, b.bank(bk)[:, 0:16], inc="sv")
                    b.read_bank(bk, ev)
                b.VCOPY(s1u, s1f)
                b.VCOPY(s2u, s2f)
                cm3 = cmbf.rearrange("p (s e) -> p s e", e=8)
                tA = tmA.rearrange("p (s e) -> p s e", e=8)
                tB = tmB.rearrange("p (s e) -> p s e", e=8)
                tC = tmC.rearrange("p (s e) -> p s e", e=8)
                b.TS(tmA.rearrange("p (s e) -> p s e", e=8), cm3, 0.0, None, OP.is_gt)
                b.VCOPY(tmB, tmA)
                for k in (1, 2, 4):
                    b.VCOPY(tmC, tmB)
                    b.TT(tB[:, :, k:8], tC[:, :, k:8], tC[:, :, 0:8 - k], OP.add)
                b.TT(tmB, tmB, tmA, OP.subtract)
                b.TS(tmB, tmB, 0.5, None, OP.is_lt)
                b.TT(tmA, tmA, tmB, OP.mult)
                b.TT(tB, cm3, tA, OP.mult)
                b.RED(c1f.rearrange("p (s u) -> p s u", u=1), tB, OP.add)
                b.RED(c2f.rearrange("p (s u) -> p s u", u=1), cm3, OP.add)
                b.TT(c2f, c2f, c1f, OP.subtract, inc="sv")
                idx_ev = ("sv", b.ctr["sv"])

            ztlA = b.AB(116.0, 1024)
            ztlB = b.AB(114.0, 1024)
            scat_evs = []
            ztl_free = [(), ()]

            def igather(out_ap, in_ap, idx_ap, gate_evs=()):
                i = b.dma_ring["gpsimd"]
                b.dma_ring["gpsimd"] = (i + 1) % 8
                sm = "sg%d" % i
                if sm in b.dma_last:
                    b.wait("gpsimd", (sm, b.dma_last[sm]))
                for gev in gate_evs:
                    b.wait("gpsimd", gev)
                def mk(e, out=out_ap, in_=in_ap, idx=idx_ap):
                    if idx[1] == "out":
                        return e.indirect_dma_start(
                            out=out, out_offset=bass.IndirectOffsetOnAxis(idx[0], 0),
                            in_=in_, in_offset=None)
                    return e.indirect_dma_start(
                        out=out, out_offset=None,
                        in_=in_, in_offset=bass.IndirectOffsetOnAxis(idx[0], 0))
                ev = b.emit("gpsimd", mk, inc=sm, n=16)
                b.dma_last[sm] = ev[1]
                return ev

            if True:
                for tc in range(16):
                    ztl = (ztlA, ztlB)[tc % 2]
                    evac_evs = []
                    for c in range(NCH):
                        pslot, pi = b.pt_slot()
                        b.wait("tensor", b.pt_last_read.get(pi))
                        b.wait("tensor", b.pt_last_read.get((pi + 2) % 4))
                        if c == 0:
                            for fe in ztl_free[tc % 2]:
                                b.wait("tensor", fe)
                        b.TRANS(pslot, zn[c][:, tc * 128:(tc + 1) * 128], b.identb, inc="st")
                        tre = b.tlast
                        eng = "vector" if c % 2 == 0 else "scalar"
                        CP = b.VCOPY if eng == "vector" else b.SCOPY
                        sem = "sv" if eng == "vector" else "sc"
                        b.wait(eng, tre)
                        for fe in ztl_free[tc % 2]:
                            b.wait(eng, fe)
                        ev = CP(ztl[:, c * 128:(c + 1) * 128], pslot, inc=sem)
                        b.pt_last_read[pi] = ev
                        evac_evs.append(ev)
                    ev1 = igather(b.zslots, ztl, (s1u[:, tc:tc + 1], "out"),
                                  gate_evs=list(evac_evs) + [idx_ev])
                    ev2 = igather(b.zslots, ztl, (s2u[:, tc:tc + 1], "out"),
                                  gate_evs=[idx_ev])
                    scat_evs += [ev1, ev2]
                    ztl_free[tc % 2] = (ev1, ev2)

            zge = b.AB(104.0, 5120)
            G = b.AB(80.0, 10240)
            ysb = [b.AB(118.0, 1024), b.AB(120.0, 1024)]
            ywr_evs = []
            ysb_busy = [None, None]
            zld_free = [(), ()]
            last_y_ckpt = None
            if True:
                for e in range(8):
                    for k in range(5):
                        ztl = (ztlA, ztlB)[k % 2]
                        if e == 0 and k < 2:
                            for sev_ in scat_evs:
                                b.wait("gpsimd", sev_)
                        for fe in zld_free[k % 2]:
                            b.wait("gpsimd", fe)
                        if e == 0:
                            for fe in ztl_free[k % 2]:
                                b.wait("gpsimd", fe)
                        zl = b.dma("gpsimd", ztl,
                                   b.zslots[e * 656 + k * 128: e * 656 + (k + 1) * 128, :])
                        b.wait("tensor", zl)
                        kevs = []
                        for c in range(NCH):
                            pslot, pi = b.pt_slot()
                            b.wait("tensor", b.pt_last_read.get(pi))
                            b.wait("tensor", b.pt_last_read.get((pi + 2) % 4))
                            b.TRANS(pslot, ztl[:, c * 128:(c + 1) * 128], b.identb, inc="st")
                            tre = b.tlast
                            eng = "vector" if c % 2 == 0 else "scalar"
                            CP = b.VCOPY if eng == "vector" else b.SCOPY
                            sem = "sv" if eng == "vector" else "sc"
                            b.wait(eng, tre)
                            dst = zge[:, c * 640 + k * 128: c * 640 + (k + 1) * 128]
                            ev = CP(dst, pslot, inc=sem)
                            b.pt_last_read[pi] = ev
                            kevs.append(ev)
                        zld_free[k % 2] = tuple(kevs)
                    zge_evs = zld_free[0] + zld_free[1]

                    for hf in range(2):
                        kH = kseg; kseg += 1
                        pump(kH + 3)
                        sl = slots[kH % 3]
                        b.wait("tensor", load_evs[kH])
                        for gev in zge_evs:
                            b.wait("tensor", gev)
                        if last_y_ckpt is not None:
                            b.wait("scalar", last_y_ckpt)
                        for fl in range(8):
                            f = hf * 8 + fl
                            for nt in range(2):
                                nlo, nsz = (0, 512) if nt == 0 else (512, 128)
                                bk = (fl * 2 + nt) % 2
                                b.use_bank(bk)
                                for c in range(NCH):
                                    b.MM(b.bank(bk, 0, nsz),
                                         sl[:, c * 1024 + fl * 128: c * 1024 + (fl + 1) * 128],
                                         zge[:, c * 640 + nlo: c * 640 + nlo + nsz],
                                         c == 0, c == NCH - 1,
                                         inc=("st" if c == NCH - 1 else None))
                                mm_ev = b.tlast
                                b.wait("scalar", mm_ev)
                                gsl = G[:, f * 640 + nlo: f * 640 + nlo + nsz]
                                silu_ev = b.ACT(gsl, b.bank(bk, 0, nsz), AF.Silu, inc="sc")
                                b.read_bank(bk, silu_ev)
                        seg_done[kH] = b.ckpt("tensor", "st")
                    kY0 = kseg; kseg += 1
                    kY1 = kseg; kseg += 1
                    pump(kY0 + 3)
                    slA = slots[kY0 % 3]
                    slB = slots[kY1 % 3]
                    b.wait("tensor", load_evs[kY0])
                    b.wait("tensor", load_evs[kY1])
                    b.wait("tensor", ("sc", b.ctr["sc"]))
                    for st6 in range(5):
                        ypar = st6 % 2
                        yevs = []
                        for dn in range(2):
                            bk = 2 + (st6 * 2 + dn) % 4
                            b.use_bank(bk)
                            for f in range(16):
                                w2sl = slA if f < 8 else slB
                                fl = f % 8
                                b.MM(b.bank(bk),
                                     G[:, f * 640 + st6 * 128: f * 640 + (st6 + 1) * 128],
                                     w2sl[:, fl * 1024 + dn * 512: fl * 1024 + (dn + 1) * 512],
                                     f == 0, f == 15,
                                     inc=("st" if f == 15 else None))
                            mm_ev = b.tlast
                            eng = "vector" if dn == 0 else "scalar"
                            CP = b.VCOPY if eng == "vector" else b.SCOPY
                            sem = "sv" if eng == "vector" else "sc"
                            b.wait(eng, mm_ev)
                            if ysb_busy[ypar] is not None:
                                b.wait(eng, ysb_busy[ypar])
                            ev = CP(ysb[ypar][:, dn * 512:(dn + 1) * 512], b.bank(bk),
                                    inc=sem)
                            b.read_bank(bk, ev)
                            yevs.append(ev)
                        for yev_ in yevs:
                            b.wait("gpsimd", yev_)
                        wev = b.dma(
                            "gpsimd",
                            b.yslots[e * 656 + st6 * 128: e * 656 + (st6 + 1) * 128, :],
                            ysb[ypar])
                        ywr_evs.append(wev)
                        ysb_busy[ypar] = wev
                    last_y_ckpt = b.ckpt("tensor", "st")
                    seg_done[kY0] = seg_done[kY1] = last_y_ckpt
                    pump(kY1 + 3)

            gbuf = [(b.AB(80.0, 1024), b.AB(84.0, 1024)),
                    (b.AB(104.0, 1024), b.AB(108.0, 1024))]
            zd = [b.AF(88.0, 1024), b.AF(96.0, 1024)]
            zdb = [b.AB(92.0, 1024), b.AB(112.0, 1024)]
            zadd_done = [None, None]
            if True:
                def issue_gather(tc):
                    par = tc % 2
                    gbufA, gbufB = gbuf[par]
                    gevs = []
                    for sidx, gb in ((s1u, gbufA), (s2u, gbufB)):
                        gate = list(ywr_evs) if tc == 0 else []
                        if zadd_done[par] is not None:
                            gate.append(zadd_done[par])
                        gevs.append(igather(gb, b.yslots, (sidx[:, tc:tc + 1], "in"),
                                            gate_evs=gate))
                    return gevs

                pend = {0: issue_gather(0)}
                ywr_evs = []
                pend[1] = issue_gather(1)
                for tc in range(16):
                    par = tc % 2
                    gbufA, gbufB = gbuf[par]
                    for gev in pend[tc]:
                        b.wait("vector", gev)
                    b.TS(zd[par], gbufA, c1f[:, tc:tc + 1], None, OP.mult)
                    b.STT(zd[par], gbufB, c2f[:, tc:tc + 1], zd[par], OP.mult, OP.add)
                    b.VCOPY(zdb[par], zd[par], inc="sv")
                    zdb_ev = ("sv", b.ctr["sv"])
                    b.wait("tensor", zdb_ev)
                    for c in range(NCH):
                        pslot, pi = b.pt_slot()
                        b.wait("tensor", b.pt_last_read.get(pi))
                        b.wait("tensor", b.pt_last_read.get((pi + 2) % 4))
                        b.TRANS(pslot, zdb[par][:, c * 128:(c + 1) * 128], b.identb,
                                inc="st")
                        tre = b.tlast
                        b.wait("vector", tre)
                        ev = b.TT(b.xc(c, tc * 128, (tc + 1) * 128), pslot,
                                  b.xc(c, tc * 128, (tc + 1) * 128), OP.add, inc="sv")
                        b.pt_last_read[pi] = ev
                    zadd_done[par] = ("sv", b.ctr["sv"])
                    if tc + 2 < 16:
                        pend[tc + 2] = issue_gather(tc + 2)
            b.barrier()
            b.barrier()

        stage = b.arf[:, int(32.0 * 256):int(32.0 * 256) + 8192]
        b.wait("sync", seg_done[-1])
        for wave in range(2):
            rev = b.dma("sync", stage.rearrange("p (c t) -> p c t", c=4),
                        b.xspill.rearrange("(c p) t -> p c t", p=128)[:, wave * 4:(wave + 1) * 4, :])
            b.wait("vector", rev)
            for c4 in range(4):
                c = wave * 4 + c4
                b.TT(b.xc(c), stage[:, c4 * 2048:(c4 + 1) * 2048], b.xc(c), OP.add, inc="sv")
            if wave == 0:
                b.wait("sync", ("sv", b.ctr["sv"]))
        b.barrier()


    def ph5_rwkv(self):
        b = self
        xln3 = b.RA
        xsh = [b.AF(32.0 + c * 8.0, 512 * 4) for c in range(NCH)]
        tmp = b.AF(96.5, 512)
        tmp2 = b.AF(112.5, 512)
        wsh = b.AB(98.5, 8192)
        tmp2 = b.AF(114.5, 512)
        b.wait("gpsimd", b.gate_t)
        ld_wsh = b.dma("gpsimd", wsh.rearrange("p (c n) -> p c n", c=NCH),
                       b.wshiftT.rearrange("(c p) n -> p c n", p=128))

        def fin_ln3(tt, lo, mu, r, mur):
            for c in range(NCH):
                b.TT(tmp, b.xc(c, lo, lo + 512), r, OP.mult)
                b.TT(xln3[c][:, lo:lo + 512], tmp, mur, OP.subtract)

        b.ln_stats([b.xc(c) for c in range(NCH)], sqkb=64.5, eps=1e-5,
                   fin=fin_ln3, base=116.5)
        b.ckpt("vector", "sv")
        xln3_ev = ("sv", b.ctr["sv"])
        b.wait("tensor", xln3_ev)
        b.wait("tensor", ld_wsh)

        for oc in range(NCH):
            for tt in range(4):
                bk = tt % 2
                b.use_bank(bk)
                rot = (tt * 512 + 1024) % 2048
                for c in range(NCH):
                    b.MM(b.bank(bk), wsh[:, c * 1024 + oc * 128: c * 1024 + (oc + 1) * 128],
                         xln3[c][:, rot:rot + 512], c == 0, c == NCH - 1,
                         inc=("st" if c == NCH - 1 else None))
                mm_ev = b.tlast
                b.wait("vector", mm_ev)
                b.TS(tmp2, xln3[oc][:, tt * 512:(tt + 1) * 512],
                     b.GC[:, NCH + oc:NCH + oc + 1], None, OP.mult)
                ev = b.STT(xsh[oc][:, tt * 512:(tt + 1) * 512], b.bank(bk),
                           b.GC[:, oc:oc + 1], tmp2, OP.mult, OP.add, inc="sv")
                b.read_bank(bk, ev)
        b.ckpt("vector", "sv")
        xsh_ev = ("sv", b.ctr["sv"])
        b.wait("scalar", xsh_ev)
        b.wait("tensor", xsh_ev)

        def fin_lni(tt, lo, mu, r, mur):
            for c in range(NCH):
                b.TT(tmp, xsh[c][:, lo:lo + 512], r, OP.mult)
                b.TT(tmp2, tmp, mur, OP.subtract)
                b.TS(xln3[c][:, lo:lo + 512], tmp2, b.colv("lnw", c),
                     b.colv("lnb", c), OP.mult, OP.add)

        b.ln_stats(xsh, sqkb=104.5, eps=1e-5, fin=fin_lni, base=116.5)
        b.ckpt("vector", "sv")
        xn_ev = ("sv", b.ctr["sv"])
        b.wait("tensor", xn_ev)

        shift_done = b.ckpt("tensor", "st")
        wkv_s = b.AB(122.0, 1024)
        b.wait("gpsimd", shift_done)
        ld_wkv = b.dma("gpsimd", wkv_s.rearrange("p (c n) -> p c n", c=NCH),
                       b.wkvT.rearrange("(c p) n -> p c n", p=128))
        wout_s = b.AB(100.5, 1024)
        ld_wout = b.dma("gpsimd", wout_s[0:64, :], b.woutT)
        wkvsb = b.AF(104.5, 2048)
        ssg = b.AF(112.5, 512)
        b.wait("tensor", ld_wkv)
        for tt in range(4):
            bk = 2 + tt % 2
            b.use_bank(bk)
            for c in range(NCH):
                b.MM(b.bank(bk), wkv_s[:, c * 128:(c + 1) * 128],
                     xln3[c][:, tt * 512:(tt + 1) * 512], c == 0, c == NCH - 1,
                     inc=("st" if c == NCH - 1 else None))
            mm_ev = b.tlast
            b.wait("scalar", mm_ev)
            b.ACT(ssg[0:64, :], b.bank(bk)[0:64, :], AF.Sigmoid)
            b.ACT(ssg[0:64, :], ssg[0:64, :], AF.Exp, scale=b.mcol(43)[0:64, :], inc="sc")
            b.wait("vector", ("sc", b.ctr["sc"]))
            ev = b.TT(wkvsb[0:64, tt * 512:(tt + 1) * 512], ssg[0:64, :],
                      b.bank(bk)[64:128, :], OP.mult, inc="sv")
            b.read_bank(bk, ev)
            b.wait("scalar", ("sv", b.ctr["sv"]))
        states = b.AF(32.0, 2048)
        wrow = b.AF(40.0, 2048)
        statesbf = b.AB(96.5, 2048)
        b.TS(wrow[0:64, :], wkvsb[0:64, :], 0.0, b.mcol(41)[0:64, :],
             OP.mult, OP.add, inc="sv")
        b.SCAN(states[0:64, :], wrow[0:64, :], wkvsb[0:64, :])
        b.VCOPY(statesbf[0:64, :], states[0:64, :], inc="sv")
        st_ev = ("sv", b.ctr["sv"])
        b.wait("tensor", st_ev)
        b.wait("tensor", ld_wout)
        for oc in range(NCH):
            for tt in range(4):
                bk = tt % 2
                b.use_bank(bk)
                b.MM(b.bank(bk), wout_s[0:64, oc * 128:(oc + 1) * 128],
                     statesbf[0:64, tt * 512:(tt + 1) * 512], True, True, inc="st")
                mm_ev = b.tlast
                b.wait("vector", mm_ev)
                ev = b.TT(b.xc(oc, tt * 512, (tt + 1) * 512), b.bank(bk),
                          b.xc(oc, tt * 512, (tt + 1) * 512), OP.add, inc="sv")
                b.read_bank(bk, ev)
        b.barrier()

    def ph6_lffn(self):
        b = self
        xl = b.RA
        mu = b.AF(116.5, 512)
        r = b.AF(118.5, 512)
        mur = b.AF(120.5, 512)
        var = b.AF(114.5, 512)
        r2 = b.AF(110.5, 512)
        mur2 = b.AF(112.5, 512)
        tmp = b.AF(96.5, 512)
        tmp2 = b.AF(98.5, 512)
        srow = b.AF(48.0, 2048)

        b.wait("scalar", b.ld["b1l"])
        b.wait("scalar", b.ld["b1h"])
        for tt in range(4):
            lo = tt * 512
            sq_evs = []
            for c in range(NCH):
                sq = b.AB(64.0 + c * 1.0, 512)
                sq_evs.append((sq, b.ACT(sq, b.xc(c, lo, lo + 512), AF.Square, inc="sc")))
            b.use_bank(0)
            b.use_bank(1)
            sum_ev = None
            for c in range(NCH):
                sum_ev = b.MM(b.bank(0), b.onesf, b.xc(c, lo, lo + 512),
                              c == 0, c == NCH - 1,
                              inc=("st" if c == NCH - 1 else None))
            for c in range(NCH):
                b.wait("tensor", sq_evs[c][1])
                b.MM(b.bank(1), b.onesb, sq_evs[c][0], c == 0, c == NCH - 1,
                     inc=("st" if c == NCH - 1 else None))
            mm_ev = b.tlast
            b.wait("vector", sum_ev)
            b.wait("vector", mm_ev)
            b.TS(mu, b.bank(0), 1.0 / D, None, OP.mult)
            b.TT(var, mu, mu, OP.mult)
            b.TS(tmp, b.bank(1), 1.0 / D, None, OP.mult)
            b.TT(var, tmp, var, OP.subtract, inc="sv")
            var_ev = ("sv", b.ctr["sv"])
            b.wait("scalar", var_ev)
            b.ACT(r, var, AF.Ln, bias=1e-5)
            b.ACT(r, r, AF.Exp, scale=-0.5)
            b.ACT(r2, var, AF.Ln)
            b.ACT(r2, r2, AF.Exp, scale=-0.5, inc="sc")
            r_ev = ("sc", b.ctr["sc"])
            b.wait("vector", r_ev)
            b.TT(mur, mu, r, OP.mult)
            b.TT(mur2, mu, r2, OP.mult, inc="sv")
            bank_free = b.ckpt("vector", "sv")
            b.read_bank(0, bank_free)
            b.read_bank(1, bank_free)
            b.use_bank(2)
            for c in range(NCH):
                b.TT(tmp, b.xc(c, lo, lo + 512), r, OP.mult)
                ev = b.TT(tmp2, tmp, mur, OP.subtract, inc="sv")
                b.wait("tensor", ev)
                b.MM(b.bank(2)[0:1, :], b.colv("rtok", c), tmp2, c == 0, c == NCH - 1,
                     inc="st")
                b.wait("vector", b.tlast)
                b.TT(tmp, b.xc(c, lo, lo + 512), r2, OP.mult)
                b.TT(tmp2, tmp, mur2, OP.subtract)
                b.TS(xl[c][:, lo:lo + 512], tmp2, b.colv("gl", c), None, OP.mult, inc="sv")
            sev = b.tlast
            b.wait("scalar", sev)
            scop = b.SCOPY(srow[0:1, lo:lo + 512], b.bank(2)[0:1, :], inc="sc")
            b.read_bank(2, scop)
            b.wait("vector", ("sc", b.ctr["sc"]))
        b.ckpt("vector", "sv")
        xl_ev = ("sv", b.ctr["sv"])
        b.ckpt("scalar", "sc")
        srow_ev = ("sc", b.ctr["sc"])
        b.wait("sync", srow_ev)

        s_tm = b.AF(123.0, 16)
        b.wait("sync", xl_ev)
        bev = b.dma("sync", b.srow_d, srow[0:1, :])
        b.wait("sync", bev)
        def _stm(e, out=s_tm, in_=b.srow_d.rearrange("u (tc p) -> (u p) tc", p=128)):
            return e.dma_start(out=out, in_=in_, allow_slow_non_contiguous=True)
        i = b.dma_ring["sync"]; b.dma_ring["sync"] = (i + 1) % 8
        _sm = "ss%d" % i
        if _sm in b.dma_last:
            b.wait("sync", (_sm, b.dma_last[_sm]))
        bev2 = b.emit("sync", _stm, inc=_sm, n=16)
        b.dma_last[_sm] = bev2[1]
        b.wait("vector", bev2)

        theta = b.mcol(38)
        step = b.mcol(39)
        dcol = b.mcol(40)
        trial = b.mcol(44)
        pc = b.mcol(45)
        gsc = b.AF(123.2, 16)
        b.RED(pc, s_tm, OP.max, absv=True, inc="sv")
        b.wait("tensor", ("sv", b.ctr["sv"]))
        b.use_bank(3)
        b.MM(b.bank(3)[:, 0:1], b.onesf, pc, True, True, inc="st")
        b.wait("vector", b.tlast)
        b.TS(theta, b.bank(3)[:, 0:1], -1.0, None, OP.mult)
        ev = b.TS(step, b.bank(3)[:, 0:1], 1.0, None, OP.mult, inc="sv")
        b.read_bank(3, ev)
        for it in range(48):
            b.TT(trial, theta, step, OP.add)
            b.TS(gsc, s_tm, trial, None, OP.is_ge)
            b.RED(pc, gsc, OP.add, inc="sv")
            b.wait("tensor", ("sv", b.ctr["sv"]))
            b.use_bank(3)
            b.MM(b.bank(3)[:, 0:1], b.onesf, pc, True, True, inc="st")
            b.wait("vector", b.tlast)
            b.TS(dcol, b.bank(3)[:, 0:1], 1024.0, None, OP.is_ge, inc="sv")
            b.read_bank(3, ("sv", b.ctr["sv"]))
            b.STT(theta, dcol, step, theta, OP.mult, OP.add)
            b.TS(step, step, 0.5, None, OP.mult)
        b.ckpt("vector", "sv")
        th_ev = ("sv", b.ctr["sv"])

        w1l_s = b.AB(96.5, 4096)
        b.wait("gpsimd", b.gate_t)
        b.wait("gpsimd", xl_ev)
        ld_w1l = b.dma("gpsimd", w1l_s.rearrange("p (c n) -> p c n", c=NCH),
                       b.w1l.rearrange("(c p) n -> p c n", p=128))
        w2l_s = b.AB(104.5, 4096)
        ld_w2l = b.dma("gpsimd", w2l_s.rearrange("p (c n) -> p c n", c=4),
                       b.w2l.rearrange("(c p) n -> p c n", p=128))
        Gl = b.AB(64.0, 8192)
        b.wait("tensor", xl_ev)
        b.wait("tensor", ld_w1l)
        for hc in range(4):
            for tt in range(4):
                bk = tt % 2
                b.use_bank(bk)
                for c in range(NCH):
                    b.MM(b.bank(bk), w1l_s[:, c * 512 + hc * 128: c * 512 + (hc + 1) * 128],
                         xl[c][:, tt * 512:(tt + 1) * 512], c == 0, c == NCH - 1,
                         inc=("st" if c == NCH - 1 else None))
                mm_ev = b.tlast
                b.wait("scalar", mm_ev)
                ev = b.ACT(Gl[:, hc * 2048 + tt * 512: hc * 2048 + (tt + 1) * 512],
                           b.bank(bk), AF.Gelu, bias=b.MISC[:, hc:hc + 1], inc="sc")
                b.read_bank(bk, ev)
        w1l_done = b.ckpt("tensor", "st")
        gl_ev = ("sc", b.ctr["sc"])
        b.wait("tensor", gl_ev)
        b.wait("tensor", ld_w2l)
        tmpl = b.AF(121.0, 512)
        for oc in range(NCH):
            for tt in range(4):
                bk = 2 + tt % 2
                b.use_bank(bk)
                for hc in range(4):
                    b.MM(b.bank(bk), w2l_s[:, hc * 1024 + oc * 128: hc * 1024 + (oc + 1) * 128],
                         Gl[:, hc * 2048 + tt * 512: hc * 2048 + (tt + 1) * 512],
                         hc == 0, hc == 3, inc=("st" if hc == 3 else None))
                mm_ev = b.tlast
                b.wait("vector", mm_ev)
                b.TS(tmpl, b.bank(bk), b.colv("b2l", oc), None, OP.add)
                ev = b.TT(b.xc(oc, tt * 512, (tt + 1) * 512), tmpl,
                          b.xc(oc, tt * 512, (tt + 1) * 512), OP.add, inc="sv")
                b.read_bank(bk, ev)
        light_done = b.ckpt("tensor", "st")

        mrow = b.AF(112.5, 2048)
        mrowb = b.AB(120.5, 2048)
        b.wait("vector", th_ev)
        b.TS(mrow[0:1, :], srow[0:1, :], theta[0:1, :], None, OP.is_ge)
        b.VCOPY(mrowb[0:1, :], mrow[0:1, :], inc="sv")
        mrow_ev = ("sv", b.ctr["sv"])
        b.wait("tensor", mrow_ev)
        b.wait("tensor", light_done)
        maskb = b.AF(96.5, 2048)
        for tt in range(4):
            b.use_bank(0)
            mevs = []
            for i in range(4):
                tc = tt * 4 + i
                mevs.append(b.MM(b.bank(0)[:, i * 128:(i + 1) * 128], b.onesb[0:1, :],
                                 mrowb[0:1, tc * 128:(tc + 1) * 128], True, True,
                                 inc="st"))
            for me_ in mevs:
                b.wait("scalar", me_)
            mev = b.ACT(maskb[:, tt * 512:(tt + 1) * 512], b.bank(0), AF.Copy, inc="sc")
            b.read_bank(0, mev)
        mask_ev = ("sc", b.ctr["sc"])
        b.wait("vector", mask_ev)

        slots = [b.AB(32.0, 8192), b.AB(48.0, 8192)]
        slot_free = [b.gate_t, mrow_ev]
        Gh = b.AB(64.0, 16384)
        w1r = b.w1h.rearrange("(c p) n -> p c n", p=128)
        w2r = b.w2h.rearrange("(j p) d -> p j d", p=128)
        b.wait("gpsimd", bev)
        nload = 0
        for g in range(4):
            si = nload % 2
            nload += 1
            b.wait("gpsimd", slot_free[si])
            ldev = b.dma("gpsimd", slots[si].rearrange("p (c n) -> p c n", c=NCH),
                         w1r[:, :, g * 1024:(g + 1) * 1024])
            b.wait("vector", ldev)
            for c in range(NCH):
                b.TS(slots[si][:, c * 1024:(c + 1) * 1024],
                     slots[si][:, c * 1024:(c + 1) * 1024],
                     b.GC[:, 2 * NCH + c:2 * NCH + c + 1], None, OP.mult,
                     inc=("sv" if c == NCH - 1 else None))
            rat_ev = ("sv", b.ctr["sv"])
            b.wait("tensor", rat_ev)
            for hcl in range(8):
                hc = g * 8 + hcl
                for tt in range(4):
                    bk = (hcl * 4 + tt) % 2
                    b.use_bank(bk)
                    for c in range(NCH):
                        b.MM(b.bank(bk),
                             slots[si][:, c * 1024 + hcl * 128: c * 1024 + (hcl + 1) * 128],
                             xl[c][:, tt * 512:(tt + 1) * 512], c == 0, c == NCH - 1,
                             inc=("st" if c == NCH - 1 else None))
                    mm_ev = b.tlast
                    b.wait("scalar", mm_ev)
                    ev = b.ACT(Gh[:, hcl * 2048 + tt * 512: hcl * 2048 + (tt + 1) * 512],
                               b.bank(bk), AF.Gelu, bias=b.MISC[:, 4 + hc:5 + hc],
                               inc="sc")
                    b.read_bank(bk, ev)
            slot_free[si] = b.ckpt("tensor", "st")
            gh_ev = ("sc", b.ctr["sc"])
            si = nload % 2
            nload += 1
            b.wait("gpsimd", slot_free[si])
            ldev = b.dma("gpsimd", slots[si].rearrange("p (j d) -> p j d", j=8),
                         w2r[:, g * 8:(g + 1) * 8, :])
            b.wait("tensor", ldev)
            b.wait("tensor", gh_ev)
            for dchunk in range(8):
                for tt in range(4):
                    bk = 2 + (dchunk * 4 + tt) % 4
                    b.use_bank(bk)
                    for j in range(8):
                        b.MM(b.bank(bk),
                             slots[si][:, j * 1024 + dchunk * 128: j * 1024 + (dchunk + 1) * 128],
                             Gh[:, j * 2048 + tt * 512: j * 2048 + (tt + 1) * 512],
                             j == 0, j == 7, inc=("st" if j == 7 else None))
                    mm_ev = b.tlast
                    b.wait("vector", mm_ev)
                    if g == 0:
                        b.TS(tmpl, b.bank(bk), b.colv("b2h", dchunk), None, OP.add)
                        b.TT(tmpl, tmpl, maskb[:, tt * 512:(tt + 1) * 512], OP.mult)
                    else:
                        b.TT(tmpl, b.bank(bk), maskb[:, tt * 512:(tt + 1) * 512],
                             OP.mult)
                    ev = b.TT(b.xc(dchunk, tt * 512, (tt + 1) * 512), tmpl,
                              b.xc(dchunk, tt * 512, (tt + 1) * 512), OP.add, inc="sv")
                    b.read_bank(bk, ev)
            slot_free[si] = b.ckpt("tensor", "st")
        b.barrier()

    def ph7_store(self):
        b = self
        b.wait("sync", ("sv", b.ctr["sv"]))
        b.wait("sync", b.tlast)
        ev = b.dma("sync", b.y_out.rearrange("(c p) t -> p c t", p=128),
                   b.xs_a.rearrange("p (c t) -> p c t", c=NCH), sem="so")
        b.wait("gpsimd", ev)

    def build(self):
        self.ph0_loads()
        self.ph1_lambda_ln1()
        self.ph1b_v()
        self.ph2_attention()
        self.ph3_attnout()
        self.ph4_moe()
        self.ph5_rwkv()
        self.ph6_lffn()
        self.ph7_store()
        nc = self.nc
        with nc.Block() as block:
            for eng in ENGINES:
                dec = getattr(block, eng)
                ops = self.stream[eng]
                def make(ops):
                    def _(e):
                        for op in ops:
                            op(e)
                    return _
                dec(make(ops))
        self.es.close()
        return nc


_CACHE = {}


def build_program():
    if "nc" not in _CACHE:
        _CACHE["nc"] = MK().build()
    return _CACHE["nc"]


def kernel(**inputs):
    g = lambda n: np.asarray(inputs[n], dtype=np.float32)
    cT = np.ascontiguousarray

    wqkvT = cT(np.concatenate([g("L0_attn_Wq").T, g("L0_attn_Wk").T,
                               g("L0_attn_Wv").T], axis=1))
    woT = cT(g("L0_attn_Wo").T)
    lam = np.zeros((1, 128), np.float32)
    lam[0, 0:32] = g("L0_attn_lambda_q1")
    lam[0, 32:64] = g("L0_attn_lambda_k1")
    lam[0, 64:96] = g("L0_attn_lambda_q2")
    lam[0, 96:128] = g("L0_attn_lambda_k2")
    sublnw4 = cT(np.tile(np.broadcast_to(g("L0_attn_subln_w"), (128, 64)), (1, 4)))
    w1T = cT(g("L0_ffn_w1").transpose(0, 2, 1))
    w2T = cT(g("L0_ffn_w2").transpose(0, 2, 1))
    wmm = np.zeros((D, 16), np.float32)
    wmm[:, 0:4] = g("L0_ffn_w_macro").T
    wmm[:, 4:12] = g("L0_ffn_w_micro").reshape(8, D).T
    wmmT = cT(wmm)
    wshiftT = cT(g("L1_attn_W_shift").T)
    wkvT = cT(np.concatenate([g("L1_attn_W_key").T, g("L1_attn_W_value").T], axis=1))
    woutT = cT(g("L1_attn_W_output").T)
    tdec = cT(g("L1_attn_time_decay").reshape(64, 1))
    tfirst = cT(g("L1_attn_time_first").reshape(64, 1))
    w1l = cT(g("L1_ffn_w1_light"))
    w2l = cT(g("L1_ffn_w2_light"))
    w1h = cT(g("L1_ffn_w1_heavy"))
    w2h = cT(g("L1_ffn_w2_heavy"))
    b1l = cT(np.broadcast_to(g("L1_ffn_b1_light").reshape(4, 128).T, (128, 4)))
    b1h = cT(np.broadcast_to(g("L1_ffn_b1_heavy").reshape(32, 128).T, (128, 32)))
    colpack = np.zeros((128, 9 * NCH), np.float32)
    for i, v in enumerate([g("L0_ffn_norm_w"), g("L1_attn_shift_gate"),
                           g("L1_attn_ln_w"), g("L1_attn_ln_b"),
                           g("L1_ffn_routing_token"), g("L1_ffn_gamma_light"),
                           g("L1_ffn_gamma_heavy"), g("L1_ffn_b2_light"),
                           g("L1_ffn_b2_heavy")]):
        colpack[:, i * NCH:(i + 1) * NCH] = v.reshape(NCH, 128).T
    colpack = cT(colpack)
    eye = np.eye(128, dtype=np.float32)
    ltri = np.zeros((128, 16), np.float32)
    for m in range(8):
        ltri[:m, m] = 1.0
    ltri[0:8, 8] = np.arange(8, dtype=np.float32) * 656.0
    ltri = cT(ltri)
    constv = cT(np.broadcast_to(np.array([0.0, 1.0, -0.5, 1e-5, 1e-6,
                                          SCALING, 1.0 / 64.0,
                                          math.log(1.0 - LAMBDA_INIT)],
                                         np.float32), (128, 8)))
    onesin = np.ones((128, 128), np.float32)

    x = g("L0_attn_x")
    nc = build_program()
    in_maps = []
    for bb in range(8):
        in_maps.append(dict(
            x=cT(x[bb].T), wqkvT=wqkvT, woT=woT, lam=lam, sublnw4=sublnw4,
            constv=constv,
            w1T=w1T, w2T=w2T, wmmT=wmmT, wshiftT=wshiftT, wkvT=wkvT,
            woutT=woutT, tdec=tdec, tfirst=tfirst, w1l=w1l, b1l=b1l, w2l=w2l,
            w1h=w1h, b1h=b1h, w2h=w2h, colpack=colpack, eye=eye, onesin=onesin,
            ltri=ltri,
        ))
    res = run_bass_kernel_spmd(nc, in_maps, core_ids=list(range(8)))
    out = np.empty((8, T, D), np.float32)
    for bb in range(8):
        out[bb] = np.asarray(res.results[bb]["y"], np.float32).T
    return out
